# version 03
# baseline (speedup 1.0000x reference)
import numpy as np
import ml_dtypes
import concourse.bass as bass
import concourse.mybir as mybir
from concourse.bass_utils import run_bass_kernel_spmd

F32 = mybir.dt.float32
BF16 = mybir.dt.bfloat16
AF = mybir.ActivationFunctionType
ALU = mybir.AluOpType
AX = mybir.AxisListType
BF = ml_dtypes.bfloat16

S = 2048
D = 4096
G512 = 512
NG = S // G512
NTT = G512 // 128
NIC = D // 128
EPS = 1e-5
NEGBIG = -1e30


class Sem:
    def __init__(self, h):
        self.h = h
        self.n = 0
    def inc(self, instr, k):
        instr.then_inc(self.h, k)
        self.n += k
        return self.n

class Ctx:
    def __init__(self, nc, block, sems):
        self.nc = nc
        self.block = block
        self.sems = sems
        self.last_use = {}

    def on(self, eng, fn):
        getattr(self.block, eng)(fn)

    def use(self, res, sem, count):
        self.last_use[res] = [(sem, count)]

    def use_add(self, res, sem, count):
        self.last_use.setdefault(res, []).append((sem, count))

    def war(self, eng_handle, res):
        for sem, count in self.last_use.get(res, ()):
            eng_handle.wait_ge(sem.h, count)




import contextlib

def alloc(nc, es, sbufs, psums, semnames, sem_es=None, prefix=""):
    tens = {}
    for name, shape, dt in sbufs:
        tens[name] = es.enter_context(nc.sbuf_tensor(prefix + name, shape, dt))
    for name, shape, dt in psums:
        tens[name] = es.enter_context(nc.psum_tensor(prefix + name, shape, dt))
    ses = sem_es if sem_es is not None else es
    sems = {name: Sem(ses.enter_context(nc.semaphore(prefix + name)))
            for name in semnames}
    block = es.enter_context(nc.Block())
    return tens, sems, block


def fz_begin(fz):
    if fz is None:
        return bass.Bass("TRN2", target_bir_lowering=False), "", {}
    return fz["nc"], fz["prefix"], fz["io"]


def fz_barrier(cx, fz):
    if fz and fz.get("barrier"):
        bsem, bcnt = fz["barrier"]
        for eng in ("tensor", "vector", "scalar", "sync", "gpsimd"):
            def _bar(e, bsem=bsem, bcnt=bcnt):
                e.wait_ge(bsem, bcnt)
            cx.on(eng, _bar)



def host_consts():
    c = {}
    c["ident_bf"] = np.eye(128, dtype=BF)
    c["ones_col_f"] = np.ones((128, 1), dtype=np.float32)
    c["ones_row_f"] = np.ones((1, 128), dtype=np.float32)
    c["ones_col_bf"] = np.ones((128, 1), dtype=BF)
    c["ones_row_bf"] = np.ones((1, 128), dtype=BF)

    ROT = 64
    t = np.arange(S, dtype=np.float64)
    inv1 = 1.0 / (10000.0 ** (np.arange(0, ROT, 2, dtype=np.float64) / ROT))
    ang1 = t[:, None] * inv1[None, :]
    C1 = np.ones((128, S), dtype=np.float32)
    S1 = np.zeros((128, S), dtype=np.float32)
    cos1 = np.cos(ang1).astype(np.float32)
    sin1 = np.sin(ang1).astype(np.float32)
    for d in range(ROT):
        i = d // 2
        C1[d, :] = cos1[:, i]
        S1[d, :] = (-sin1[:, i] if d % 2 == 0 else sin1[:, i])
    c["rope1_C"] = C1
    c["rope1_S"] = S1
    P1 = np.zeros((128, 64), dtype=BF)
    for d in range(ROT):
        P1[d + 1 if d % 2 == 0 else d - 1, d] = 1
    c["rope1_P"] = P1

    inv0 = 1.0 / (10000.0 ** (np.arange(0, 128, 2, dtype=np.float64) / 128))
    ang0 = t[:, None] * inv0[None, :]
    C0 = np.zeros((128, S), dtype=np.float32)
    S0 = np.zeros((128, S), dtype=np.float32)
    cos0 = np.cos(ang0).astype(np.float32)
    sin0 = np.sin(ang0).astype(np.float32)
    for d in range(128):
        i = d // 2
        C0[d, :] = cos0[:, i]
        S0[d, :] = (-sin0[:, i] if d % 2 == 0 else sin0[:, i])
    c["rope0_C"] = C0
    c["rope0_S"] = S0
    P0 = np.zeros((128, 128), dtype=BF)
    for d in range(128):
        P0[d ^ 1, d] = 1
    c["rope0_P"] = P0

    dm = np.zeros((4, 128, 512), dtype=BF)
    for j, off in enumerate((0, 128, 256, 384)):
        k = np.arange(128)[:, None] + off
        q = np.arange(512)[None, :]
        dm[j] = (k <= q).astype(BF)
    c["l1_diag01"] = dm

    qa = np.arange(S)[:, None]
    cc = np.arange(128)[None, :]
    c["sel_vis_add"] = np.where(16 * (cc + 1) <= qa, 0.0, NEGBIG).astype(np.float32)
    c["own_oh"] = (cc == (qa // 16)).astype(np.float32)

    cv = np.zeros((NG, 128, G512), dtype=BF)
    for g in range(NG):
        qg = g * G512 + np.arange(G512)[None, :]
        cv[g] = (16 * (np.arange(128)[:, None] + 1) <= qg).astype(BF)
    c["cmp_vis01"] = cv

    fd = np.ones((4, 128, 512), dtype=BF)
    for j, off in enumerate((0, 128, 256, 384)):
        ka = np.arange(128)[:, None] + off
        qg = np.arange(512)[None, :]
        same = (ka // 16) == (qg // 16)
        future = same & (ka > qg)
        fd[j] = np.where(future, 0.0, 1.0).astype(BF)
    c["fine_diag01"] = fd

    sb = np.zeros((5, 128, 512), dtype=BF)
    for j, off in enumerate((-1, 0, 1, 2, 3)):
        ka = np.arange(128)[:, None] + 128 * off
        qg = np.arange(512)[None, :]
        sb[j] = ((qg - ka >= 0) & (qg - ka <= 64)).astype(BF)
    c["band01"] = sb

    BIGC = 1e30
    E = np.zeros((16, 128, 128), dtype=BF)
    for kb in range(16):
        for k in range(128):
            E[kb, 8 * kb + k // 16, k] = BIGC
    c["selE"] = E
    return c



def emit_ln_group(cx, g, xin_ap, sb, xg_tiled=True):
    sems = cx.sems
    ld, lnv, lns, lnt, lne = (sems[k] for k in ("ld", "lnv", "lns", "lnt", "lne"))
    chv, chs = sems["chv"], sems["chs"]
    x_g = sb["x_g"]; xlnT = sb["xlnT"]; bs = sb["bs"]
    st = sb["stats"]; xln_tok = sb["xln_tok"]
    ptrA = sb["ptrA"]; ptrB = sb["ptrB"]; ident = sb["ident"]

    for tt in range(NTT):
        t0 = g * G512 + tt * 128
        xo = tt * D if xg_tiled else 0

        def ldf(y, tt=tt, t0=t0, g=g, xo=xo):
            if (tt == 0) or (not xg_tiled):
                cx.war(y, "x_g")
            i = y.dma_start(out=x_g[:, xo:xo + D], in_=xin_ap[t0:t0 + 128, :])
            ld.inc(i, 16)
        cx.on("sync", ldf)
        ld_done = ld.n

        def vstats(v, tt=tt, ld_done=ld_done, xo=xo):
            v.wait_ge(ld.h, ld_done)
            cx.war(v, "stats")
            for i8 in range(D // 512):
                i = v.bn_stats(bs[:, i8 * 6:(i8 + 1) * 6],
                               x_g[:, xo + i8 * 512: xo + (i8 + 1) * 512])
            chv.inc(i, 1); v.wait_ge(chv.h, chv.n)
            i = v.bn_aggr(st[:, 0:2], bs[:, :])
            lnv.inc(i, 1)
        cx.on("vector", vstats)
        v_done = lnv.n

        def sstats2(s, v_done=v_done):
            s.wait_ge(lnv.h, v_done)
            i = s.activation(st[:, 6:7], st[:, 1:2], AF.Ln, bias=sb["eps_col"][:, 0:1])
            chs.inc(i, 1); s.wait_ge(chs.h, chs.n)
            i = s.activation(st[:, 7:8], st[:, 6:7], AF.Exp, scale=-0.5)
            lns.inc(i, 1)
        cx.on("scalar", sstats2)
        s2_done = lns.n
        cx.use("stats", lns, s2_done)

        def vnorm(v, tt=tt, s2_done=s2_done, xo=xo):
            v.wait_ge(lns.h, s2_done)
            cx.war(v, "xln_tok")
            i = v.tensor_tensor(st[:, 8:9], st[:, 0:1], st[:, 7:8], op=ALU.mult)
            chv.inc(i, 1); v.wait_ge(chv.h, chv.n)
            i = v.tensor_scalar(st[:, 9:10], st[:, 8:9], -1.0, None, op0=ALU.mult)
            chv.inc(i, 1); v.wait_ge(chv.h, chv.n)
            i = v.tensor_scalar(xln_tok[:, :], x_g[:, xo:xo + D],
                                st[:, 7:8], st[:, 9:10], op0=ALU.mult, op1=ALU.add)
            lnv.inc(i, 1)
        cx.on("vector", vnorm)
        n_done = lnv.n
        if not xg_tiled:
            cx.use("x_g", lnv, lnv.n)

        same_bank = ptrA is ptrB
        for half in range(4):
            bank, bankname = ((ptrA, "ptrA"), (ptrB, "ptrB"))[0 if same_bank else half % 2]
            def ttr(t, half=half, bank=bank, bankname=bankname, n_done=n_done):
                t.wait_ge(lnv.h, n_done)
                cx.war(t, bankname)
                for j in range(8):
                    ic = half * 8 + j
                    i = t.transpose(bank[:, j * 128:(j + 1) * 128],
                                    xln_tok[:, ic * 128:(ic + 1) * 128], ident[:, :])
                lnt.inc(i, 1)
            cx.on("tensor", ttr)
            t_done = lnt.n
            if half == 3:
                cx.use("xln_tok", lnt, t_done)

            def vev(v, tt=tt, half=half, bank=bank, t_done=t_done):
                v.wait_ge(lnt.h, t_done)
                cx.war(v, "xlnT")
                for j in range(8):
                    ic = half * 8 + j
                    i = v.tensor_copy(
                        xlnT[:, ic * G512 + tt * 128: ic * G512 + (tt + 1) * 128],
                        bank[:, j * 128:(j + 1) * 128])
                lne.inc(i, 1)
            cx.on("vector", vev)
            cx.use(bankname, lne, lne.n)
    return lne.n



def build_l0_ffn(fz=None):
    nc, pre, io = fz_begin(fz)
    xin = io.get("xin") if fz else nc.dram_tensor("xin", [S, D], F32, kind="ExternalInput").ap()
    wT = nc.dram_tensor(pre + "wT", [D, D], F32, kind="ExternalInput").ap()
    scflat = nc.dram_tensor(pre + "scflat", [1, 1024], F32, kind="ExternalInput").ap()
    ident_in = nc.dram_tensor(pre + "ident_bf", [128, 128], BF16, kind="ExternalInput").ap()
    onesr_in = nc.dram_tensor(pre + "ones_row_f", [1, 128], F32, kind="ExternalInput").ap()
    xout = io.get("xout") if fz else nc.dram_tensor("xout", [S, D], F32, kind="ExternalOutput").ap()

    OB = 128
    NOB = D // OB

    es = contextlib.ExitStack()
    with es:
        tens, sems, block = alloc(
            nc, es,
            sbufs=[("x_g", [128, NTT * D], F32), ("xlnT", [128, NIC * G512], BF16),
                   ("bs", [128, 48], F32), ("stats", [128, 16], F32),
                   ("xln_tok", [128, D], BF16), ("ident", [128, 128], BF16),
                   ("onesr", [1, 128], F32), ("scrow", [1, 1024], F32),
                   ("SC", [128, 1024], F32),
                   ("wstage0", [128, NIC * OB], F32), ("wstage1", [128, NIC * OB], F32),
                   ("wbf0", [128, NIC * OB], BF16), ("wbf1", [128, NIC * OB], BF16),
                   ("outsb", [128, 2 * OB], F32), ("eps_col", [128, 1], F32)],
            psums=[("psA", [128, 512], F32), ("psB", [128, 512], F32),
                   ("psC", [128, 512], F32),
                   ("ptrA", [128, 1024], BF16), ("ptrB", [128, 1024], BF16)],
            semnames=["ld", "lnv", "lns", "lnt", "lne", "wld", "wdq", "mm", "ev", "st", "chv", "chs"],
            sem_es=(fz["es"] if fz else None), prefix=pre)
        (x_g, xlnT, stats, xln_tok, ident, onesr, scrow, SC, wstage0,
         wstage1, wbf0, wbf1, outsb, psA, psB, psC, ptrA, ptrB) = (
            tens[k] for k in ("x_g", "xlnT", "stats", "xln_tok", "ident",
                              "onesr", "scrow", "SC", "wstage0", "wstage1",
                              "wbf0", "wbf1", "outsb", "psA", "psB", "psC",
                              "ptrA", "ptrB"))
        cx = Ctx(nc, block, sems)
        fz_barrier(cx, fz)
        ld, lnv, lns, lnt, lne, wld, wdq, mm, ev, st = (
            sems[k] for k in ("ld", "lnv", "lns", "lnt", "lne", "wld", "wdq",
                              "mm", "ev", "st"))

        sb = {"x_g": x_g, "xlnT": xlnT, "bs": tens["bs"], "stats": stats,
              "xln_tok": xln_tok, "ident": ident, "ptrA": ptrA, "ptrB": ptrB,
              "eps_col": tens["eps_col"]}
        def init_eps(v):
            v.memset(tens["eps_col"][:, :], EPS)
        cx.on("vector", init_eps)

        def prolog_dma(y):
            ld.inc(y.dma_start(out=ident[:, :], in_=ident_in[:, :]), 16)
            ld.inc(y.dma_start(out=onesr[:, :], in_=onesr_in[:, :]), 16)
            ld.inc(y.dma_start(out=scrow[:, :], in_=scflat[:, :]), 16)
        cx.on("sync", prolog_dma)
        pro_ld = ld.n

        def prolog_mm(t, pro_ld=pro_ld):
            t.wait_ge(ld.h, pro_ld)
            t.matmul(psC[:, 0:512], onesr[:, :], scrow[:, 0:512], start=True, stop=True)
            i = t.matmul(psB[:, 0:512], onesr[:, :], scrow[:, 512:1024], start=True, stop=True)
            mm.inc(i, 1)
        cx.on("tensor", prolog_mm)
        pro_mm = mm.n

        def prolog_ev(v, pro_mm=pro_mm):
            v.wait_ge(mm.h, pro_mm)
            v.tensor_copy(SC[:, 0:512], psC[:, 0:512])
            i = v.tensor_copy(SC[:, 512:1024], psB[:, 0:512])
            ev.inc(i, 1)
        cx.on("vector", prolog_ev)
        pro_ev = ev.n
        cx.use("psB", ev, pro_ev)

        for g in range(NG):
            emit_ln_group(cx, g, xin, sb)
            xln_ready = lne.n

            for ob in range(NOB):
                par = ob % 2
                wstage, wsname = ((wstage0, "ws0"), (wstage1, "ws1"))[par]
                wbf, wbname = ((wbf0, "wb0"), (wbf1, "wb1"))[par]

                def wload(y, ob=ob, wstage=wstage, wsname=wsname):
                    cx.war(y, wsname)
                    src = wT.rearrange("(ic p) o -> p ic o", p=128)
                    for quarter in range(4):
                        dst = wstage[:, quarter * 8 * OB:(quarter + 1) * 8 * OB]
                        dst = dst.rearrange("p (ic o) -> p ic o", o=OB)
                        i = y.dma_start(
                            out=dst,
                            in_=src[:, quarter * 8:(quarter + 1) * 8,
                                    ob * OB:(ob + 1) * OB])
                        wld.inc(i, 16)
                cx.on("sync", wload)
                wld_done = wld.n

                def wdeq(v, ob=ob, wstage=wstage, wbf=wbf, wbname=wbname,
                         wld_done=wld_done, pro_ev=pro_ev):
                    v.wait_ge(wld.h, wld_done)
                    v.wait_ge(ev.h, pro_ev)
                    cx.war(v, wbname)
                    oc = ob
                    for ic in range(NIC):
                        i = v.tensor_scalar(
                            wbf[:, ic * OB:(ic + 1) * OB],
                            wstage[:, ic * OB:(ic + 1) * OB],
                            SC[:, oc * 32 + ic: oc * 32 + ic + 1], None,
                            op0=ALU.mult)
                    wdq.inc(i, 1)
                cx.on("vector", wdeq)
                cx.use(wsname, wdq, wdq.n)
                wdq_done = wdq.n

                for tt in range(NTT):
                    ps, psname = ((psA, "psA"), (psB, "psB"))[tt % 2]
                    def mmf(t, ob=ob, tt=tt, ps=ps, psname=psname,
                            wbf=wbf, wdq_done=wdq_done, xln_ready=xln_ready):
                        t.wait_ge(wdq.h, wdq_done)
                        t.wait_ge(lne.h, xln_ready)
                        cx.war(t, psname)
                        for ic in range(NIC):
                            i = t.matmul(
                                ps[:, 0:OB],
                                xlnT[:, ic * G512 + tt * 128: ic * G512 + (tt + 1) * 128],
                                wbf[:, ic * OB:(ic + 1) * OB],
                                start=(ic == 0), stop=(ic == NIC - 1))
                        mm.inc(i, 1)
                    cx.on("tensor", mmf)
                    mm_done = mm.n
                    if tt == NTT - 1:
                        cx.use(wbname, mm, mm.n)

                    slot = (ob * NTT + tt) % 2
                    def evf(v, ob=ob, tt=tt, ps=ps, mm_done=mm_done, slot=slot):
                        v.wait_ge(mm.h, mm_done)
                        cx.war(v, f"outsb{slot}")
                        i = v.tensor_tensor(
                            outsb[:, slot * OB:(slot + 1) * OB], ps[:, 0:OB],
                            x_g[:, tt * D + ob * OB: tt * D + (ob + 1) * OB],
                            op=ALU.add)
                        ev.inc(i, 1)
                    cx.on("vector", evf)
                    cx.use(psname, ev, ev.n)
                    ev_done = ev.n

                    def stf(y, ob=ob, tt=tt, ev_done=ev_done, g=g, slot=slot):
                        y.wait_ge(ev.h, ev_done)
                        t0 = g * G512 + tt * 128
                        i = y.dma_start(out=xout[t0:t0 + 128, ob * OB:(ob + 1) * OB],
                                        in_=outsb[:, slot * OB:(slot + 1) * OB])
                        st.inc(i, 16)
                    cx.on("gpsimd", stf)
                    cx.use(f"outsb{slot}", st, st.n)

            cx.use("x_g", ev, ev.n)
            cx.use("xlnT", mm, mm.n)
            cx.use("wb0", mm, mm.n)
            cx.use("wb1", mm, mm.n)

        final_st = st.n
        def fin(y, final_st=final_st):
            y.wait_ge(st.h, final_st)
        cx.on("gpsimd", fin)
        if fz is not None:
            fz["barrier"] = (st.h, st.n)

    return nc



DBG = {"ng": NG, "nkvh": 8, "ngh": 4, "fine": True, "slide": True,
       "cmp": True, "sel": True, "gates": True, "mlp": True,
       "kvproj": 2, "gconst": True, "krope": True, "vtr": True, "cin": True,
       "dump": False}

def build_l0_attn(fz=None):
    nc, pre, io = fz_begin(fz)
    xin = io.get("xin") if fz else nc.dram_tensor("xin", [S, D], F32, kind="ExternalInput").ap()
    qkvwT = nc.dram_tensor(pre + "qkvwT", [D, 6144], F32, kind="ExternalInput").ap()
    wgT_d = nc.dram_tensor(pre + "wgT", [D, 96], F32, kind="ExternalInput").ap()
    bg_d = nc.dram_tensor(pre + "bgrow", [1, 96], F32, kind="ExternalInput").ap()
    kw1T_d = nc.dram_tensor(pre + "kw1T", [2048, 2048], F32, kind="ExternalInput").ap()
    vw1T_d = nc.dram_tensor(pre + "vw1T", [2048, 2048], F32, kind="ExternalInput").ap()
    kw2T_d = nc.dram_tensor(pre + "kw2T", [2048, 128], F32, kind="ExternalInput").ap()
    vw2T_d = nc.dram_tensor(pre + "vw2T", [2048, 128], F32, kind="ExternalInput").ap()
    kb1r_d = nc.dram_tensor(pre + "kb1r", [128, 16], F32, kind="ExternalInput").ap()
    vb1r_d = nc.dram_tensor(pre + "vb1r", [128, 16], F32, kind="ExternalInput").ap()
    kb2c_d = nc.dram_tensor(pre + "kb2c", [128, 1], F32, kind="ExternalInput").ap()
    vb2c_d = nc.dram_tensor(pre + "vb2c", [128, 1], F32, kind="ExternalInput").ap()
    kpT_d = nc.dram_tensor(pre + "kpT", [8, 128, 512], F32, kind="ExternalInput").ap()
    vpT_d = nc.dram_tensor(pre + "vpT", [8, 128, 512], F32, kind="ExternalInput").ap()
    memkT_d = nc.dram_tensor(pre + "memkT", [128, 8], BF16, kind="ExternalInput").ap()
    memv_d = nc.dram_tensor(pre + "memv", [1, 8 * 128], BF16, kind="ExternalInput").ap()
    ident_in = nc.dram_tensor(pre + "ident_bf", [128, 128], BF16, kind="ExternalInput").ap()
    onescb_in = nc.dram_tensor(pre + "ones_col_bf", [128, 1], BF16, kind="ExternalInput").ap()
    onesrb_in = nc.dram_tensor(pre + "ones_row_bf", [1, 128], BF16, kind="ExternalInput").ap()
    ropeC_in = nc.dram_tensor(pre + "ropeC0", [128, S], F32, kind="ExternalInput").ap()
    ropeS_in = nc.dram_tensor(pre + "ropeS0", [128, S], F32, kind="ExternalInput").ap()
    ropeP_in = nc.dram_tensor(pre + "ropeP0", [128, 128], BF16, kind="ExternalInput").ap()
    selvis_d = nc.dram_tensor(pre + "selvis", [S, 128], F32, kind="ExternalInput").ap()
    ownm1_d = nc.dram_tensor(pre + "ownm1", [S, 128], F32, kind="ExternalInput").ap()
    cmpvis_d = nc.dram_tensor(pre + "cmpvis", [NG, 128, 512], BF16, kind="ExternalInput").ap()
    fdiag_d = nc.dram_tensor(pre + "fdiag", [4, 128, 512], BF16, kind="ExternalInput").ap()
    band_d = nc.dram_tensor(pre + "band", [5, 128, 512], BF16, kind="ExternalInput").ap()
    selE_d = nc.dram_tensor(pre + "selE", [16, 128, 128], BF16, kind="ExternalInput").ap()
    xattT_out = io.get("xattT") if fz else nc.dram_tensor("xattT", [D, S], BF16, kind="ExternalOutput").ap()
    if DBG["dump"]:
        dmp = {nm: nc.dram_tensor(pre + "dmp_" + nm, shp, dt, kind="ExternalOutput").ap()
               for nm, shp, dt in (
                   ("rkT", [128, 8 * S], BF16), ("vtok", [128, 8 * S], BF16),
                   ("ckT", [128, 8 * 128], BF16), ("cvT", [128, 8 * 128], BF16),
                   ("selT", [128, 512], BF16), ("gsig", [128, 4 * 96], F32),
                   ("q4", [128, 4 * 512], BF16), ("Zsb", [128, 12], F32),
                   ("gz", [128, 12], F32), ("xlnT", [128, NIC * G512], BF16))}

    KVH = 8
    GH = 4
    SC0 = 1.0 / np.sqrt(128.0)
    SCSEL = SC0 / 4.0

    es = contextlib.ExitStack()
    with es:
        tens, sems, block = alloc(
            nc, es,
            sbufs=[
                ("x_g", [128, D], F32), ("xlnT", [128, NIC * G512], BF16),
                ("bs", [128, 48], F32), ("stats", [128, 16], F32),
                ("xln_tok", [128, D], BF16), ("ident", [128, 128], BF16),
                ("eps_col", [128, 1], F32), ("n1000", [128, 1], F32),
                ("onescb", [128, 1], BF16), ("onesrb", [1, 128], BF16),
                ("ropeCS", [128, 2 * G512], F32),
                ("ropeP", [128, 128], BF16),
                ("rkT", [128, KVH * S], BF16),
                ("vtok", [128, KVH * S], BF16),
                ("q4", [128, GH * G512], BF16),
                ("rqh", [128, G512], BF16),
                ("ktmp", [128, G512], BF16), ("vtmp", [128, G512], BF16),
                ("kvpT", [128, 2 * G512], BF16),
                ("ckinr", [128, 16 * 256], BF16),
                ("cvinr", [128, 16 * 256], BF16),
                ("h1a", [128, 16 * 256], BF16),
                ("ckT", [128, KVH * 128], BF16), ("cvT", [128, KVH * 128], BF16),
                ("cvbm", [128, KVH * 128], BF16),
                ("selT", [128, G512], BF16),
                ("gsig", [128, NTT * 96], F32),
                ("selvis_g", [128, NTT * 128], BF16),
                ("ownm1_g", [128, NTT * 128], BF16),
                ("cmpvis_g", [128, G512], BF16),
                ("fdiag_s", [128, 4 * 512], BF16),
                ("band_s", [128, 5 * 512], BF16),
                ("selE_s", [128, 16 * 128], BF16),
                ("memkT_s", [128, 8], BF16), ("memv_s", [1, 8 * 128], BF16),
                ("kb1r_s", [128, 16], F32), ("vb1r_s", [128, 16], F32),
                ("kb2c_s", [128, 1], F32), ("vb2c_s", [128, 1], F32),
                ("qw0", [128, NIC * 128], BF16),
                ("mw0", [128, 16 * 128], BF16),
                ("bg_bf", [1, 96], BF16),
                ("exp0", [128, 512], BF16), ("exp1", [128, 512], BF16),
                ("exp2", [128, 512], BF16),
                ("rt1", [128, 512], F32), ("rt2", [128, 512], F32),
                ("psel", [128, 128], F32), ("pscr", [128, 128], F32),
                ("m8", [128, 8], F32), ("selm", [128, 128], F32),
                ("Zsb", [128, 12], F32), ("Zinv", [128, 12], F32),
                ("gz", [128, 12], F32),
                ("xatt_tok", [128, 128], BF16),
                ("xattT_kvh", [128, G512], BF16),
            ],
            psums=[("psA", [128, 512], F32), ("psB", [128, 512], F32),
                   ("psC", [128, 512], F32), ("psF", [128, 512], F32),
                   ("psS", [128, 512], F32), ("psU", [128, 512], F32),
                   ("psZ", [128, 512], F32),
                   ("ptrA", [128, 1024], BF16)],
            semnames=["ld", "pld", "lnv", "lns", "lnt", "lne", "wld", "mm",
                      "ev", "st", "chv", "chs", "rp"],
            sem_es=(fz["es"] if fz else None), prefix=pre)
        cx = Ctx(nc, block, sems)
        fz_barrier(cx, fz)
        ld, pld, lnv, lns, lnt, lne, wld, mm, ev, st, chv, chs, rp = (
            sems[k] for k in ("ld", "pld", "lnv", "lns", "lnt", "lne", "wld", "mm",
                              "ev", "st", "chv", "chs", "rp"))
        T = tens
        x_g = T["x_g"]; xlnT = T["xlnT"]; ident = T["ident"]
        rkT = T["rkT"]; vtok = T["vtok"]; q4 = T["q4"]; rqh = T["rqh"]
        psA, psB, psC, psF, psS, psU, psZ = (
            T[k] for k in ("psA", "psB", "psC", "psF", "psS", "psU", "psZ"))
        ptrA = T["ptrA"]
        expsl = [T["exp0"], T["exp1"], T["exp2"]]

        sb = {"x_g": x_g, "xlnT": xlnT, "bs": T["bs"], "stats": T["stats"],
              "xln_tok": T["xln_tok"], "ident": ident, "ptrA": ptrA,
              "ptrB": ptrA, "eps_col": T["eps_col"]}

        def init_consts(v):
            v.memset(T["eps_col"][:, :], EPS)
            v.memset(T["n1000"][:, :], -1000.0)
            v.memset(T["ckT"][:, :], 0.0)
            v.memset(T["cvT"][:, :], 0.0)
            v.memset(T["cvbm"][:, :], 0.0)
        cx.on("vector", init_consts)

        def prolog_dma(y):
            for dst, src in ((ident, ident_in), (T["onescb"], onescb_in),
                             (T["onesrb"], onesrb_in),
                             (T["ropeP"], ropeP_in), (T["memkT_s"], memkT_d),
                             (T["memv_s"], memv_d), (T["kb1r_s"], kb1r_d),
                             (T["vb1r_s"], vb1r_d), (T["kb2c_s"], kb2c_d),
                             (T["vb2c_s"], vb2c_d)):
                pld.inc(y.dma_start(out=dst[:, :], in_=src[:, :]), 16)
            for j in range(4):
                pld.inc(y.dma_start(out=T["fdiag_s"][:, j * 512:(j + 1) * 512],
                                   in_=fdiag_d[j, :, :]), 16)
            for j in range(5):
                pld.inc(y.dma_start(out=T["band_s"][:, j * 512:(j + 1) * 512],
                                   in_=band_d[j, :, :]), 16)
            for j in range(16):
                pld.inc(y.dma_start(out=T["selE_s"][:, j * 128:(j + 1) * 128],
                                   in_=selE_d[j, :, :]), 16)
        cx.on("sync", prolog_dma)
        def prolog_dmag(y):
            wld.inc(y.dma_start(out=T["bg_bf"][:, :], in_=bg_d[:, :]), 16)
        cx.on("gpsimd", prolog_dmag)
        pro_ld = pld.n

        for eng in ("tensor", "vector", "scalar"):
            def gate_(e, pro_ld=pro_ld):
                e.wait_ge(pld.h, pro_ld)
            cx.on(eng, gate_)

        def stream_block(buf, bufname, src_ap, col0, ncols, nic, blk=16):
            def wl(y, buf=buf, bufname=bufname, src_ap=src_ap, col0=col0,
                   ncols=ncols, nic=nic, blk=blk):
                cx.war(y, bufname)
                src = src_ap.rearrange("(ic p) o -> p ic o", p=128)
                nq = max(1, nic // blk)
                per = nic // nq
                for qq in range(nq):
                    dst = buf[:, qq * per * ncols:(qq + 1) * per * ncols]
                    dst = dst.rearrange("p (ic o) -> p ic o", o=ncols)
                    i = y.dma_start(out=dst,
                                    in_=src[:, qq * per:(qq + 1) * per,
                                            col0:col0 + ncols])
                    wld.inc(i, 16)
            cx.on("gpsimd", wl)
            return wld.n

        for g in range(DBG["ng"]):
            emit_ln_group(cx, g, xin, sb, xg_tiled=False)
            xln_ready = lne.n

            def gconsts(y, g=g):
                cx.war(y, "ropeCS")
                pld.inc(y.dma_start(out=T["ropeCS"][:, 0:512],
                                   in_=ropeC_in[:, g * G512:(g + 1) * G512]), 16)
                pld.inc(y.dma_start(out=T["ropeCS"][:, 512:1024],
                                   in_=ropeS_in[:, g * G512:(g + 1) * G512]), 16)
                cx.war(y, "cmpvis_g")
                pld.inc(y.dma_start(out=T["cmpvis_g"][:, :], in_=cmpvis_d[g, :, :]), 16)

            if DBG["gconst"]:
                cx.on("sync", gconsts)
            def gconsts2(y, g=g):
                cx.war(y, "selvis_g")
                cx.war(y, "ownm1_g")
                for qc in range(NTT):
                    q0 = g * G512 + qc * 128
                    pld.inc(y.dma_start(out=T["selvis_g"][:, qc * 128:(qc + 1) * 128],
                                       in_=selvis_d[q0:q0 + 128, :]), 16)
                    pld.inc(y.dma_start(out=T["ownm1_g"][:, qc * 128:(qc + 1) * 128],
                                       in_=ownm1_d[q0:q0 + 128, :]), 16)
            if DBG["gconst"]:
                cx.on("gpsimd", gconsts2)
            gconst_ld = pld.n
            gconst_wld = wld.n

            wg_done = stream_block(T["qw0"], "qw0", wgT_d, 0, 96, NIC) if DBG["gates"] else 0
            for qc in range(NTT if DBG["gates"] else 0):
                def gmm(t, qc=qc, xln_ready=xln_ready, wg_done=wg_done):
                    t.wait_ge(lne.h, xln_ready)
                    t.wait_ge(wld.h, wg_done)
                    cx.war(t, "psU")
                    for ic in range(NIC):
                        t.matmul(psU[:, 0:96],
                                 xlnT[:, ic * G512 + qc * 128: ic * G512 + (qc + 1) * 128],
                                 T["qw0"][:, ic * 96:(ic + 1) * 96],
                                 start=(ic == 0), stop=False)
                    i = t.matmul(psU[:, 0:96], T["onesrb"][:, :],
                                 T["bg_bf"][:, :], start=False, stop=True)
                    mm.inc(i, 1)
                cx.on("tensor", gmm)
                gmm_done = mm.n
                cx.use_add("qw0", mm, mm.n)

                def gsigf(s_, qc=qc, gmm_done=gmm_done):
                    s_.wait_ge(mm.h, gmm_done)
                    cx.war(s_, "rt12")
                    i = s_.activation(T["rt1"][:, 0:96], psU[:, 0:96], AF.Exp,
                                      scale=-1.0)
                    lns.inc(i, 1)
                cx.on("scalar", gsigf)
                cx.use("psU", lns, lns.n)
                gs_done = lns.n

                def gsig2(v, qc=qc, gs_done=gs_done):
                    v.wait_ge(lns.h, gs_done)
                    if qc == 0:
                        cx.war(v, "gsig")
                    i = v.tensor_scalar(T["rt1"][:, 0:96], T["rt1"][:, 0:96], 1.0,
                                        None, op0=ALU.add)
                    chv.inc(i, 1); v.wait_ge(chv.h, chv.n)
                    i = v.reciprocal(T["gsig"][:, qc * 96:(qc + 1) * 96],
                                     T["rt1"][:, 0:96])
                    rp.inc(i, 1)
                    cx.use("rt12", rp, rp.n)
                cx.on("vector", gsig2)

            for kv in range(DBG["kvproj"]):
                for kvh in range(KVH):
                    ob = 32 + kv * 8 + kvh
                    qw, qwname = T["qw0"], "qw0"
                    wld_done = stream_block(qw, qwname, qkvwT, ob * 128, 128, NIC)

                    ps, psname = ((psA, "psA"), (psB, "psB"))[ob % 2]
                    def pmm(t, qw=qw, ps=ps, psname=psname, wld_done=wld_done,
                            xln_ready=xln_ready):
                        t.wait_ge(wld.h, wld_done)
                        t.wait_ge(lne.h, xln_ready)
                        cx.war(t, psname)
                        for ic in range(NIC):
                            i = t.matmul(ps[:, 0:512],
                                         qw[:, ic * 128:(ic + 1) * 128],
                                         xlnT[:, ic * G512:(ic + 1) * G512],
                                         start=(ic == 0), stop=(ic == NIC - 1))
                        mm.inc(i, 1)
                    cx.on("tensor", pmm)
                    cx.use(qwname, mm, mm.n)
                    mm_done = mm.n

                    tmp, tmpname = (T["ktmp"], "ktmp") if kv == 0 else (T["vtmp"], "vtmp")
                    def pev(v, ps=ps, mm_done=mm_done, tmp=tmp, tmpname=tmpname):
                        v.wait_ge(mm.h, mm_done)
                        cx.war(v, tmpname)
                        i = v.tensor_copy(tmp[:, :], ps[:, 0:512])
                        lns.inc(i, 1)
                    cx.on("vector", pev)
                    ev_done = lns.n

                    def kpl(y, kvh=kvh, kv=kv):
                        cx.war(y, "kvpT")
                        src = (kpT_d if kv == 0 else vpT_d)
                        i = y.dma_start(out=T["kvpT"][:, kv * 512:(kv + 1) * 512],
                                        in_=src[kvh, :, :])
                        wld.inc(i, 16)
                    cx.on("gpsimd", kpl) if DBG["cin"] else None
                    kp_done = wld.n

                    cin, cinname = (T["ckinr"], "ckinr") if kv == 0 else (T["cvinr"], "cvinr")
                    def cinf(v, ps=ps, mm_done=mm_done, kp_done=kp_done,
                             cin=cin, cinname=cinname, kvh=kvh, kv=kv):
                        v.wait_ge(mm.h, mm_done)
                        v.wait_ge(wld.h, kp_done)
                        cx.war(v, "rt12")
                        i = v.tensor_tensor(T["rt1"][:, :], ps[:, 0:512],
                                            T["kvpT"][:, kv * 512:(kv + 1) * 512],
                                            op=ALU.add)
                        chv.inc(i, 1); v.wait_ge(chv.h, chv.n)
                        cx.war(v, cinname)
                        dst = cin[:, :].rearrange("p (tb kv c) -> p tb kv c",
                                                  kv=KVH, c=32)[:, :, kvh, :]
                        src = T["rt1"][:, :].rearrange("p (c tb) -> p tb c", tb=16)
                        i = v.tensor_copy(dst, src)
                        ev.inc(i, 1)
                    cx.on("vector", cinf) if DBG["cin"] else None
                    cx.use("rt12", ev, ev.n)
                    cx.use_add(psname, ev, ev.n)
                    cx.use("kvpT", ev, ev.n)

                    if kv == 0:
                        def kr0(t, ev_done=ev_done, tmp=tmp):
                            t.wait_ge(lns.h, ev_done)
                            cx.war(t, "psU")
                            i = t.matmul(psU[:, 0:512], T["ropeP"][:, :],
                                         tmp[:, :], start=True, stop=True)
                            mm.inc(i, 1)
                        cx.on("tensor", kr0) if DBG["krope"] else None
                        r0_done = mm.n

                        def kr1(v, ev_done=ev_done, tmp=tmp, kvh=kvh, g=g,
                                r0_done=r0_done, gconst_ld=gconst_ld):
                            v.wait_ge(lns.h, ev_done)
                            v.wait_ge(pld.h, gconst_ld)
                            cx.war(v, "rt12")
                            i = v.tensor_tensor(T["rt1"][:, :], tmp[:, :],
                                                T["ropeCS"][:, 0:512], op=ALU.mult)
                            chv.inc(i, 1); v.wait_ge(chv.h, chv.n)
                            v.wait_ge(mm.h, r0_done)
                            i = v.tensor_tensor(T["rt2"][:, :], psU[:, 0:512],
                                                T["ropeCS"][:, 512:1024], op=ALU.mult)
                            chv.inc(i, 1); v.wait_ge(chv.h, chv.n)
                            cx.use_add("psU", chv, chv.n)
                            toff = kvh * S + g * G512
                            i = v.tensor_tensor(rkT[:, toff:toff + G512],
                                                T["rt1"][:, :], T["rt2"][:, :],
                                                op=ALU.add)
                            rp.inc(i, 1)
                            cx.use("rt12", rp, rp.n)
                        cx.on("vector", kr1) if DBG["krope"] else None
                        cx.use("ktmp", rp, rp.n)
                        cx.use_add("ktmp", mm, mm.n)
                    else:
                        for tt in range(NTT):
                            def vt(t, ev_done=ev_done, tt=tt, tmp=tmp):
                                t.wait_ge(lns.h, ev_done)
                                cx.war(t, "ptrA")
                                i = t.transpose(ptrA[:, 0:128],
                                                tmp[:, tt * 128:(tt + 1) * 128],
                                                ident[:, :])
                                lnt.inc(i, 1)
                            cx.on("tensor", vt) if DBG["vtr"] else None
                            t_done = lnt.n
                            def vtev(v, t_done=t_done, tt=tt, kvh=kvh, g=g):
                                v.wait_ge(lnt.h, t_done)
                                kb = g * NTT + tt
                                i = v.tensor_copy(
                                    vtok[:, kvh * S + kb * 128: kvh * S + (kb + 1) * 128],
                                    ptrA[:, 0:128])
                                rp.inc(i, 1)
                            cx.on("vector", vtev) if DBG["vtr"] else None
                            cx.use("ptrA", rp, rp.n)
                        if tt == NTT - 1:
                            cx.use("vtmp", lnt, lnt.n)
            kv_ready = rp.n
            cin_ready = ev.n

            for kv in range(2 if DBG["mlp"] else 0):
                cin = T["ckinr"] if kv == 0 else T["cvinr"]
                w1d = kw1T_d if kv == 0 else vw1T_d
                w2d = kw2T_d if kv == 0 else vw2T_d
                b1 = T["kb1r_s"] if kv == 0 else T["vb1r_s"]
                b2 = T["kb2c_s"] if kv == 0 else T["vb2c_s"]
                cT, cTname = (T["ckT"], "ckT") if kv == 0 else (T["cvT"], "cvT")

                for oc in range(16):
                    mw, mwname = T["mw0"], "mw0"
                    wld_done = stream_block(mw, mwname, w1d, oc * 128, 128, 16)
                    ps, psname = ((psA, "psA"), (psB, "psB"))[oc % 2]

                    def h1mm(t, mw=mw, ps=ps, psname=psname, cin=cin,
                             wld_done=wld_done, cin_ready=cin_ready):
                        t.wait_ge(wld.h, wld_done)
                        t.wait_ge(ev.h, cin_ready)
                        cx.war(t, psname)
                        for tb in range(16):
                            i = t.matmul(ps[:, 0:256],
                                         mw[:, tb * 128:(tb + 1) * 128],
                                         cin[:, tb * 256:(tb + 1) * 256],
                                         start=(tb == 0), stop=(tb == 15))
                        mm.inc(i, 1)
                    cx.on("tensor", h1mm)
                    cx.use(mwname, mm, mm.n)
                    h1_done = mm.n

                    def h1ev(v, oc=oc, ps=ps, h1_done=h1_done, b1=b1):
                        v.wait_ge(mm.h, h1_done)
                        cx.war(v, "h1a") if oc == 0 else None
                        i = v.tensor_scalar(T["h1a"][:, oc * 256:(oc + 1) * 256],
                                            ps[:, 0:256], b1[:, oc:oc + 1], 0.0,
                                            op0=ALU.add, op1=ALU.max)
                        lns.inc(i, 1)
                    cx.on("vector", h1ev)
                    cx.use(psname, lns, lns.n)
                cx.use("ckinr" if kv == 0 else "cvinr", mm, mm.n)
                h1_ready = lns.n

                wld2 = stream_block(T["mw0"], "mw0", w2d, 0, 128, 16)
                def h2mm(t, wld2=wld2, h1_ready=h1_ready):
                    t.wait_ge(wld.h, wld2)
                    t.wait_ge(lns.h, h1_ready)
                    cx.war(t, "psA")
                    for oc in range(16):
                        i = t.matmul(psA[:, 0:256],
                                     T["mw0"][:, oc * 128:(oc + 1) * 128],
                                     T["h1a"][:, oc * 256:(oc + 1) * 256],
                                     start=(oc == 0), stop=(oc == 15))
                    mm.inc(i, 1)
                cx.on("tensor", h2mm)
                cx.use("mw0", mm, mm.n)
                cx.use("h1a", mm, mm.n)
                h2_done = mm.n

                def h2ev(v, h2_done=h2_done, b2=b2, cT=cT, cTname=cTname, g=g):
                    v.wait_ge(mm.h, h2_done)
                    cx.war(v, cTname)
                    dst = cT[:, :].rearrange("p (kv c) -> p kv c", c=128)[
                        :, :, g * 32:(g + 1) * 32]
                    src = psA[:, 0:256].rearrange("p (kv c) -> p kv c", c=32)
                    i = v.tensor_scalar(dst, src, b2[:, 0:1], None, op0=ALU.add)
                    lns.inc(i, 1)
                cx.on("vector", h2ev)
                cx.use("psA", lns, lns.n)
            mlp_ready = lns.n

            for kvh in range(KVH if DBG["mlp"] else 0):
                def cvt(t, kvh=kvh, mlp_ready=mlp_ready):
                    t.wait_ge(lns.h, mlp_ready)
                    cx.war(t, "ptrA")
                    i = t.transpose(ptrA[:, 0:128],
                                    T["cvT"][:, kvh * 128:(kvh + 1) * 128],
                                    ident[:, :])
                    lnt.inc(i, 1)
                cx.on("tensor", cvt)
                t_done = lnt.n
                def cvtev(v, t_done=t_done, kvh=kvh):
                    v.wait_ge(lnt.h, t_done)
                    cx.war(v, "cvbm")
                    i = v.tensor_copy(T["cvbm"][:, kvh * 128:(kvh + 1) * 128],
                                      ptrA[:, 0:128])
                    rp.inc(i, 1)
                cx.on("vector", cvtev)
                cx.use("ptrA", rp, rp.n)
            cvbm_ready = rp.n

            for kvh in range(DBG["nkvh"]):
                for gh in range(GH):
                    h = kvh * GH + gh
                    qw, qwname = T["qw0"], "qw0"
                    wld_done = stream_block(qw, qwname, qkvwT, h * 128, 128, NIC)
                    ps, psname = ((psA, "psA"), (psB, "psB"))[gh % 2]
                    def qmm(t, qw=qw, ps=ps, psname=psname, wld_done=wld_done,
                            xln_ready=xln_ready):
                        t.wait_ge(wld.h, wld_done)
                        t.wait_ge(lne.h, xln_ready)
                        cx.war(t, psname)
                        for ic in range(NIC):
                            i = t.matmul(ps[:, 0:512],
                                         qw[:, ic * 128:(ic + 1) * 128],
                                         xlnT[:, ic * G512:(ic + 1) * G512],
                                         start=(ic == 0), stop=(ic == NIC - 1))
                        mm.inc(i, 1)
                    cx.on("tensor", qmm)
                    cx.use(qwname, mm, mm.n)
                    mm_done = mm.n
                    def qev(v, gh=gh, ps=ps, mm_done=mm_done):
                        v.wait_ge(mm.h, mm_done)
                        if gh == 0:
                            cx.war(v, "q4")
                        i = v.tensor_copy(T["q4"][:, gh * G512:(gh + 1) * G512],
                                          ps[:, 0:512])
                        lns.inc(i, 1)
                    cx.on("vector", qev)
                    cx.use(psname, lns, lns.n)
                q4_ready = lns.n

                for qc in range(NTT if DBG["sel"] else 0):
                    def selmm(t, kvh=kvh, qc=qc, q4_ready=q4_ready,
                              mlp_ready=mlp_ready, gconst_ld=gconst_ld):
                        t.wait_ge(lns.h, max(q4_ready, mlp_ready))
                        t.wait_ge(pld.h, gconst_ld)
                        cx.war(t, "psU")
                        for gh in range(GH):
                            i = t.matmul(
                                psU[:, 0:128],
                                T["q4"][:, gh * G512 + qc * 128: gh * G512 + (qc + 1) * 128],
                                T["ckT"][:, kvh * 128:(kvh + 1) * 128],
                                start=(gh == 0), stop=(gh == GH - 1))
                        mm.inc(i, 1)
                    cx.on("tensor", selmm)
                    selmm_done = mm.n

                    def selsoft(v, s_, qc=qc, selmm_done=selmm_done):
                        pass
                    def selv1(v, qc=qc, selmm_done=selmm_done,
                              gconst_wld=gconst_wld):
                        v.wait_ge(mm.h, selmm_done)
                        v.wait_ge(wld.h, gconst_wld)
                        cx.war(v, "pselw")
                        i = v.tensor_tensor(psU[:, 0:128], psU[:, 0:128],
                                            T["selvis_g"][:, qc * 128:(qc + 1) * 128],
                                            op=ALU.add)
                        chv.inc(i, 1); v.wait_ge(chv.h, chv.n)
                        i = v.tensor_reduce(T["stats"][:, 10:11], psU[:, 0:128],
                                            axis=AX.X, op=ALU.max)
                        chv.inc(i, 1); v.wait_ge(chv.h, chv.n)
                        i = v.tensor_scalar(T["stats"][:, 11:12], T["stats"][:, 10:11],
                                            SCSEL, -1000.0, op0=ALU.mult, op1=ALU.max)
                        chv.inc(i, 1); v.wait_ge(chv.h, chv.n)
                        i = v.tensor_scalar(T["stats"][:, 12:13], T["stats"][:, 11:12],
                                            -1.0, None, op0=ALU.mult)
                        ev.inc(i, 1)
                    cx.on("vector", selv1)
                    neg_done = ev.n

                    def sele(s_, qc=qc, neg_done=neg_done):
                        s_.wait_ge(ev.h, neg_done)
                        cx.war(s_, "pselw2")
                        i = s_.activation(T["psel"][:, :], psU[:, 0:128], AF.Exp,
                                          scale=SCSEL, bias=T["stats"][:, 12:13],
                                          accum_out=T["stats"][:, 13:14])
                        chs.inc(i, 1); s_.wait_ge(chs.h, chs.n)
                        i = s_.activation(T["stats"][:, 14:15], T["stats"][:, 12:13],
                                          AF.Exp, bias=T["n1000"][:, 0:1])
                        lns.inc(i, 1)
                    cx.on("scalar", sele)
                    cx.use("psU", lns, lns.n)
                    sele_done = lns.n

                    def selv2(v, qc=qc, sele_done=sele_done):
                        v.wait_ge(lns.h, sele_done)
                        i = v.tensor_tensor(T["stats"][:, 15:16], T["stats"][:, 13:14],
                                            T["stats"][:, 14:15], op=ALU.add)
                        chv.inc(i, 1); v.wait_ge(chv.h, chv.n)
                        i = v.tensor_scalar(T["stats"][:, 15:16], T["stats"][:, 15:16],
                                            1e-10, None, op0=ALU.mult)
                        chv.inc(i, 1); v.wait_ge(chv.h, chv.n)
                        i = v.max(out=T["m8"][:, :], in_=T["psel"][:, :])
                        chv.inc(i, 1); v.wait_ge(chv.h, chv.n)
                        i = v.memset(T["m8"][:, 4:8], -1.0)
                        chv.inc(i, 1); v.wait_ge(chv.h, chv.n)
                        i = v.match_replace(out=T["pscr"][:, :], in_to_replace=T["m8"][:, :],
                                            in_values=T["psel"][:, :], imm_value=-1.0)
                        chv.inc(i, 1); v.wait_ge(chv.h, chv.n)
                        i = v.tensor_tensor(T["selm"][:, :], T["psel"][:, :],
                                            T["pscr"][:, :], op=ALU.not_equal)
                        chv.inc(i, 1); v.wait_ge(chv.h, chv.n)
                        i = v.tensor_scalar(T["pscr"][:, :], T["psel"][:, :],
                                            T["stats"][:, 15:16], None, op0=ALU.is_gt)
                        chv.inc(i, 1); v.wait_ge(chv.h, chv.n)
                        i = v.tensor_tensor(T["selm"][:, :], T["selm"][:, :],
                                            T["pscr"][:, :], op=ALU.mult)
                        chv.inc(i, 1); v.wait_ge(chv.h, chv.n)
                        i = v.tensor_tensor(T["pscr"][:, :], T["selm"][:, :],
                                            T["ownm1_g"][:, qc * 128:(qc + 1) * 128],
                                            op=ALU.add)
                        ev.inc(i, 1)
                        chv.inc(i, 0) if False else None
                    cx.on("vector", selv2)
                    selm_done = ev.n

                    def selcast(v, qc=qc, selm_done=selm_done):
                        v.wait_ge(ev.h, selm_done)
                        cx.war(v, "pselbf")
                        i = v.tensor_copy(T["xatt_tok"][:, :], T["pscr"][:, :])
                        ev.inc(i, 1)
                    cx.on("vector", selcast)
                    selc_done = ev.n

                    def selt(t, qc=qc, selc_done=selc_done):
                        t.wait_ge(ev.h, selc_done)
                        cx.war(t, "ptrA")
                        i = t.transpose(ptrA[:, 0:128], T["xatt_tok"][:, :],
                                        ident[:, :])
                        lnt.inc(i, 1)
                    cx.on("tensor", selt)
                    cx.use("pselbf", lnt, lnt.n)
                    t_done = lnt.n

                    def seltev(v, qc=qc, t_done=t_done):
                        v.wait_ge(lnt.h, t_done)
                        cx.war(v, "selTw")
                        i = v.tensor_copy(T["selT"][:, qc * 128:(qc + 1) * 128],
                                          ptrA[:, 0:128])
                        rp.inc(i, 1)
                    cx.on("vector", seltev)
                    cx.use("ptrA", rp, rp.n)
                selT_ready = rp.n

                for gh in range(DBG["ngh"]):
                    h = kvh * GH + gh
                    qsl = slice(gh * G512, (gh + 1) * G512)

                    if True:
                      pass
                    def cmm(t, kvh=kvh, qsl=qsl, q4_ready=q4_ready,
                            mlp_ready=mlp_ready):
                        t.wait_ge(lns.h, max(q4_ready, mlp_ready))
                        cx.war(t, "psA")
                        i = t.matmul(psA[:, 0:512],
                                     T["ckT"][:, kvh * 128:(kvh + 1) * 128],
                                     T["q4"][:, qsl], start=True, stop=True)
                        mm.inc(i, 1)
                        cx.war(t, "psU")
                        i = t.matmul(psU[0:1, 0:512],
                                     T["memkT_s"][:, kvh:kvh + 1],
                                     T["q4"][:, qsl], start=True, stop=True)
                        mm.inc(i, 1)
                    cx.on("tensor", cmm) if DBG["cmp"] else None
                    cmm_done = mm.n

                    def cexp(s_, cmm_done=cmm_done):
                        s_.wait_ge(mm.h, cmm_done)
                        cx.war(s_, "exp0")
                        i = s_.activation(expsl[0][:, :], psA[:, 0:512], AF.Exp,
                                          scale=SC0)
                        chs.inc(i, 1)
                        cx.war(s_, "exp1")
                        i = s_.activation(T["exp1"][0:1, 0:512], psU[0:1, 0:512], AF.Exp,
                                          scale=SC0)
                        lns.inc(i, 1)
                    cx.on("scalar", cexp) if DBG["cmp"] else None
                    cx.use("psA", lns, lns.n)
                    cx.use_add("psU", lns, lns.n)
                    cexp_done = lns.n

                    def cmask(v, cexp_done=cexp_done, gconst_ld=gconst_ld):
                        v.wait_ge(lns.h, cexp_done)
                        v.wait_ge(pld.h, gconst_ld)
                        i = v.tensor_tensor(expsl[0][:, :], expsl[0][:, :],
                                            T["cmpvis_g"][:, :], op=ALU.mult)
                        chv.inc(i, 1); v.wait_ge(chv.h, chv.n)
                    cx.on("vector", cmask) if DBG["cmp"] else None
                    cmask_done = chv.n

                    def cpv(t, kvh=kvh, cmask_done=cmask_done, cvbm_ready=cvbm_ready):
                        t.wait_ge(chv.h, cmask_done)
                        t.wait_ge(rp.h, cvbm_ready)
                        cx.war(t, "psC")
                        cx.war(t, "psZ")
                        for qc in range(NTT):
                            t.matmul(psC[:, qc * 128:(qc + 1) * 128],
                                     expsl[0][:, qc * 128:(qc + 1) * 128],
                                     T["cvbm"][:, kvh * 128:(kvh + 1) * 128],
                                     start=(qc == 0), stop=False,
                                     skip_group_check=True)
                            t.matmul(psC[:, qc * 128:(qc + 1) * 128],
                                     T["exp1"][0:1, qc * 128:(qc + 1) * 128],
                                     T["memv_s"][:, kvh * 128:(kvh + 1) * 128],
                                     start=False, stop=True,
                                     skip_group_check=True)
                            t.matmul(psZ[:, qc * 3:qc * 3 + 1],
                                     expsl[0][:, qc * 128:(qc + 1) * 128],
                                     T["onescb"][:, :], start=(qc == 0), stop=False,
                                     skip_group_check=True)
                            i = t.matmul(psZ[:, qc * 3:qc * 3 + 1],
                                         T["exp1"][0:1, qc * 128:(qc + 1) * 128],
                                         T["onescb"][0:1, :], start=False, stop=True,
                                         skip_group_check=True)
                        mm.inc(i, 1)
                    cx.on("tensor", cpv) if DBG["cmp"] else None
                    cx.use("exp0", mm, mm.n)
                    cx.use_add("exp1", mm, mm.n)

                    def hr0(t, qsl=qsl, q4_ready=q4_ready):
                        t.wait_ge(lns.h, q4_ready)
                        cx.war(t, "psU")
                        i = t.matmul(psU[:, 0:512], T["ropeP"][:, :],
                                     T["q4"][:, qsl], start=True, stop=True)
                        mm.inc(i, 1)
                    cx.on("tensor", hr0)
                    hr0_done = mm.n

                    def hr1(v, qsl=qsl, hr0_done=hr0_done, q4_ready=q4_ready,
                            gconst_ld=gconst_ld):
                        v.wait_ge(lns.h, q4_ready)
                        v.wait_ge(pld.h, gconst_ld)
                        cx.war(v, "rt12")
                        i = v.tensor_tensor(T["rt1"][:, :], T["q4"][:, qsl],
                                            T["ropeCS"][:, 0:512], op=ALU.mult)
                        chv.inc(i, 1); v.wait_ge(chv.h, chv.n)
                        v.wait_ge(mm.h, hr0_done)
                        i = v.tensor_tensor(T["rt2"][:, :], psU[:, 0:512],
                                            T["ropeCS"][:, 512:1024], op=ALU.mult)
                        chv.inc(i, 1); v.wait_ge(chv.h, chv.n)
                        cx.use_add("psU", chv, chv.n)
                        cx.war(v, "rqh")
                        i = v.tensor_tensor(T["rqh"][:, :], T["rt1"][:, :],
                                            T["rt2"][:, :], op=ALU.add)
                        rp.inc(i, 1)
                        cx.use("rt12", rp, rp.n)
                    cx.on("vector", hr1)
                    rqh_ready = rp.n

                    NKB = 4 * (g + 1)
                    for kb in range(NKB if DBG["fine"] else 0):
                        sps, spsname = ((psA, "psA"), (psB, "psB"))[kb % 2]
                        slot = 1 + (kb % 2)
                        esb = expsl[slot]
                        def fmm(t, kvh=kvh, kb=kb, sps=sps, spsname=spsname,
                                rqh_ready=rqh_ready, kv_ready=kv_ready,
                                selT_ready=selT_ready):
                            t.wait_ge(rp.h, max(rqh_ready, kv_ready, selT_ready))
                            cx.war(t, spsname)
                            t.matmul(sps[:, 0:512],
                                     rkT[:, kvh * S + kb * 128: kvh * S + (kb + 1) * 128],
                                     T["rqh"][:, :], start=True, stop=False)
                            i = t.matmul(sps[:, 0:512],
                                         T["selE_s"][:, kb * 128:(kb + 1) * 128],
                                         T["selT"][:, :], start=False, stop=True)
                            mm.inc(i, 1)
                        cx.on("tensor", fmm)
                        fmm_done = mm.n
                        cx.use_add("selTw", mm, mm.n)

                        def fexp(s_, sps=sps, esb=esb, slot=slot, fmm_done=fmm_done):
                            s_.wait_ge(mm.h, fmm_done)
                            cx.war(s_, f"exp{slot}")
                            i = s_.activation(esb[:, :], sps[:, 0:512], AF.Exp,
                                              scale=SC0)
                            lns.inc(i, 1)
                        cx.on("scalar", fexp)
                        cx.use(spsname, lns, lns.n)
                        fexp_done = lns.n

                        diag = (kb >= 4 * g)
                        if diag:
                            off = kb - 4 * g
                            def fmask(v, esb=esb, off=off, fexp_done=fexp_done):
                                v.wait_ge(lns.h, fexp_done)
                                i = v.tensor_tensor(
                                    esb[:, :], esb[:, :],
                                    T["fdiag_s"][:, off * 512:(off + 1) * 512],
                                    op=ALU.mult)
                                chv.inc(i, 1); v.wait_ge(chv.h, chv.n)
                            cx.on("vector", fmask)
                        fm_done = chv.n

                        def fpv(t, kvh=kvh, kb=kb, esb=esb, slot=slot, NKB=NKB,
                                diag=diag, fm_done=fm_done, fexp_done=fexp_done):
                            if diag:
                                t.wait_ge(chv.h, fm_done)
                            else:
                                t.wait_ge(lns.h, fexp_done)
                            if kb == 0:
                                cx.war(t, "psF")
                            for qc in range(NTT):
                                t.matmul(psF[:, qc * 128:(qc + 1) * 128],
                                         esb[:, qc * 128:(qc + 1) * 128],
                                         vtok[:, kvh * S + kb * 128: kvh * S + (kb + 1) * 128],
                                         start=(kb == 0 and qc == 0),
                                         stop=(kb == NKB - 1),
                                         skip_group_check=True)
                                i = t.matmul(psZ[:, qc * 3 + 1:qc * 3 + 2],
                                             esb[:, qc * 128:(qc + 1) * 128],
                                             T["onescb"][:, :],
                                             start=False,
                                             stop=(kb == NKB - 1),
                                             skip_group_check=True)
                            mm.inc(i, 1)
                        cx.on("tensor", fpv)
                        cx.use(f"exp{slot}", mm, mm.n)

                    kb0 = max(0, 4 * g - 1)
                    for kb in range(kb0, 4 * g + 4) if DBG["slide"] else []:
                        sps, spsname = ((psA, "psA"), (psB, "psB"))[kb % 2]
                        slot = 1 + (kb % 2)
                        esb = expsl[slot]
                        first = (kb == kb0)
                        last = (kb == 4 * g + 3)
                        def smm2(t, kvh=kvh, kb=kb, sps=sps, spsname=spsname,
                                 rqh_ready=rqh_ready, kv_ready=kv_ready):
                            t.wait_ge(rp.h, max(rqh_ready, kv_ready))
                            cx.war(t, spsname)
                            i = t.matmul(sps[:, 0:512],
                                         rkT[:, kvh * S + kb * 128: kvh * S + (kb + 1) * 128],
                                         T["rqh"][:, :], start=True, stop=True)
                            mm.inc(i, 1)
                        cx.on("tensor", smm2)
                        smm2_done = mm.n

                        def sexp2(s_, sps=sps, esb=esb, slot=slot,
                                  smm2_done=smm2_done):
                            s_.wait_ge(mm.h, smm2_done)
                            cx.war(s_, f"exp{slot}")
                            i = s_.activation(esb[:, :], sps[:, 0:512], AF.Exp,
                                              scale=SC0)
                            lns.inc(i, 1)
                        cx.on("scalar", sexp2)
                        cx.use(spsname, lns, lns.n)
                        sexp2_done = lns.n

                        off = kb - 4 * g + 1
                        def smask(v, esb=esb, off=off, sexp2_done=sexp2_done):
                            v.wait_ge(lns.h, sexp2_done)
                            i = v.tensor_tensor(
                                esb[:, :], esb[:, :],
                                T["band_s"][:, off * 512:(off + 1) * 512],
                                op=ALU.mult)
                            chv.inc(i, 1); v.wait_ge(chv.h, chv.n)
                        cx.on("vector", smask)
                        sm_done = chv.n

                        def spv(t, kvh=kvh, kb=kb, esb=esb, slot=slot,
                                first=first, last=last, sm_done=sm_done):
                            t.wait_ge(chv.h, sm_done)
                            if first:
                                cx.war(t, "psS")
                            for qc in range(NTT):
                                t.matmul(psS[:, qc * 128:(qc + 1) * 128],
                                         esb[:, qc * 128:(qc + 1) * 128],
                                         vtok[:, kvh * S + kb * 128: kvh * S + (kb + 1) * 128],
                                         start=(first and qc == 0), stop=last,
                                         skip_group_check=True)
                                i = t.matmul(psZ[:, qc * 3 + 2:qc * 3 + 3],
                                             esb[:, qc * 128:(qc + 1) * 128],
                                             T["onescb"][:, :],
                                             start=False, stop=last,
                                             skip_group_check=True)
                            mm.inc(i, 1)
                        cx.on("tensor", spv)
                        cx.use(f"exp{slot}", mm, mm.n)
                    allpv_done = mm.n

                    def zev2(v, h=h, allpv_done=allpv_done):
                        v.wait_ge(mm.h, allpv_done)
                        v.wait_ge(rp.h, 0)
                        i = v.tensor_copy(T["Zsb"][:, :], psZ[:, 0:12])
                        chv.inc(i, 1); v.wait_ge(chv.h, chv.n)
                        cx.use("psZ", chv, chv.n)
                        i = v.reciprocal(T["Zinv"][:, :], T["Zsb"][:, :])
                        chv.inc(i, 1); v.wait_ge(chv.h, chv.n)
                        gv = T["gsig"][:, :].rearrange("p (qc x) -> p qc x", x=96)[
                            :, :, 3 * h: 3 * h + 3]
                        zv = T["Zinv"][:, :].rearrange("p (qc b) -> p qc b", b=3)
                        gzv = T["gz"][:, :].rearrange("p (qc b) -> p qc b", b=3)
                        i = v.tensor_tensor(gzv, gv, zv, op=ALU.mult)
                        chv.inc(i, 1); v.wait_ge(chv.h, chv.n)
                    cx.on("vector", zev2)
                    gz_done = chv.n

                    for qc in range(NTT):
                        def comb(v, qc=qc, gz_done=gz_done):
                            v.wait_ge(chv.h, gz_done)
                            cx.war(v, "rt12")
                            i = v.tensor_scalar(T["rt1"][:, 0:128], psC[:, qc * 128:(qc + 1) * 128],
                                                T["gz"][:, qc * 3:qc * 3 + 1], None,
                                                op0=ALU.mult)
                            chv.inc(i, 1); v.wait_ge(chv.h, chv.n)
                            i = v.scalar_tensor_tensor(
                                T["rt2"][:, 0:128], psF[:, qc * 128:(qc + 1) * 128],
                                T["gz"][:, qc * 3 + 1:qc * 3 + 2],
                                T["rt1"][:, 0:128], op0=ALU.mult, op1=ALU.add)
                            chv.inc(i, 1); v.wait_ge(chv.h, chv.n)
                            cx.war(v, "xatt_tok_w")
                            i = v.scalar_tensor_tensor(
                                T["xatt_tok"][:, :], psS[:, qc * 128:(qc + 1) * 128],
                                T["gz"][:, qc * 3 + 2:qc * 3 + 3],
                                T["rt2"][:, 0:128], op0=ALU.mult, op1=ALU.add)
                            ev.inc(i, 1)
                            cx.use("rt12", ev, ev.n)
                        cx.on("vector", comb)
                        comb_done = ev.n
                        if qc == NTT - 1:
                            cx.use("psC", ev, ev.n)
                            cx.use("psF", ev, ev.n)
                            cx.use("psS", ev, ev.n)

                        def xt(t, comb_done=comb_done):
                            t.wait_ge(ev.h, comb_done)
                            cx.war(t, "ptrA")
                            i = t.transpose(ptrA[:, 0:128], T["xatt_tok"][:, :],
                                            ident[:, :])
                            lnt.inc(i, 1)
                        cx.on("tensor", xt)
                        cx.use("xatt_tok_w", lnt, lnt.n)
                        t_done = lnt.n

                        def xtev(v, gh=gh, qc=qc, t_done=t_done):
                            v.wait_ge(lnt.h, t_done)
                            if qc == 0:
                                cx.war(v, "xattT_kvh")
                            i = v.tensor_copy(
                                T["xattT_kvh"][:, qc * 128:(qc + 1) * 128],
                                ptrA[:, 0:128])
                            rp.inc(i, 1)
                        cx.on("vector", xtev)
                        cx.use("ptrA", rp, rp.n)
                    xatt_done = rp.n

                    def xst(y, h=h, g=g, xatt_done=xatt_done):
                        y.wait_ge(rp.h, xatt_done)
                        i = y.dma_start(
                            out=xattT_out[h * 128:(h + 1) * 128,
                                          g * G512:(g + 1) * G512],
                            in_=T["xattT_kvh"][:, :])
                        st.inc(i, 16)
                    cx.on("sync", xst)
                    cx.use("xattT_kvh", st, st.n)
                cx.use("q4", mm, mm.n)
                cx.use_add("q4", rp, rp.n)
            cx.use("xlnT", mm, mm.n)
            cx.use("selvis_g", chv, chv.n)
            cx.use("ownm1_g", ev, ev.n)
            cx.use("cmpvis_g", chv, chv.n)
            cx.use("ropeCS", chv, chv.n)
            cx.use("ckT", mm, mm.n)
            cx.use("cvT", mm, mm.n)
            cx.use("cvbm", mm, mm.n)
            cx.use("gsig", chv, chv.n)
            cx.use("q4", mm, mm.n)
            cx.use_add("q4", rp, rp.n)

        if DBG["dump"]:
            def dmpf(y):
                y.wait_ge(rp.h, rp.n)
                y.wait_ge(ev.h, ev.n)
                y.wait_ge(mm.h, mm.n)
                for nm in dmp:
                    st.inc(y.dma_start(out=dmp[nm], in_=T[nm][:, :]), 16)
            cx.on("sync", dmpf)
        final_st = st.n
        def fin(y, final_st=final_st):
            y.wait_ge(st.h, final_st)
        cx.on("sync", fin)
        if fz is not None:
            fz["barrier"] = (st.h, st.n)
    return nc



def build_proj_residual(fz=None):
    nc, pre, io = fz_begin(fz)
    xres = io.get("xres") if fz else nc.dram_tensor("xres", [S, D], F32, kind="ExternalInput").ap()
    xT = io.get("xT") if fz else nc.dram_tensor("xT", [D, S], BF16, kind="ExternalInput").ap()
    wT = nc.dram_tensor(pre + "wT", [D, D], F32, kind="ExternalInput").ap()
    xout = io.get("xout") if fz else nc.dram_tensor("xout", [S, D], F32, kind="ExternalOutput").ap()

    OB = 256
    NOB = D // OB

    es = contextlib.ExitStack()
    with es:
        tens, sems, block = alloc(
            nc, es,
            sbufs=[("lh0", [128, NIC * 128], BF16), ("lh1", [128, NIC * 128], BF16),
                   ("wbf0", [128, NIC * OB], BF16), ("wbf1", [128, NIC * OB], BF16),
                   ("xr", [128, 2 * D], F32),
                   ("outsb", [128, 2 * OB], F32)],
            psums=[("psA", [128, 512], F32), ("psB", [128, 512], F32)],
            semnames=["ld", "wld", "mm", "ev", "st"],
            sem_es=(fz["es"] if fz else None), prefix=pre)
        cx = Ctx(nc, block, sems)
        fz_barrier(cx, fz)
        ld, wld, mm, ev, st = (sems[k] for k in ("ld", "wld", "mm", "ev", "st"))
        lh = [tens["lh0"], tens["lh1"]]
        wb = [tens["wbf0"], tens["wbf1"]]
        xr = tens["xr"]; outsb = tens["outsb"]
        psA, psB = tens["psA"], tens["psB"]

        for ob in range(NOB):
            par = ob % 2
            wbf = wb[par]
            def wload(y, ob=ob, wbf=wbf, par=par):
                cx.war(y, f"wb{par}")
                src = wT.rearrange("(ic p) o -> p ic o", p=128)
                for quarter in range(4):
                    dst = wbf[:, quarter * 8 * OB:(quarter + 1) * 8 * OB]
                    dst = dst.rearrange("p (ic o) -> p ic o", o=OB)
                    i = y.dma_start(out=dst,
                                    in_=src[:, quarter * 8:(quarter + 1) * 8,
                                            ob * OB:(ob + 1) * OB])
                    wld.inc(i, 16)
            cx.on("gpsimd", wload)
            wld_done = wld.n

            for tt in range(16):
                tpar = tt % 2
                lhb = lh[tpar]
                def lhload(y, ob=ob, tt=tt, lhb=lhb, tpar=tpar):
                    cx.war(y, f"lh{tpar}")
                    src = xT.rearrange("(ic p) t -> p ic t", p=128)
                    i = y.dma_start(
                        out=lhb[:, :].rearrange("p (ic t) -> p ic t", t=128),
                        in_=src[:, :, tt * 128:(tt + 1) * 128])
                    wld.inc(i, 16)
                    cx.war(y, f"xr{tpar}")
                    i = y.dma_start(out=xr[:, tpar * D:tpar * D + D],
                                    in_=xres[tt * 128:(tt + 1) * 128, :])
                    ld.inc(i, 16)
                cx.on("sync", lhload)
                lh_done = wld.n
                xr_done = ld.n

                ps, psname = ((psA, "psA"), (psB, "psB"))[tpar]
                def mmf(t, ob=ob, tt=tt, lhb=lhb, wbf=wbf, ps=ps, psname=psname,
                        wld_done=wld_done, lh_done=lh_done):
                    t.wait_ge(wld.h, lh_done)
                    cx.war(t, psname)
                    for ic in range(NIC):
                        i = t.matmul(ps[:, 0:OB],
                                     lhb[:, ic * 128:(ic + 1) * 128],
                                     wbf[:, ic * OB:(ic + 1) * OB],
                                     start=(ic == 0), stop=(ic == NIC - 1))
                    mm.inc(i, 1)
                cx.on("tensor", mmf)
                mm_done = mm.n
                cx.use(f"lh{tpar}", mm, mm.n)
                if tt == 15:
                    cx.use(f"wb{par}", mm, mm.n)

                slot = tt % 2
                def evf(v, ob=ob, tt=tt, ps=ps, mm_done=mm_done, slot=slot,
                        xr_done=xr_done, tpar=tpar):
                    v.wait_ge(mm.h, mm_done)
                    v.wait_ge(ld.h, xr_done)
                    cx.war(v, f"outsb{slot}")
                    i = v.tensor_tensor(
                        outsb[:, slot * OB:(slot + 1) * OB], ps[:, 0:OB],
                        xr[:, tpar * D + ob * OB: tpar * D + (ob + 1) * OB],
                        op=ALU.add)
                    ev.inc(i, 1)
                cx.on("vector", evf)
                cx.use(psname, ev, ev.n)
                cx.use_add(f"xr{tpar}", ev, ev.n)
                ev_done = ev.n

                def stf(y, ob=ob, tt=tt, ev_done=ev_done, slot=slot):
                    y.wait_ge(ev.h, ev_done)
                    i = y.dma_start(out=xout[tt * 128:(tt + 1) * 128,
                                             ob * OB:(ob + 1) * OB],
                                    in_=outsb[:, slot * OB:(slot + 1) * OB])
                    st.inc(i, 16)
                cx.on("gpsimd", stf)
                cx.use(f"outsb{slot}", st, st.n)

        final_st = st.n
        def fin(y, final_st=final_st):
            y.wait_ge(st.h, final_st)
        cx.on("gpsimd", fin)
        if fz is not None:
            fz["barrier"] = (st.h, st.n)
    return nc


def prep_l0_attn_inputs(inputs, consts, xcur, pre=""):
    f32 = np.float32
    g = lambda k: np.asarray(inputs[k], dtype=f32)
    qkvwT = np.ascontiguousarray(g("L0_attn_w_qkv").T)
    wgT = np.ascontiguousarray(g("L0_attn_w_gate").T)
    bgrow = np.ascontiguousarray(g("L0_attn_b_gate").reshape(1, 96))
    kw1T = np.ascontiguousarray(g("L0_attn_k_w1").T)
    vw1T = np.ascontiguousarray(g("L0_attn_v_w1").T)
    kw2T = np.ascontiguousarray(g("L0_attn_k_w2").T)
    vw2T = np.ascontiguousarray(g("L0_attn_v_w2").T)
    kb1r = np.ascontiguousarray(g("L0_attn_k_b1").reshape(16, 128).T)
    vb1r = np.ascontiguousarray(g("L0_attn_v_b1").reshape(16, 128).T)
    kb2c = np.ascontiguousarray(g("L0_attn_k_b2").reshape(128, 1))
    vb2c = np.ascontiguousarray(g("L0_attn_v_b2").reshape(128, 1))
    kp = g("L0_attn_k_pos")
    vp = g("L0_attn_v_pos")
    kpT = np.ascontiguousarray(np.stack(
        [np.tile(kp[h], (32, 1)).T for h in range(8)]))
    vpT = np.ascontiguousarray(np.stack(
        [np.tile(vp[h], (32, 1)).T for h in range(8)]))
    mem = g("L0_attn_mem_kv")
    memkT = np.ascontiguousarray(mem[0, :, 0, :].T.astype(BF))
    memv = np.ascontiguousarray(mem[1, :, 0, :].reshape(1, 8 * 128).astype(BF))
    maps = []
    for core in range(8):
        maps.append({
            "xin": np.ascontiguousarray(xcur[core]),
            "qkvwT": qkvwT, "wgT": wgT, "bgrow": bgrow,
            "kw1T": kw1T, "vw1T": vw1T, "kw2T": kw2T, "vw2T": vw2T,
            "kb1r": kb1r, "vb1r": vb1r, "kb2c": kb2c, "vb2c": vb2c,
            "kpT": kpT, "vpT": vpT, "memkT": memkT, "memv": memv,
            "ident_bf": consts["ident_bf"],
            "ones_col_bf": consts["ones_col_bf"],
            "ones_row_bf": consts["ones_row_bf"],
            "ropeC0": consts["rope0_C"], "ropeS0": consts["rope0_S"],
            "ropeP0": consts["rope0_P"],
            "selvis": consts["sel_vis_add"],
            "ownm1": (consts["own_oh"] - 1.0).astype(np.float32),
            "cmpvis": consts["cmp_vis01"],
            "fdiag": consts["fine_diag01"], "band": consts["band01"],
            "selE": consts["selE"],
        })
    return maps



def build_l1_attn(fz=None):
    nc, pre, io = fz_begin(fz)
    xin = io.get("xin") if fz else nc.dram_tensor("xin", [S, D], F32, kind="ExternalInput").ap()
    qkvwT = nc.dram_tensor(pre + "qkvwT", [D, 4608], F32, kind="ExternalInput").ap()
    qkvbT = nc.dram_tensor(pre + "qkvbT", [128, 36], F32, kind="ExternalInput").ap()
    dwT = nc.dram_tensor(pre + "dwT", [D, D], F32, kind="ExternalInput").ap()
    ident_in = nc.dram_tensor(pre + "ident_bf", [128, 128], BF16, kind="ExternalInput").ap()
    onesc_in = nc.dram_tensor(pre + "ones_col_f", [128, 1], F32, kind="ExternalInput").ap()
    onesr_in = nc.dram_tensor(pre + "ones_row_f", [1, 128], F32, kind="ExternalInput").ap()
    ropeC_in = nc.dram_tensor(pre + "ropeC", [64, S], F32, kind="ExternalInput").ap()
    ropeS_in = nc.dram_tensor(pre + "ropeS", [64, S], F32, kind="ExternalInput").ap()
    ropeP_in = nc.dram_tensor(pre + "ropeP", [128, 64], BF16, kind="ExternalInput").ap()
    diag_in = nc.dram_tensor(pre + "diag01", [4, 128, 512], BF16, kind="ExternalInput").ap()
    xout = io.get("xout") if fz else nc.dram_tensor("xout", [S, D], F32, kind="ExternalOutput").ap()

    NH = 32
    GQ = 16
    SCALE = 1.0 / np.sqrt(128.0)

    es = contextlib.ExitStack()
    with es:
        tens, sems, block = alloc(
            nc, es,
            sbufs=[
                ("x_g", [128, D], F32), ("xlnT", [128, NIC * G512], BF16),
                ("bs", [128, 48], F32), ("stats", [128, 16], F32),
                ("xln_tok", [128, D], BF16), ("ident", [128, 128], BF16),
                ("eps_col", [128, 1], F32),
                ("onesc", [128, 1], F32), ("onesr", [1, 128], F32),
                ("biasT", [128, 36], F32),
                ("ropeCS", [128, S], F32),
                ("ropeP_s", [128, 64], BF16), ("diag01_s", [128, 4 * 512], BF16),
                ("qw0", [128, NIC * 128], BF16), ("qw1", [128, NIC * 128], BF16),
                ("rq", [128, NH * G512], BF16),
                ("ktmp", [128, 2 * G512], BF16),
                ("krT", [128, 2 * S], BF16),
                ("vtmp", [128, 2 * G512], BF16),
                ("vtok", [128, 2 * S], BF16),
                ("ctxT", [128, NH * G512], BF16),
                ("expS0", [128, 512], BF16), ("expS1", [128, 512], BF16),
                ("expS2", [128, 512], BF16),
                ("Zacc", [128, 512], F32), ("zrow", [1, 512], F32),
                ("zinv", [1, 512], F32), ("zb_sb", [128, 512], F32),
                ("rt1", [64, 512], F32), ("rt2", [64, 512], F32),
                ("dw0", [128, NIC * 128], BF16), ("dw1", [128, NIC * 128], BF16),
                ("outsb", [128, 2 * 128], F32), ("xres", [128, NTT * 128], F32),
            ],
            psums=[("psA", [128, 512], F32), ("psB", [128, 512], F32),
                   ("psC", [128, 512], F32), ("psD", [128, 512], F32),
                   ("psE", [128, 512], F32),
                   ("ptrA", [128, 1024], BF16), ("ptrB", [128, 1024], BF16)],
            semnames=["ld", "lnv", "lns", "lnt", "lne", "wld", "mm",
                      "ev", "st", "chv", "chs", "rp"],
            sem_es=(fz["es"] if fz else None), prefix=pre)
        cx = Ctx(nc, block, sems)
        fz_barrier(cx, fz)
        ld, lnv, lns, lnt, lne, wld, mm, ev, st, chv, chs, rp = (
            sems[k] for k in ("ld", "lnv", "lns", "lnt", "lne", "wld", "mm",
                              "ev", "st", "chv", "chs", "rp"))
        x_g = tens["x_g"]; xlnT = tens["xlnT"]; ident = tens["ident"]
        rq = tens["rq"]; krT = tens["krT"]; vtok = tens["vtok"]
        ktmp = tens["ktmp"]; vtmp = tens["vtmp"]; ctxT = tens["ctxT"]
        Zacc = tens["Zacc"]; zrow = tens["zrow"]; zinv = tens["zinv"]
        zb_sb = tens["zb_sb"]; rt1 = tens["rt1"]; rt2 = tens["rt2"]
        psA, psB, psC, psD, psE = (tens[k] for k in ("psA", "psB", "psC", "psD", "psE"))
        ptrA, ptrB = tens["ptrA"], tens["ptrB"]
        outsb = tens["outsb"]; xres = tens["xres"]
        expS = [tens["expS0"], tens["expS1"], tens["expS2"]]

        sb = {"x_g": x_g, "xlnT": xlnT, "bs": tens["bs"], "stats": tens["stats"],
              "xln_tok": tens["xln_tok"], "ident": ident, "ptrA": ptrA,
              "ptrB": ptrB, "eps_col": tens["eps_col"]}

        def init_eps(v):
            v.memset(tens["eps_col"][:, :], EPS)
        cx.on("vector", init_eps)

        def prolog_dma(y):
            for dst, src in ((ident, ident_in), (tens["onesc"], onesc_in),
                             (tens["onesr"], onesr_in), (tens["biasT"], qkvbT),
                             (tens["ropeP_s"], ropeP_in)):
                ld.inc(y.dma_start(out=dst[:, :], in_=src[:, :]), 16)
            ld.inc(y.dma_start(out=tens["ropeCS"][0:64, :], in_=ropeC_in[:, :]), 16)
            ld.inc(y.dma_start(out=tens["ropeCS"][64:128, :], in_=ropeS_in[:, :]), 16)
            for j in range(4):
                ld.inc(y.dma_start(out=tens["diag01_s"][:, j * 512:(j + 1) * 512],
                                   in_=diag_in[j, :, :]), 16)
        cx.on("sync", prolog_dma)
        pro_ld = ld.n

        for eng in ("tensor", "vector", "scalar"):
            def gate(e, pro_ld=pro_ld):
                e.wait_ge(ld.h, pro_ld)
            cx.on(eng, gate)

        for g in range(NG):
            emit_ln_group(cx, g, xin, sb, xg_tiled=False)
            xln_ready = lne.n

            for ob in range(36):
                par = ob % 2
                qw, qwname = ((tens["qw0"], "qw0"), (tens["qw1"], "qw1"))[par]

                def wload(y, ob=ob, qw=qw, qwname=qwname):
                    cx.war(y, qwname)
                    src = qkvwT.rearrange("(ic p) o -> p ic o", p=128)
                    for half in range(2):
                        dst = qw[:, half * 16 * 128:(half + 1) * 16 * 128]
                        dst = dst.rearrange("p (ic o) -> p ic o", o=128)
                        i = y.dma_start(
                            out=dst,
                            in_=src[:, half * 16:(half + 1) * 16,
                                    ob * 128:(ob + 1) * 128])
                        wld.inc(i, 16)
                cx.on("gpsimd", wload)
                wld_done = wld.n

                ps, psname = ((psA, "psA"), (psB, "psB"))[par]
                def pmm(t, ob=ob, qw=qw, ps=ps, psname=psname,
                        wld_done=wld_done, xln_ready=xln_ready):
                    t.wait_ge(wld.h, wld_done)
                    t.wait_ge(lne.h, xln_ready)
                    cx.war(t, psname)
                    for ic in range(NIC):
                        i = t.matmul(ps[:, 0:512],
                                     qw[:, ic * 128:(ic + 1) * 128],
                                     xlnT[:, ic * G512:(ic + 1) * G512],
                                     start=(ic == 0), stop=(ic == NIC - 1))
                    mm.inc(i, 1)
                cx.on("tensor", pmm)
                cx.use(qwname, mm, mm.n)
                mm_done = mm.n

                if ob < 32:
                    dst, dname, doff = rq, "rq", ob * G512
                elif ob < 34:
                    dst, dname, doff = ktmp, "ktmp", (ob - 32) * G512
                else:
                    dst, dname, doff = vtmp, "vtmp", (ob - 34) * G512

                def pev(v, ob=ob, ps=ps, mm_done=mm_done, dst=dst,
                        dname=dname, doff=doff):
                    v.wait_ge(mm.h, mm_done)
                    cx.war(v, dname)
                    i = v.tensor_scalar(dst[:, doff:doff + G512], ps[:, 0:512],
                                        tens["biasT"][:, ob:ob + 1], None,
                                        op0=ALU.add)
                    lns.inc(i, 1)
                cx.on("vector", pev)
                cx.use(psname, lns, lns.n)
                ev_done = lns.n

                if ob < 34:
                    if ob < 32:
                        tgt, toff = rq, ob * G512
                    else:
                        tgt, toff = krT, (ob - 32) * S + g * G512
                    def r0(t, ev_done=ev_done, dst=dst, doff=doff):
                        t.wait_ge(lns.h, ev_done)
                        cx.war(t, "psD")
                        i = t.matmul(psD[0:64, 0:512], tens["ropeP_s"][:, :],
                                     dst[:, doff:doff + G512], start=True, stop=True)
                        mm.inc(i, 1)
                    cx.on("tensor", r0)
                    r0_done = mm.n

                    def r1(v, ev_done=ev_done, dst=dst, doff=doff, tgt=tgt,
                           toff=toff, g=g, r0_done=r0_done):
                        gsl = slice(g * G512, (g + 1) * G512)
                        v.wait_ge(lns.h, ev_done)
                        cx.war(v, "rt12")
                        i = v.tensor_tensor(rt1[:, :], dst[0:64, doff:doff + G512],
                                            tens["ropeCS"][0:64, gsl], op=ALU.mult)
                        chv.inc(i, 1); v.wait_ge(chv.h, chv.n)
                        v.wait_ge(mm.h, r0_done)
                        i = v.tensor_tensor(rt2[:, :], psD[0:64, 0:512],
                                            tens["ropeCS"][64:128, gsl], op=ALU.mult)
                        chv.inc(i, 1); v.wait_ge(chv.h, chv.n)
                        cx.use_add("psD", chv, chv.n)
                        i = v.tensor_tensor(tgt[0:64, toff:toff + G512],
                                            rt1[:, :], rt2[:, :], op=ALU.add)
                        rp.inc(i, 1)
                        cx.use("rt12", rp, rp.n)
                        if tgt is not dst:
                            i = v.tensor_copy(tgt[64:128, toff:toff + G512],
                                              dst[64:128, doff:doff + G512])
                            rp.inc(i, 1)
                    cx.on("vector", r1)
                else:
                    kvh = ob - 34
                    for tt in range(NTT):
                        bank, bn = ((ptrA, "ptrA"), (ptrB, "ptrB"))[tt % 2]
                        def vt(t, ev_done=ev_done, tt=tt, bank=bank, bn=bn,
                               kvh=kvh):
                            t.wait_ge(lns.h, ev_done)
                            cx.war(t, bn)
                            i = t.transpose(bank[:, 0:128],
                                            vtmp[:, kvh * G512 + tt * 128:
                                                 kvh * G512 + (tt + 1) * 128],
                                            ident[:, :])
                            lnt.inc(i, 1)
                        cx.on("tensor", vt) if DBG["vtr"] else None
                        t_done = lnt.n
                        def vtev(v, t_done=t_done, tt=tt, bank=bank, kvh=kvh,
                                 g=g):
                            v.wait_ge(lnt.h, t_done)
                            kb = g * NTT + tt
                            i = v.tensor_copy(
                                vtok[:, kvh * S + kb * 128: kvh * S + (kb + 1) * 128],
                                bank[:, 0:128])
                            rp.inc(i, 1)
                        cx.on("vector", vtev) if DBG["vtr"] else None
                        cx.use(bn, rp, rp.n)
            cx.use("xlnT", mm, mm.n)
            rope_ready = rp.n

            NKB = 4 * (g + 1)
            for h in range(NH):
                kvh = h // GQ
                zacc_done = 0
                for kb in range(NKB):
                    sps, spsname = ((psA, "psA"), (psB, "psB"))[kb % 2]
                    slot = kb % 3
                    esb = expS[slot]

                    def smm(t, kb=kb, h=h, kvh=kvh, sps=sps, spsname=spsname,
                            rope_ready=rope_ready):
                        t.wait_ge(rp.h, rope_ready)
                        cx.war(t, spsname)
                        i = t.matmul(
                            sps[:, 0:512],
                            krT[:, kvh * S + kb * 128: kvh * S + (kb + 1) * 128],
                            rq[:, h * G512:(h + 1) * G512],
                            start=True, stop=True)
                        mm.inc(i, 1)
                    cx.on("tensor", smm)
                    smm_done = mm.n

                    def sexp(s_, sps=sps, esb=esb, slot=slot, smm_done=smm_done):
                        s_.wait_ge(mm.h, smm_done)
                        cx.war(s_, f"expS{slot}")
                        i = s_.activation(esb[:, :], sps[:, 0:512], AF.Exp,
                                          scale=SCALE)
                        lns.inc(i, 1)
                    cx.on("scalar", sexp)
                    cx.use(spsname, lns, lns.n)
                    exp_done = lns.n

                    diag = (kb >= 4 * g)
                    if diag:
                        off = kb - 4 * g
                        def dmask(v, esb=esb, off=off, exp_done=exp_done):
                            v.wait_ge(lns.h, exp_done)
                            i = v.tensor_tensor(
                                esb[:, :], esb[:, :],
                                tens["diag01_s"][:, off * 512:(off + 1) * 512],
                                op=ALU.mult)
                            chv.inc(i, 1); v.wait_ge(chv.h, chv.n)
                        cx.on("vector", dmask)

                    def zadd(v, kb=kb, esb=esb, exp_done=exp_done, diag=diag):
                        if not diag:
                            v.wait_ge(lns.h, exp_done)
                        if kb == 0:
                            i = v.tensor_copy(Zacc[:, :], esb[:, :])
                        else:
                            i = v.tensor_tensor(Zacc[:, :], Zacc[:, :],
                                                esb[:, :], op=ALU.add)
                        chv.inc(i, 1); v.wait_ge(chv.h, chv.n)
                    cx.on("vector", zadd)
                    zacc_done = chv.n

                    def pv(t, kb=kb, kvh=kvh, esb=esb, NKB=NKB,
                           zacc_done=zacc_done):
                        t.wait_ge(chv.h, zacc_done)
                        if kb == 0:
                            cx.war(t, "psC")
                        i = t.matmul(
                            psC[:, 0:512],
                            vtok[:, kvh * S + kb * 128: kvh * S + (kb + 1) * 128],
                            esb[:, :],
                            start=(kb == 0), stop=(kb == NKB - 1))
                        mm.inc(i, 1)
                    cx.on("tensor", pv)
                    cx.use(f"expS{slot}", mm, mm.n)
                pv_done = mm.n

                def zred(t, zacc_done=zacc_done):
                    t.wait_ge(chv.h, zacc_done)
                    cx.war(t, "psE")
                    i = t.matmul(psE[0:1, 0:512], tens["onesc"][:, :],
                                 Zacc[:, :], start=True, stop=True)
                    mm.inc(i, 1)
                cx.on("tensor", zred)
                zred_done = mm.n
                cx.use("Zacc", mm, zred_done)

                def zfin(v, zred_done=zred_done):
                    v.wait_ge(mm.h, zred_done)
                    i = v.tensor_copy(zrow[:, :], psE[0:1, 0:512])
                    chv.inc(i, 1); v.wait_ge(chv.h, chv.n)
                    i = v.reciprocal(zinv[:, :], zrow[:, :])
                    ev.inc(i, 1)
                cx.on("vector", zfin)
                cx.use("psE", ev, ev.n)
                zinv_done = ev.n

                def zbc(t, zinv_done=zinv_done):
                    t.wait_ge(ev.h, zinv_done)
                    cx.war(t, "psE")
                    i = t.matmul(psE[:, 0:512], tens["onesr"][:, :],
                                 zinv[:, :], start=True, stop=True)
                    mm.inc(i, 1)
                cx.on("tensor", zbc)
                zbc_done = mm.n

                def ctxev(v, h=h, zbc_done=zbc_done, pv_done=pv_done):
                    v.wait_ge(mm.h, max(zbc_done, pv_done))
                    i = v.tensor_copy(zb_sb[:, :], psE[:, 0:512])
                    chv.inc(i, 1); v.wait_ge(chv.h, chv.n)
                    cx.use("psE", chv, chv.n)
                    cx.war(v, "ctxT")
                    i = v.tensor_tensor(ctxT[:, h * G512:(h + 1) * G512],
                                        psC[:, 0:512], zb_sb[:, :], op=ALU.mult)
                    rp.inc(i, 1)
                cx.on("vector", ctxev)
                cx.use("psC", rp, rp.n)
            ctx_ready = rp.n

            for ob in range(32):
                par = ob % 2
                dw, dwname = ((tens["dw0"], "dw0"), (tens["dw1"], "dw1"))[par]

                def dload(y, ob=ob, dw=dw, dwname=dwname):
                    cx.war(y, dwname)
                    src = dwT.rearrange("(ic p) o -> p ic o", p=128)
                    for half in range(2):
                        dsx = dw[:, half * 16 * 128:(half + 1) * 16 * 128]
                        dsx = dsx.rearrange("p (ic o) -> p ic o", o=128)
                        i = y.dma_start(
                            out=dsx,
                            in_=src[:, half * 16:(half + 1) * 16,
                                    ob * 128:(ob + 1) * 128])
                        wld.inc(i, 16)
                cx.on("gpsimd", dload)
                dld_done = wld.n

                def xload(y, ob=ob, g=g):
                    cx.war(y, "xres")
                    i = y.dma_start(
                        out=xres[:, :].rearrange("p (tt o) -> p tt o", o=128),
                        in_=xin[g * G512:(g + 1) * G512,
                                ob * 128:(ob + 1) * 128].rearrange(
                                    "(tt p) o -> p tt o", p=128))
                    ld.inc(i, 16)
                cx.on("sync", xload)
                xres_done = ld.n

                for tt in range(NTT):
                    ps, psname = ((psA, "psA"), (psB, "psB"))[tt % 2]
                    def dmm(t, ob=ob, tt=tt, dw=dw, ps=ps, psname=psname,
                            dld_done=dld_done, ctx_ready=ctx_ready):
                        t.wait_ge(wld.h, dld_done)
                        t.wait_ge(rp.h, ctx_ready)
                        cx.war(t, psname)
                        for hc in range(NIC):
                            i = t.matmul(
                                ps[:, 0:128],
                                ctxT[:, hc * G512 + tt * 128: hc * G512 + (tt + 1) * 128],
                                dw[:, hc * 128:(hc + 1) * 128],
                                start=(hc == 0), stop=(hc == NIC - 1))
                        mm.inc(i, 1)
                    cx.on("tensor", dmm)
                    mm_done = mm.n
                    if tt == NTT - 1:
                        cx.use(dwname, mm, mm.n)

                    slot = (ob * NTT + tt) % 2
                    def evf(v, tt=tt, ps=ps, mm_done=mm_done, slot=slot,
                            xres_done=xres_done):
                        v.wait_ge(mm.h, mm_done)
                        v.wait_ge(ld.h, xres_done)
                        cx.war(v, f"outsb{slot}")
                        i = v.tensor_tensor(
                            outsb[:, slot * 128:(slot + 1) * 128], ps[:, 0:128],
                            xres[:, tt * 128:(tt + 1) * 128], op=ALU.add)
                        ev.inc(i, 1)
                    cx.on("vector", evf)
                    cx.use(psname, ev, ev.n)
                    cx.use_add("xres", ev, ev.n)
                    ev_done = ev.n

                    def stf(y, ob=ob, tt=tt, ev_done=ev_done, g=g, slot=slot):
                        y.wait_ge(ev.h, ev_done)
                        t0 = g * G512 + tt * 128
                        i = y.dma_start(
                            out=xout[t0:t0 + 128, ob * 128:(ob + 1) * 128],
                            in_=outsb[:, slot * 128:(slot + 1) * 128])
                        st.inc(i, 16)
                    cx.on("gpsimd", stf)
                    cx.use(f"outsb{slot}", st, st.n)

            cx.use("ctxT", mm, mm.n)
            cx.use("rq", mm, mm.n)
            cx.use("ktmp", mm, mm.n)
            cx.use("vtmp", lnt, lnt.n)

        final_st = st.n
        def fin(y, final_st=final_st):
            y.wait_ge(st.h, final_st)
        cx.on("gpsimd", fin)
        if fz is not None:
            fz["barrier"] = (st.h, st.n)

    return nc


def prep_l1_attn_inputs(inputs, consts, xcur, pre=""):
    qkvwT = np.ascontiguousarray(np.asarray(inputs["L1_attn_qkv_w"], dtype=np.float32).T)
    qkvbT = np.ascontiguousarray(
        np.asarray(inputs["L1_attn_qkv_b"], dtype=np.float32).reshape(36, 128).T)
    dwT = np.ascontiguousarray(np.asarray(inputs["L1_attn_dense_w"], dtype=np.float32).T)
    maps = []
    for core in range(8):
        maps.append({
            "xin": np.ascontiguousarray(xcur[core]),
            "qkvwT": qkvwT, "qkvbT": qkvbT, "dwT": dwT,
            "ident_bf": consts["ident_bf"],
            "ones_col_f": consts["ones_col_f"],
            "ones_row_f": consts["ones_row_f"],
            "ropeC": np.ascontiguousarray(consts["rope1_C"][:64]),
            "ropeS": np.ascontiguousarray(consts["rope1_S"][:64]),
            "ropeP": consts["rope1_P"],
            "diag01": consts["l1_diag01"],
        })
    return maps



def build_l1_ffn(fz=None):
    nc, pre, io = fz_begin(fz)
    xin = io.get("xin") if fz else nc.dram_tensor("xin", [S, D], F32, kind="ExternalInput").ap()
    wupT = nc.dram_tensor(pre + "wupT", [D, D], F32, kind="ExternalInput").ap()
    atgw = nc.dram_tensor(pre + "atgw", [D, 36], F32, kind="ExternalInput").ap()
    btc = nc.dram_tensor(pre + "btc", [8, 4 * D], F32, kind="ExternalInput").ap()
    ebias = nc.dram_tensor(pre + "ebias", [1, 4], F32, kind="ExternalInput").ap()
    ident_in = nc.dram_tensor(pre + "ident_bf", [128, 128], BF16, kind="ExternalInput").ap()
    onesr_in = nc.dram_tensor(pre + "ones_row_f", [1, 128], F32, kind="ExternalInput").ap()
    xout = io.get("xout") if fz else nc.dram_tensor("xout", [S, D], F32, kind="ExternalOutput").ap()

    OB = 256
    NOB = D // OB

    es = contextlib.ExitStack()
    with es:
        tens, sems, block = alloc(
            nc, es,
            sbufs=[("x_g", [128, NTT * D], F32), ("xlnT", [128, NIC * G512], BF16),
                   ("bs", [128, 48], F32), ("stats", [128, 16], F32),
                   ("xln_tok", [128, D], BF16), ("ident", [128, 128], BF16),
                   ("onesr", [1, 128], F32), ("ebrow", [1, 4], F32),
                   ("ebb", [128, 4], F32), ("atgw_bf", [128, NIC * 36], BF16),
                   ("bt_bf", [8, 4 * D], BF16),
                   ("wbf0", [128, NIC * OB], BF16), ("wbf1", [128, NIC * OB], BF16),
                   ("z_sb", [128, 32], F32), ("lg", [128, 8], F32),
                   ("oh", [128, 8], F32), ("zm_bf", [128, 32], BF16),
                   ("zmT", [8, NTT * 4 * 128], BF16), ("outsb", [128, 2 * OB], F32),
                   ("eps_col", [128, 1], F32)],
            psums=[("psA", [128, 512], F32), ("psB", [128, 512], F32),
                   ("psZ", [128, 512], F32),
                   ("ptrA", [128, 1024], BF16), ("ptrB", [128, 1024], BF16)],
            semnames=["ld", "lnv", "lns", "lnt", "lne", "wld", "wdq", "mm", "ev", "st", "chv", "chs"],
            sem_es=(fz["es"] if fz else None), prefix=pre)
        (x_g, xlnT, stats, xln_tok, ident, onesr, ebrow, ebb, atgw_bf,
         bt_bf, wbf0, wbf1, z_sb, lg, oh, zm_bf, zmT, outsb,
         psA, psB, psZ, ptrA, ptrB) = (
            tens[k] for k in ("x_g", "xlnT", "stats", "xln_tok", "ident",
                              "onesr", "ebrow", "ebb", "atgw_bf", "bt_bf",
                              "wbf0", "wbf1", "z_sb", "lg", "oh", "zm_bf",
                              "zmT", "outsb", "psA", "psB", "psZ", "ptrA", "ptrB"))
        cx = Ctx(nc, block, sems)
        fz_barrier(cx, fz)
        ld, lnv, lns, lnt, lne, wld, wdq, mm, ev, st = (
            sems[k] for k in ("ld", "lnv", "lns", "lnt", "lne", "wld", "wdq",
                              "mm", "ev", "st"))

        sb = {"x_g": x_g, "xlnT": xlnT, "bs": tens["bs"], "stats": stats,
              "xln_tok": xln_tok, "ident": ident, "ptrA": ptrA, "ptrB": ptrB,
              "eps_col": tens["eps_col"]}
        def init_eps(v):
            v.memset(tens["eps_col"][:, :], EPS)
        cx.on("vector", init_eps)

        def prolog_dma(y):
            ld.inc(y.dma_start(out=ident[:, :], in_=ident_in[:, :]), 16)
            ld.inc(y.dma_start(out=onesr[:, :], in_=onesr_in[:, :]), 16)
            ld.inc(y.dma_start(out=ebrow[:, :], in_=ebias[:, :]), 16)
        cx.on("sync", prolog_dma)
        def prolog_dma2(y):
            src = atgw.rearrange("(ic p) c -> p ic c", p=128)
            wld.inc(y.dma_start(out=atgw_bf[:, :].rearrange("p (ic c) -> p ic c", c=36),
                                in_=src), 16)
            wld.inc(y.dma_start(out=bt_bf[:, :], in_=btc[:, :]), 16)
        cx.on("gpsimd", prolog_dma2)
        pro_ld = ld.n
        pro_wld = wld.n

        def prolog_mm(t, pro_ld=pro_ld):
            t.wait_ge(ld.h, pro_ld)
            i = t.matmul(psZ[:, 0:4], onesr[:, :], ebrow[:, :], start=True, stop=True)
            mm.inc(i, 1)
        cx.on("tensor", prolog_mm)
        pro_mm = mm.n

        def prolog_ev(v, pro_mm=pro_mm):
            v.wait_ge(mm.h, pro_mm)
            i = v.tensor_copy(ebb[:, :], psZ[:, 0:4])
            ev.inc(i, 1)
        cx.on("vector", prolog_ev)
        pro_ev = ev.n
        cx.use("psZ", ev, pro_ev)

        for g in range(NG):
            emit_ln_group(cx, g, xin, sb)
            xln_ready = lne.n

            for tt in range(NTT):
                def zmm(t, tt=tt, xln_ready=xln_ready, pro_wld=pro_wld):
                    t.wait_ge(lne.h, xln_ready)
                    t.wait_ge(wld.h, pro_wld)
                    cx.war(t, "psZ")
                    for ic in range(NIC):
                        i = t.matmul(psZ[:, 0:36],
                                     xlnT[:, ic * G512 + tt * 128: ic * G512 + (tt + 1) * 128],
                                     atgw_bf[:, ic * 36:(ic + 1) * 36],
                                     start=(ic == 0), stop=(ic == NIC - 1))
                    mm.inc(i, 1)
                cx.on("tensor", zmm)
                z_done = mm.n

                def zev(v, tt=tt, z_done=z_done):
                    chv = sems["chv"]
                    def ch(i):
                        chv.inc(i, 1); v.wait_ge(chv.h, chv.n)
                    v.wait_ge(mm.h, z_done)
                    v.tensor_copy(z_sb[:, :], psZ[:, 0:32])
                    ch(v.tensor_tensor(lg[:, 0:4], psZ[:, 32:36], ebb[:, :], op=ALU.add))
                    ch(v.tensor_reduce(lg[:, 4:5], lg[:, 0:4], axis=AX.X, op=ALU.max))
                    ch(v.tensor_scalar(oh[:, 0:4], lg[:, 0:4], lg[:, 4:5], None, op0=ALU.is_ge))
                    ch(v.tensor_scalar(oh[:, 4:5], oh[:, 0:1], -1.0, 1.0, op0=ALU.mult, op1=ALU.add))
                    for e in range(1, 4):
                        ch(v.tensor_tensor(oh[:, e:e + 1], oh[:, e:e + 1], oh[:, 4:5], op=ALU.mult))
                        if e < 3:
                            ch(v.tensor_scalar(oh[:, 5:6], oh[:, e:e + 1], -1.0, 1.0, op0=ALU.mult, op1=ALU.add))
                            ch(v.tensor_tensor(oh[:, 4:5], oh[:, 4:5], oh[:, 5:6], op=ALU.mult))
                    for e in range(4):
                        i = v.tensor_scalar(zm_bf[:, e * 8:(e + 1) * 8],
                                            z_sb[:, e * 8:(e + 1) * 8],
                                            oh[:, e:e + 1], None, op0=ALU.mult)
                    ev.inc(i, 1)
                cx.on("vector", zev)
                cx.use("psZ", ev, ev.n)
                zm_done = ev.n

                def ztr(t, tt=tt, zm_done=zm_done):
                    t.wait_ge(ev.h, zm_done)
                    cx.war(t, "ptrA")
                    for e in range(4):
                        i = t.transpose(ptrA[0:8, e * 128:(e + 1) * 128],
                                        zm_bf[:, e * 8:(e + 1) * 8], ident[:, :])
                    lnt.inc(i, 1)
                cx.on("tensor", ztr)
                ztr_done = lnt.n

                def ztev(v, tt=tt, ztr_done=ztr_done):
                    v.wait_ge(lnt.h, ztr_done)
                    i = v.tensor_copy(zmT[:, tt * 512:(tt + 1) * 512],
                                      ptrA[0:8, 0:512])
                    lne.inc(i, 1)
                cx.on("vector", ztev)
                cx.use("ptrA", lne, lne.n)
            zmT_ready = lne.n

            for ob in range(NOB):
                par = ob % 2
                wbf, wbname = ((wbf0, "wb0"), (wbf1, "wb1"))[par]

                def wload(y, ob=ob, wbf=wbf, wbname=wbname):
                    cx.war(y, wbname)
                    src = wupT.rearrange("(ic p) o -> p ic o", p=128)
                    for quarter in range(4):
                        dst = wbf[:, quarter * 8 * OB:(quarter + 1) * 8 * OB]
                        dst = dst.rearrange("p (ic o) -> p ic o", o=OB)
                        i = y.dma_start(
                            out=dst,
                            in_=src[:, quarter * 8:(quarter + 1) * 8,
                                    ob * OB:(ob + 1) * OB])
                        wld.inc(i, 16)
                cx.on("gpsimd", wload)
                wld_done = wld.n

                for tt in range(NTT):
                    ps, psname = ((psA, "psA"), (psB, "psB"))[tt % 2]
                    def mmf(t, ob=ob, tt=tt, ps=ps, psname=psname, wbf=wbf,
                            wld_done=wld_done, xln_ready=xln_ready,
                            zmT_ready=zmT_ready):
                        t.wait_ge(wld.h, wld_done)
                        t.wait_ge(lne.h, max(xln_ready, zmT_ready))
                        cx.war(t, psname)
                        for ic in range(NIC):
                            t.matmul(ps[:, 0:OB],
                                     xlnT[:, ic * G512 + tt * 128: ic * G512 + (tt + 1) * 128],
                                     wbf[:, ic * OB:(ic + 1) * OB],
                                     start=(ic == 0), stop=False)
                        for e in range(4):
                            i = t.matmul(ps[:, 0:OB],
                                         zmT[0:8, tt * 512 + e * 128: tt * 512 + (e + 1) * 128],
                                         bt_bf[0:8, e * D + ob * OB: e * D + (ob + 1) * OB],
                                         start=False, stop=(e == 3))
                        mm.inc(i, 1)
                    cx.on("tensor", mmf)
                    mm_done = mm.n
                    if tt == NTT - 1:
                        cx.use(wbname, mm, mm.n)

                    slot = (ob * NTT + tt) % 2
                    def evf(v, ob=ob, tt=tt, ps=ps, mm_done=mm_done, slot=slot):
                        v.wait_ge(mm.h, mm_done)
                        cx.war(v, f"outsb{slot}")
                        i = v.tensor_tensor(
                            outsb[:, slot * OB:(slot + 1) * OB], ps[:, 0:OB],
                            x_g[:, tt * D + ob * OB: tt * D + (ob + 1) * OB],
                            op=ALU.add)
                        ev.inc(i, 1)
                    cx.on("vector", evf)
                    cx.use(psname, ev, ev.n)
                    ev_done = ev.n

                    def stf(y, ob=ob, tt=tt, ev_done=ev_done, g=g, slot=slot):
                        y.wait_ge(ev.h, ev_done)
                        t0 = g * G512 + tt * 128
                        i = y.dma_start(out=xout[t0:t0 + 128, ob * OB:(ob + 1) * OB],
                                        in_=outsb[:, slot * OB:(slot + 1) * OB])
                        st.inc(i, 16)
                    cx.on("sync", stf)
                    cx.use(f"outsb{slot}", st, st.n)

            cx.use("x_g", ev, ev.n)
            cx.use("xlnT", mm, mm.n)
            cx.use("wb0", mm, mm.n)
            cx.use("wb1", mm, mm.n)

        final_st = st.n
        def fin(y, final_st=final_st):
            y.wait_ge(st.h, final_st)
        cx.on("sync", fin)
        if fz is not None:
            fz["barrier"] = (st.h, st.n)

    return nc


def prep_l1_ffn_inputs(inputs, consts, xcur, pre=""):
    wupT = np.ascontiguousarray(np.asarray(inputs["L1_ffn_W_up"], dtype=np.float32).T)
    A = np.asarray(inputs["L1_ffn_lora_A"], dtype=np.float32)
    gw = np.asarray(inputs["L1_ffn_gate_w"], dtype=np.float32)
    atgw = np.concatenate([A[e].T for e in range(4)] + [gw.T], axis=1)
    atgw = np.ascontiguousarray(atgw)
    B = np.asarray(inputs["L1_ffn_lora_B"], dtype=np.float32)
    btc = np.ascontiguousarray(
        np.concatenate([B[e].T for e in range(4)], axis=1))
    ebias = np.ascontiguousarray(
        np.asarray(inputs["L1_ffn_expert_bias"], dtype=np.float32).reshape(1, 4))
    maps = []
    for core in range(8):
        maps.append({
            "xin": np.ascontiguousarray(xcur[core]),
            "wupT": wupT, "atgw": atgw, "btc": btc, "ebias": ebias,
            "ident_bf": consts["ident_bf"],
            "ones_row_f": consts["ones_row_f"],
        })
    return maps



def prep_l0_ffn_inputs(inputs, consts, xcur, pre=""):
    wT = np.ascontiguousarray(inputs["L0_ffn_weight"].T)
    scflat = np.ascontiguousarray(
        np.asarray(inputs["L0_ffn_weight_scale"], dtype=np.float32).reshape(1, 1024))
    maps = []
    for core in range(8):
        maps.append({
            "xin": np.ascontiguousarray(xcur[core]),
            "wT": wT,
            "scflat": scflat,
            "ident_bf": consts["ident_bf"],
            "ones_row_f": consts["ones_row_f"],
        })
    return maps


def build_fused():
    nc = bass.Bass("TRN2", target_bir_lowering=False)
    x0 = nc.dram_tensor("x0", [S, D], F32, kind="ExternalInput").ap()
    xout = nc.dram_tensor("xout", [S, D], F32, kind="ExternalOutput").ap()
    xattT = nc.dram_tensor("t_xattT", [D, S], BF16).ap()
    x1 = nc.dram_tensor("t_x1", [S, D], F32).ap()
    x2 = nc.dram_tensor("t_x2", [S, D], F32).ap()
    x3 = nc.dram_tensor("t_x3", [S, D], F32).ap()
    es = contextlib.ExitStack()
    with es:
        fz = {"nc": nc, "es": es, "barrier": None,
              "prefix": "a_", "io": {"xin": x0, "xattT": xattT}}
        build_l0_attn(fz)
        fz["prefix"] = "b_"; fz["io"] = {"xres": x0, "xT": xattT, "xout": x1}
        build_proj_residual(fz)
        fz["prefix"] = "f_"; fz["io"] = {"xin": x1, "xout": x2}
        build_l0_ffn(fz)
        fz["prefix"] = "c_"; fz["io"] = {"xin": x2, "xout": x3}
        build_l1_attn(fz)
        fz["prefix"] = "d_"; fz["io"] = {"xin": x3, "xout": xout}
        build_l1_ffn(fz)
    return nc


def kernel(**inputs):
    consts = host_consts()
    x = np.ascontiguousarray(np.asarray(inputs["L0_attn_hidden_states"], dtype=np.float32))

    nc = build_fused()
    woT = np.ascontiguousarray(np.asarray(inputs["L0_attn_w_o"], dtype=np.float32).T)
    shared = {}
    for pre, prep in (("a_", prep_l0_attn_inputs), ("f_", prep_l0_ffn_inputs),
                      ("c_", prep_l1_attn_inputs), ("d_", prep_l1_ffn_inputs)):
        m = prep(inputs, consts, x)[0]
        shared.update({pre + k: v for k, v in m.items() if k != "xin"})
    shared["b_wT"] = woT
    maps = [dict(shared, x0=np.ascontiguousarray(x[core])) for core in range(8)]
    res = run_bass_kernel_spmd(nc, maps, core_ids=list(range(8)))
    return np.stack([res.results[i]["xout"] for i in range(8)], axis=0)


# version 05 (final)
# speedup vs baseline: 1.0011x; 1.0011x over previous
import numpy as np
import ml_dtypes
import concourse.bass as bass
import concourse.mybir as mybir
from concourse.bass_utils import run_bass_kernel_spmd

F32 = mybir.dt.float32
BF16 = mybir.dt.bfloat16
AF = mybir.ActivationFunctionType
ALU = mybir.AluOpType
AX = mybir.AxisListType
BF = ml_dtypes.bfloat16

S = 2048
D = 4096
G512 = 512
NG = S // G512
NTT = G512 // 128
NIC = D // 128
EPS = 1e-5
NEGBIG = -1e30


class Sem:
    def __init__(self, h):
        self.h = h
        self.n = 0
    def inc(self, instr, k):
        instr.then_inc(self.h, k)
        self.n += k
        return self.n

class Ctx:
    def __init__(self, nc, block, sems):
        self.nc = nc
        self.block = block
        self.sems = sems
        self.last_use = {}

    def on(self, eng, fn):
        getattr(self.block, eng)(fn)

    def use(self, res, sem, count):
        self.last_use[res] = [(sem, count)]

    def use_add(self, res, sem, count):
        self.last_use.setdefault(res, []).append((sem, count))

    def war(self, eng_handle, res):
        for sem, count in self.last_use.get(res, ()):
            eng_handle.wait_ge(sem.h, count)




import contextlib

def alloc(nc, es, sbufs, psums, semnames, sem_es=None, prefix=""):
    tens = {}
    for name, shape, dt in sbufs:
        tens[name] = es.enter_context(nc.sbuf_tensor(prefix + name, shape, dt))
    for name, shape, dt in psums:
        tens[name] = es.enter_context(nc.psum_tensor(prefix + name, shape, dt))
    ses = sem_es if sem_es is not None else es
    sems = {name: Sem(ses.enter_context(nc.semaphore(prefix + name)))
            for name in semnames}
    block = es.enter_context(nc.Block())
    return tens, sems, block


def fz_begin(fz):
    if fz is None:
        return bass.Bass("TRN2", target_bir_lowering=False), "", {}
    return fz["nc"], fz["prefix"], fz["io"]


def fz_barrier(cx, fz):
    if fz and fz.get("barrier"):
        bsem, bcnt = fz["barrier"]
        for eng in ("tensor", "vector", "scalar", "sync", "gpsimd"):
            def _bar(e, bsem=bsem, bcnt=bcnt):
                e.wait_ge(bsem, bcnt)
            cx.on(eng, _bar)



def host_consts():
    c = {}
    c["ident_bf"] = np.eye(128, dtype=BF)
    c["ones_col_f"] = np.ones((128, 1), dtype=np.float32)
    c["ones_row_f"] = np.ones((1, 128), dtype=np.float32)
    c["ones_col_bf"] = np.ones((128, 1), dtype=BF)
    c["ones_row_bf"] = np.ones((1, 128), dtype=BF)

    ROT = 64
    t = np.arange(S, dtype=np.float64)
    inv1 = 1.0 / (10000.0 ** (np.arange(0, ROT, 2, dtype=np.float64) / ROT))
    ang1 = t[:, None] * inv1[None, :]
    C1 = np.ones((128, S), dtype=np.float32)
    S1 = np.zeros((128, S), dtype=np.float32)
    cos1 = np.cos(ang1).astype(np.float32)
    sin1 = np.sin(ang1).astype(np.float32)
    for d in range(ROT):
        i = d // 2
        C1[d, :] = cos1[:, i]
        S1[d, :] = (-sin1[:, i] if d % 2 == 0 else sin1[:, i])
    c["rope1_C"] = C1
    c["rope1_S"] = S1
    P1 = np.zeros((128, 64), dtype=BF)
    for d in range(ROT):
        P1[d + 1 if d % 2 == 0 else d - 1, d] = 1
    c["rope1_P"] = P1

    inv0 = 1.0 / (10000.0 ** (np.arange(0, 128, 2, dtype=np.float64) / 128))
    ang0 = t[:, None] * inv0[None, :]
    C0 = np.zeros((128, S), dtype=np.float32)
    S0 = np.zeros((128, S), dtype=np.float32)
    cos0 = np.cos(ang0).astype(np.float32)
    sin0 = np.sin(ang0).astype(np.float32)
    for d in range(128):
        i = d // 2
        C0[d, :] = cos0[:, i]
        S0[d, :] = (-sin0[:, i] if d % 2 == 0 else sin0[:, i])
    c["rope0_C"] = C0
    c["rope0_S"] = S0
    P0 = np.zeros((128, 128), dtype=BF)
    for d in range(128):
        P0[d ^ 1, d] = 1
    c["rope0_P"] = P0

    dm = np.zeros((4, 128, 512), dtype=BF)
    for j, off in enumerate((0, 128, 256, 384)):
        k = np.arange(128)[:, None] + off
        q = np.arange(512)[None, :]
        dm[j] = (k <= q).astype(BF)
    c["l1_diag01"] = dm

    qa = np.arange(S)[:, None]
    cc = np.arange(128)[None, :]
    c["sel_vis_add"] = np.where(16 * (cc + 1) <= qa, 0.0, NEGBIG).astype(np.float32)
    c["own_oh"] = (cc == (qa // 16)).astype(np.float32)

    cv = np.zeros((NG, 128, G512), dtype=BF)
    for g in range(NG):
        qg = g * G512 + np.arange(G512)[None, :]
        cv[g] = (16 * (np.arange(128)[:, None] + 1) <= qg).astype(BF)
    c["cmp_vis01"] = cv

    fd = np.ones((4, 128, 512), dtype=BF)
    for j, off in enumerate((0, 128, 256, 384)):
        ka = np.arange(128)[:, None] + off
        qg = np.arange(512)[None, :]
        same = (ka // 16) == (qg // 16)
        future = same & (ka > qg)
        fd[j] = np.where(future, 0.0, 1.0).astype(BF)
    c["fine_diag01"] = fd

    sb = np.zeros((5, 128, 512), dtype=BF)
    for j, off in enumerate((-1, 0, 1, 2, 3)):
        ka = np.arange(128)[:, None] + 128 * off
        qg = np.arange(512)[None, :]
        sb[j] = ((qg - ka >= 0) & (qg - ka <= 64)).astype(BF)
    c["band01"] = sb

    BIGC = 1e30
    E = np.zeros((16, 128, 128), dtype=BF)
    for kb in range(16):
        for k in range(128):
            E[kb, 8 * kb + k // 16, k] = BIGC
    c["selE"] = E
    return c



def emit_ln_group(cx, g, xin_ap, sb, xg_tiled=True):
    sems = cx.sems
    ld, lnv, lns, lnt, lne = (sems[k] for k in ("ld", "lnv", "lns", "lnt", "lne"))
    chv, chs = sems["chv"], sems["chs"]
    x_g = sb["x_g"]; xlnT = sb["xlnT"]; bs = sb["bs"]
    st = sb["stats"]; xln_tok = sb["xln_tok"]
    ptrA = sb["ptrA"]; ptrB = sb["ptrB"]; ident = sb["ident"]

    for tt in range(NTT):
        t0 = g * G512 + tt * 128
        xo = tt * D if xg_tiled else 0

        def ldf(y, tt=tt, t0=t0, g=g, xo=xo):
            if (tt == 0) or (not xg_tiled):
                cx.war(y, "x_g")
            i = y.dma_start(out=x_g[:, xo:xo + D], in_=xin_ap[t0:t0 + 128, :])
            ld.inc(i, 16)
        cx.on("sync", ldf)
        ld_done = ld.n

        def vstats(v, tt=tt, ld_done=ld_done, xo=xo):
            v.wait_ge(ld.h, ld_done)
            cx.war(v, "stats")
            for i8 in range(D // 512):
                i = v.bn_stats(bs[:, i8 * 6:(i8 + 1) * 6],
                               x_g[:, xo + i8 * 512: xo + (i8 + 1) * 512])
            chv.inc(i, 1); v.wait_ge(chv.h, chv.n)
            i = v.bn_aggr(st[:, 0:2], bs[:, :])
            lnv.inc(i, 1)
        cx.on("vector", vstats)
        v_done = lnv.n

        def sstats2(s, v_done=v_done):
            s.wait_ge(lnv.h, v_done)
            i = s.activation(st[:, 6:7], st[:, 1:2], AF.Ln, bias=sb["eps_col"][:, 0:1])
            chs.inc(i, 1); s.wait_ge(chs.h, chs.n)
            i = s.activation(st[:, 7:8], st[:, 6:7], AF.Exp, scale=-0.5)
            lns.inc(i, 1)
        cx.on("scalar", sstats2)
        s2_done = lns.n
        cx.use("stats", lns, s2_done)

        def vnorm(v, tt=tt, s2_done=s2_done, xo=xo):
            v.wait_ge(lns.h, s2_done)
            cx.war(v, "xln_tok")
            i = v.tensor_tensor(st[:, 8:9], st[:, 0:1], st[:, 7:8], op=ALU.mult)
            chv.inc(i, 1); v.wait_ge(chv.h, chv.n)
            i = v.tensor_scalar(st[:, 9:10], st[:, 8:9], -1.0, None, op0=ALU.mult)
            chv.inc(i, 1); v.wait_ge(chv.h, chv.n)
            i = v.tensor_scalar(xln_tok[:, :], x_g[:, xo:xo + D],
                                st[:, 7:8], st[:, 9:10], op0=ALU.mult, op1=ALU.add)
            lnv.inc(i, 1)
        cx.on("vector", vnorm)
        n_done = lnv.n
        if not xg_tiled:
            cx.use("x_g", lnv, lnv.n)

        same_bank = ptrA is ptrB
        for half in range(4):
            bank, bankname = ((ptrA, "ptrA"), (ptrB, "ptrB"))[0 if same_bank else half % 2]
            def ttr(t, half=half, bank=bank, bankname=bankname, n_done=n_done):
                t.wait_ge(lnv.h, n_done)
                cx.war(t, bankname)
                for j in range(8):
                    ic = half * 8 + j
                    i = t.transpose(bank[:, j * 128:(j + 1) * 128],
                                    xln_tok[:, ic * 128:(ic + 1) * 128], ident[:, :])
                lnt.inc(i, 1)
            cx.on("tensor", ttr)
            t_done = lnt.n
            if half == 3:
                cx.use("xln_tok", lnt, t_done)

            def vev(v, tt=tt, half=half, bank=bank, t_done=t_done):
                v.wait_ge(lnt.h, t_done)
                cx.war(v, "xlnT")
                for j in range(8):
                    ic = half * 8 + j
                    i = v.tensor_copy(
                        xlnT[:, ic * G512 + tt * 128: ic * G512 + (tt + 1) * 128],
                        bank[:, j * 128:(j + 1) * 128])
                lne.inc(i, 1)
            cx.on("vector", vev)
            cx.use(bankname, lne, lne.n)
    return lne.n



def build_l0_ffn(fz=None):
    nc, pre, io = fz_begin(fz)
    xin = io.get("xin") if fz else nc.dram_tensor("xin", [S, D], F32, kind="ExternalInput").ap()
    wT = nc.dram_tensor(pre + "wT", [D, D], F32, kind="ExternalInput").ap()
    scflat = nc.dram_tensor(pre + "scflat", [1, 1024], F32, kind="ExternalInput").ap()
    ident_in = nc.dram_tensor(pre + "ident_bf", [128, 128], BF16, kind="ExternalInput").ap()
    onesr_in = nc.dram_tensor(pre + "ones_row_f", [1, 128], F32, kind="ExternalInput").ap()
    xout = io.get("xout") if fz else nc.dram_tensor("xout", [S, D], F32, kind="ExternalOutput").ap()

    OB = 128
    NOB = D // OB

    es = contextlib.ExitStack()
    with es:
        tens, sems, block = alloc(
            nc, es,
            sbufs=[("x_g", [128, NTT * D], F32), ("xlnT", [128, NIC * G512], BF16),
                   ("bs", [128, 48], F32), ("stats", [128, 16], F32),
                   ("xln_tok", [128, D], BF16), ("ident", [128, 128], BF16),
                   ("onesr", [1, 128], F32), ("scrow", [1, 1024], F32),
                   ("SC", [128, 1024], F32),
                   ("wstage0", [128, NIC * OB], F32), ("wstage1", [128, NIC * OB], F32),
                   ("wbf0", [128, NIC * OB], BF16), ("wbf1", [128, NIC * OB], BF16),
                   ("outsb", [128, 2 * OB], F32), ("eps_col", [128, 1], F32)],
            psums=[("psA", [128, 512], F32), ("psB", [128, 512], F32),
                   ("psC", [128, 512], F32),
                   ("ptrA", [128, 1024], BF16), ("ptrB", [128, 1024], BF16)],
            semnames=["ld", "lnv", "lns", "lnt", "lne", "wld", "wdq", "mm", "ev", "st", "chv", "chs"],
            sem_es=(fz["es"] if fz else None), prefix=pre)
        (x_g, xlnT, stats, xln_tok, ident, onesr, scrow, SC, wstage0,
         wstage1, wbf0, wbf1, outsb, psA, psB, psC, ptrA, ptrB) = (
            tens[k] for k in ("x_g", "xlnT", "stats", "xln_tok", "ident",
                              "onesr", "scrow", "SC", "wstage0", "wstage1",
                              "wbf0", "wbf1", "outsb", "psA", "psB", "psC",
                              "ptrA", "ptrB"))
        cx = Ctx(nc, block, sems)
        fz_barrier(cx, fz)
        ld, lnv, lns, lnt, lne, wld, wdq, mm, ev, st = (
            sems[k] for k in ("ld", "lnv", "lns", "lnt", "lne", "wld", "wdq",
                              "mm", "ev", "st"))

        sb = {"x_g": x_g, "xlnT": xlnT, "bs": tens["bs"], "stats": stats,
              "xln_tok": xln_tok, "ident": ident, "ptrA": ptrA, "ptrB": ptrB,
              "eps_col": tens["eps_col"]}
        def init_eps(v):
            v.memset(tens["eps_col"][:, :], EPS)
        cx.on("vector", init_eps)

        def prolog_dma(y):
            ld.inc(y.dma_start(out=ident[:, :], in_=ident_in[:, :]), 16)
            ld.inc(y.dma_start(out=onesr[:, :], in_=onesr_in[:, :]), 16)
            ld.inc(y.dma_start(out=scrow[:, :], in_=scflat[:, :]), 16)
        cx.on("sync", prolog_dma)
        pro_ld = ld.n

        def prolog_mm(t, pro_ld=pro_ld):
            t.wait_ge(ld.h, pro_ld)
            t.matmul(psC[:, 0:512], onesr[:, :], scrow[:, 0:512], start=True, stop=True)
            i = t.matmul(psB[:, 0:512], onesr[:, :], scrow[:, 512:1024], start=True, stop=True)
            mm.inc(i, 1)
        cx.on("tensor", prolog_mm)
        pro_mm = mm.n

        def prolog_ev(v, pro_mm=pro_mm):
            v.wait_ge(mm.h, pro_mm)
            v.tensor_copy(SC[:, 0:512], psC[:, 0:512])
            i = v.tensor_copy(SC[:, 512:1024], psB[:, 0:512])
            ev.inc(i, 1)
        cx.on("vector", prolog_ev)
        pro_ev = ev.n
        cx.use("psB", ev, pro_ev)

        for g in range(NG):
            emit_ln_group(cx, g, xin, sb)
            xln_ready = lne.n

            for ob in range(NOB):
                par = ob % 2
                wstage, wsname = ((wstage0, "ws0"), (wstage1, "ws1"))[par]
                wbf, wbname = ((wbf0, "wb0"), (wbf1, "wb1"))[par]

                def wload(y, ob=ob, wstage=wstage, wsname=wsname):
                    cx.war(y, wsname)
                    src = wT.rearrange("(ic p) o -> p ic o", p=128)
                    for quarter in range(4):
                        dst = wstage[:, quarter * 8 * OB:(quarter + 1) * 8 * OB]
                        dst = dst.rearrange("p (ic o) -> p ic o", o=OB)
                        i = y.dma_start(
                            out=dst,
                            in_=src[:, quarter * 8:(quarter + 1) * 8,
                                    ob * OB:(ob + 1) * OB])
                        wld.inc(i, 16)
                cx.on("sync", wload)
                wld_done = wld.n

                def wdeq(v, ob=ob, wstage=wstage, wbf=wbf, wbname=wbname,
                         wld_done=wld_done, pro_ev=pro_ev):
                    v.wait_ge(wld.h, wld_done)
                    v.wait_ge(ev.h, pro_ev)
                    cx.war(v, wbname)
                    oc = ob
                    for ic in range(NIC):
                        i = v.tensor_scalar(
                            wbf[:, ic * OB:(ic + 1) * OB],
                            wstage[:, ic * OB:(ic + 1) * OB],
                            SC[:, oc * 32 + ic: oc * 32 + ic + 1], None,
                            op0=ALU.mult)
                    wdq.inc(i, 1)
                cx.on("vector", wdeq)
                cx.use(wsname, wdq, wdq.n)
                wdq_done = wdq.n

                for tt in range(NTT):
                    ps, psname = ((psA, "psA"), (psB, "psB"))[tt % 2]
                    def mmf(t, ob=ob, tt=tt, ps=ps, psname=psname,
                            wbf=wbf, wdq_done=wdq_done, xln_ready=xln_ready):
                        t.wait_ge(wdq.h, wdq_done)
                        t.wait_ge(lne.h, xln_ready)
                        cx.war(t, psname)
                        for ic in range(NIC):
                            i = t.matmul(
                                ps[:, 0:OB],
                                xlnT[:, ic * G512 + tt * 128: ic * G512 + (tt + 1) * 128],
                                wbf[:, ic * OB:(ic + 1) * OB],
                                start=(ic == 0), stop=(ic == NIC - 1))
                        mm.inc(i, 1)
                    cx.on("tensor", mmf)
                    mm_done = mm.n
                    if tt == NTT - 1:
                        cx.use(wbname, mm, mm.n)

                    slot = (ob * NTT + tt) % 2
                    def evf(v, ob=ob, tt=tt, ps=ps, mm_done=mm_done, slot=slot):
                        v.wait_ge(mm.h, mm_done)
                        cx.war(v, f"outsb{slot}")
                        i = v.tensor_tensor(
                            outsb[:, slot * OB:(slot + 1) * OB], ps[:, 0:OB],
                            x_g[:, tt * D + ob * OB: tt * D + (ob + 1) * OB],
                            op=ALU.add)
                        ev.inc(i, 1)
                    cx.on("vector", evf)
                    cx.use(psname, ev, ev.n)
                    ev_done = ev.n

                    def stf(y, ob=ob, tt=tt, ev_done=ev_done, g=g, slot=slot):
                        y.wait_ge(ev.h, ev_done)
                        t0 = g * G512 + tt * 128
                        i = y.dma_start(out=xout[t0:t0 + 128, ob * OB:(ob + 1) * OB],
                                        in_=outsb[:, slot * OB:(slot + 1) * OB])
                        st.inc(i, 16)
                    cx.on("gpsimd", stf)
                    cx.use(f"outsb{slot}", st, st.n)

            cx.use("x_g", ev, ev.n)
            cx.use("xlnT", mm, mm.n)
            cx.use("wb0", mm, mm.n)
            cx.use("wb1", mm, mm.n)

        final_st = st.n
        def fin(y, final_st=final_st):
            y.wait_ge(st.h, final_st)
        cx.on("gpsimd", fin)
        if fz is not None:
            fz["barrier"] = (st.h, st.n)

    return nc



DBG = {"ng": NG, "nkvh": 8, "ngh": 4, "fine": True, "slide": True,
       "cmp": True, "sel": True, "gates": True, "mlp": True,
       "kvproj": 2, "gconst": True, "krope": True, "vtr": True, "cin": True,
       "dump": False}

def build_l0_attn(fz=None):
    nc, pre, io = fz_begin(fz)
    xin = io.get("xin") if fz else nc.dram_tensor("xin", [S, D], F32, kind="ExternalInput").ap()
    qkvwT = nc.dram_tensor(pre + "qkvwT", [D, 6144], F32, kind="ExternalInput").ap()
    wgT_d = nc.dram_tensor(pre + "wgT", [D, 96], F32, kind="ExternalInput").ap()
    bg_d = nc.dram_tensor(pre + "bgrow", [1, 96], F32, kind="ExternalInput").ap()
    kw1T_d = nc.dram_tensor(pre + "kw1T", [2048, 2048], F32, kind="ExternalInput").ap()
    vw1T_d = nc.dram_tensor(pre + "vw1T", [2048, 2048], F32, kind="ExternalInput").ap()
    kw2T_d = nc.dram_tensor(pre + "kw2T", [2048, 128], F32, kind="ExternalInput").ap()
    vw2T_d = nc.dram_tensor(pre + "vw2T", [2048, 128], F32, kind="ExternalInput").ap()
    kb1r_d = nc.dram_tensor(pre + "kb1r", [128, 16], F32, kind="ExternalInput").ap()
    vb1r_d = nc.dram_tensor(pre + "vb1r", [128, 16], F32, kind="ExternalInput").ap()
    kb2c_d = nc.dram_tensor(pre + "kb2c", [128, 1], F32, kind="ExternalInput").ap()
    vb2c_d = nc.dram_tensor(pre + "vb2c", [128, 1], F32, kind="ExternalInput").ap()
    kpT_d = nc.dram_tensor(pre + "kpT", [8, 128, 512], F32, kind="ExternalInput").ap()
    vpT_d = nc.dram_tensor(pre + "vpT", [8, 128, 512], F32, kind="ExternalInput").ap()
    memkT_d = nc.dram_tensor(pre + "memkT", [128, 8], BF16, kind="ExternalInput").ap()
    memv_d = nc.dram_tensor(pre + "memv", [1, 8 * 128], BF16, kind="ExternalInput").ap()
    ident_in = nc.dram_tensor(pre + "ident_bf", [128, 128], BF16, kind="ExternalInput").ap()
    onescb_in = nc.dram_tensor(pre + "ones_col_bf", [128, 1], BF16, kind="ExternalInput").ap()
    onesrb_in = nc.dram_tensor(pre + "ones_row_bf", [1, 128], BF16, kind="ExternalInput").ap()
    ropeC_in = nc.dram_tensor(pre + "ropeC0", [128, S], F32, kind="ExternalInput").ap()
    ropeS_in = nc.dram_tensor(pre + "ropeS0", [128, S], F32, kind="ExternalInput").ap()
    ropeP_in = nc.dram_tensor(pre + "ropeP0", [128, 128], BF16, kind="ExternalInput").ap()
    selvis_d = nc.dram_tensor(pre + "selvis", [S, 128], F32, kind="ExternalInput").ap()
    ownm1_d = nc.dram_tensor(pre + "ownm1", [S, 128], F32, kind="ExternalInput").ap()
    cmpvis_d = nc.dram_tensor(pre + "cmpvis", [NG, 128, 512], BF16, kind="ExternalInput").ap()
    fdiag_d = nc.dram_tensor(pre + "fdiag", [4, 128, 512], BF16, kind="ExternalInput").ap()
    band_d = nc.dram_tensor(pre + "band", [5, 128, 512], BF16, kind="ExternalInput").ap()
    selE_d = nc.dram_tensor(pre + "selE", [16, 128, 128], BF16, kind="ExternalInput").ap()
    xattT_out = io.get("xattT") if fz else nc.dram_tensor("xattT", [D, S], BF16, kind="ExternalOutput").ap()
    if DBG["dump"]:
        dmp = {nm: nc.dram_tensor(pre + "dmp_" + nm, shp, dt, kind="ExternalOutput").ap()
               for nm, shp, dt in (
                   ("rkT", [128, 8 * S], BF16), ("vtok", [128, 8 * S], BF16),
                   ("ckT", [128, 8 * 128], BF16), ("cvT", [128, 8 * 128], BF16),
                   ("selT", [128, 512], BF16), ("gsig", [128, 4 * 96], F32),
                   ("q4", [128, 4 * 512], BF16), ("Zsb", [128, 12], F32),
                   ("gz", [128, 12], F32), ("xlnT", [128, NIC * G512], BF16))}

    KVH = 8
    GH = 4
    SC0 = 1.0 / np.sqrt(128.0)
    SCSEL = SC0 / 4.0

    es = contextlib.ExitStack()
    with es:
        tens, sems, block = alloc(
            nc, es,
            sbufs=[
                ("x_g", [128, D], F32), ("xlnT", [128, NIC * G512], BF16),
                ("bs", [128, 48], F32), ("stats", [128, 16], F32),
                ("xln_tok", [128, D], BF16), ("ident", [128, 128], BF16),
                ("eps_col", [128, 1], F32), ("n1000", [128, 1], F32),
                ("onescb", [128, 1], BF16), ("onesrb", [1, 128], BF16),
                ("ropeCS", [128, 2 * G512], F32),
                ("ropeP", [128, 128], BF16),
                ("rkT", [128, KVH * S], BF16),
                ("vtok", [128, KVH * S], BF16),
                ("q4", [128, GH * G512], BF16),
                ("rqh", [128, G512], BF16),
                ("ktmp", [128, G512], BF16), ("vtmp", [128, G512], BF16),
                ("kvpT", [128, 2 * G512], BF16),
                ("ckinr", [128, 16 * 256], BF16),
                ("cvinr", [128, 16 * 256], BF16),
                ("h1a", [128, 16 * 256], BF16),
                ("ckT", [128, KVH * 128], BF16), ("cvT", [128, KVH * 128], BF16),
                ("cvbm", [128, KVH * 128], BF16),
                ("selT", [128, G512], BF16),
                ("gsig", [128, NTT * 96], F32),
                ("selvis_g", [128, NTT * 128], BF16),
                ("ownm1_g", [128, NTT * 128], BF16),
                ("cmpvis_g", [128, G512], BF16),
                ("fdiag_s", [128, 4 * 512], BF16),
                ("band_s", [128, 5 * 512], BF16),
                ("selE_s", [128, 16 * 128], BF16),
                ("memkT_s", [128, 8], BF16), ("memv_s", [1, 8 * 128], BF16),
                ("kb1r_s", [128, 16], F32), ("vb1r_s", [128, 16], F32),
                ("kb2c_s", [128, 1], F32), ("vb2c_s", [128, 1], F32),
                ("qw0", [128, NIC * 128], BF16),
                ("mw0", [128, 16 * 128], BF16),
                ("bg_bf", [1, 96], BF16),
                ("exp0", [128, 512], BF16), ("exp1", [128, 512], BF16),
                ("exp2", [128, 512], BF16),
                ("rt1", [128, 512], F32), ("rt2", [128, 512], F32),
                ("psel", [128, 128], F32), ("pscr", [128, 128], F32),
                ("m8", [128, 8], F32), ("selm", [128, 128], F32),
                ("Zsb", [128, 12], F32), ("Zinv", [128, 12], F32),
                ("gz", [128, 12], F32),
                ("xatt_tok", [128, 128], BF16),
                ("xattT_kvh", [128, G512], BF16),
            ],
            psums=[("psA", [128, 512], F32), ("psB", [128, 512], F32),
                   ("psC", [128, 512], F32), ("psF", [128, 512], F32),
                   ("psS", [128, 512], F32), ("psU", [128, 512], F32),
                   ("psZ", [128, 512], F32),
                   ("ptrA", [128, 1024], BF16)],
            semnames=["ld", "pld", "lnv", "lns", "lnt", "lne", "wld", "mm",
                      "ev", "st", "chv", "chs", "rp"],
            sem_es=(fz["es"] if fz else None), prefix=pre)
        cx = Ctx(nc, block, sems)
        fz_barrier(cx, fz)
        ld, pld, lnv, lns, lnt, lne, wld, mm, ev, st, chv, chs, rp = (
            sems[k] for k in ("ld", "pld", "lnv", "lns", "lnt", "lne", "wld", "mm",
                              "ev", "st", "chv", "chs", "rp"))
        T = tens
        x_g = T["x_g"]; xlnT = T["xlnT"]; ident = T["ident"]
        rkT = T["rkT"]; vtok = T["vtok"]; q4 = T["q4"]; rqh = T["rqh"]
        psA, psB, psC, psF, psS, psU, psZ = (
            T[k] for k in ("psA", "psB", "psC", "psF", "psS", "psU", "psZ"))
        ptrA = T["ptrA"]
        expsl = [T["exp0"], T["exp1"], T["exp2"]]

        sb = {"x_g": x_g, "xlnT": xlnT, "bs": T["bs"], "stats": T["stats"],
              "xln_tok": T["xln_tok"], "ident": ident, "ptrA": ptrA,
              "ptrB": ptrA, "eps_col": T["eps_col"]}

        def init_consts(v):
            v.memset(T["eps_col"][:, :], EPS)
            v.memset(T["n1000"][:, :], -1000.0)
            v.memset(T["ckT"][:, :], 0.0)
            v.memset(T["cvT"][:, :], 0.0)
            v.memset(T["cvbm"][:, :], 0.0)
        cx.on("vector", init_consts)

        def prolog_dma(y):
            for dst, src in ((ident, ident_in), (T["onescb"], onescb_in),
                             (T["onesrb"], onesrb_in),
                             (T["ropeP"], ropeP_in), (T["memkT_s"], memkT_d),
                             (T["memv_s"], memv_d), (T["kb1r_s"], kb1r_d),
                             (T["vb1r_s"], vb1r_d), (T["kb2c_s"], kb2c_d),
                             (T["vb2c_s"], vb2c_d)):
                pld.inc(y.dma_start(out=dst[:, :], in_=src[:, :]), 16)
            for j in range(4):
                pld.inc(y.dma_start(out=T["fdiag_s"][:, j * 512:(j + 1) * 512],
                                   in_=fdiag_d[j, :, :]), 16)
            for j in range(5):
                pld.inc(y.dma_start(out=T["band_s"][:, j * 512:(j + 1) * 512],
                                   in_=band_d[j, :, :]), 16)
            for j in range(16):
                pld.inc(y.dma_start(out=T["selE_s"][:, j * 128:(j + 1) * 128],
                                   in_=selE_d[j, :, :]), 16)
        cx.on("sync", prolog_dma)
        def prolog_dmag(y):
            wld.inc(y.dma_start(out=T["bg_bf"][:, :], in_=bg_d[:, :]), 16)
        cx.on("gpsimd", prolog_dmag)
        pro_ld = pld.n

        for eng in ("tensor", "vector", "scalar"):
            def gate_(e, pro_ld=pro_ld):
                e.wait_ge(pld.h, pro_ld)
            cx.on(eng, gate_)

        def stream_block(buf, bufname, src_ap, col0, ncols, nic, blk=16):
            def wl(y, buf=buf, bufname=bufname, src_ap=src_ap, col0=col0,
                   ncols=ncols, nic=nic, blk=blk):
                cx.war(y, bufname)
                src = src_ap.rearrange("(ic p) o -> p ic o", p=128)
                nq = max(1, nic // blk)
                per = nic // nq
                for qq in range(nq):
                    dst = buf[:, qq * per * ncols:(qq + 1) * per * ncols]
                    dst = dst.rearrange("p (ic o) -> p ic o", o=ncols)
                    i = y.dma_start(out=dst,
                                    in_=src[:, qq * per:(qq + 1) * per,
                                            col0:col0 + ncols])
                    wld.inc(i, 16)
            cx.on("gpsimd", wl)
            return wld.n

        for g in range(DBG["ng"]):
            emit_ln_group(cx, g, xin, sb, xg_tiled=False)
            xln_ready = lne.n

            def gconsts(y, g=g):
                cx.war(y, "ropeCS")
                pld.inc(y.dma_start(out=T["ropeCS"][:, 0:512],
                                   in_=ropeC_in[:, g * G512:(g + 1) * G512]), 16)
                pld.inc(y.dma_start(out=T["ropeCS"][:, 512:1024],
                                   in_=ropeS_in[:, g * G512:(g + 1) * G512]), 16)
                cx.war(y, "cmpvis_g")
                pld.inc(y.dma_start(out=T["cmpvis_g"][:, :], in_=cmpvis_d[g, :, :]), 16)

            if DBG["gconst"]:
                cx.on("sync", gconsts)
            def gconsts2(y, g=g):
                cx.war(y, "selvis_g")
                cx.war(y, "ownm1_g")
                for qc in range(NTT):
                    q0 = g * G512 + qc * 128
                    pld.inc(y.dma_start(out=T["selvis_g"][:, qc * 128:(qc + 1) * 128],
                                       in_=selvis_d[q0:q0 + 128, :]), 16)
                    pld.inc(y.dma_start(out=T["ownm1_g"][:, qc * 128:(qc + 1) * 128],
                                       in_=ownm1_d[q0:q0 + 128, :]), 16)
            if DBG["gconst"]:
                cx.on("gpsimd", gconsts2)
            gconst_ld = pld.n
            gconst_wld = wld.n

            wg_done = stream_block(T["qw0"], "qw0", wgT_d, 0, 96, NIC) if DBG["gates"] else 0
            for qc in range(NTT if DBG["gates"] else 0):
                def gmm(t, qc=qc, xln_ready=xln_ready, wg_done=wg_done):
                    t.wait_ge(lne.h, xln_ready)
                    t.wait_ge(wld.h, wg_done)
                    cx.war(t, "psU")
                    for ic in range(NIC):
                        t.matmul(psU[:, 0:96],
                                 xlnT[:, ic * G512 + qc * 128: ic * G512 + (qc + 1) * 128],
                                 T["qw0"][:, ic * 96:(ic + 1) * 96],
                                 start=(ic == 0), stop=False)
                    i = t.matmul(psU[:, 0:96], T["onesrb"][:, :],
                                 T["bg_bf"][:, :], start=False, stop=True)
                    mm.inc(i, 1)
                cx.on("tensor", gmm)
                gmm_done = mm.n
                cx.use_add("qw0", mm, mm.n)

                def gsigf(s_, qc=qc, gmm_done=gmm_done):
                    s_.wait_ge(mm.h, gmm_done)
                    cx.war(s_, "rt12")
                    i = s_.activation(T["rt1"][:, 0:96], psU[:, 0:96], AF.Exp,
                                      scale=-1.0)
                    lns.inc(i, 1)
                cx.on("scalar", gsigf)
                cx.use("psU", lns, lns.n)
                gs_done = lns.n

                def gsig2(v, qc=qc, gs_done=gs_done):
                    v.wait_ge(lns.h, gs_done)
                    if qc == 0:
                        cx.war(v, "gsig")
                    i = v.tensor_scalar(T["rt1"][:, 0:96], T["rt1"][:, 0:96], 1.0,
                                        None, op0=ALU.add)
                    chv.inc(i, 1); v.wait_ge(chv.h, chv.n)
                    i = v.reciprocal(T["gsig"][:, qc * 96:(qc + 1) * 96],
                                     T["rt1"][:, 0:96])
                    rp.inc(i, 1)
                    cx.use("rt12", rp, rp.n)
                cx.on("vector", gsig2)

            for kv in range(DBG["kvproj"]):
                for kvh in range(KVH):
                    ob = 32 + kv * 8 + kvh
                    qw, qwname = T["qw0"], "qw0"
                    wld_done = stream_block(qw, qwname, qkvwT, ob * 128, 128, NIC)

                    ps, psname = ((psA, "psA"), (psB, "psB"))[ob % 2]
                    def pmm(t, qw=qw, ps=ps, psname=psname, wld_done=wld_done,
                            xln_ready=xln_ready):
                        t.wait_ge(wld.h, wld_done)
                        t.wait_ge(lne.h, xln_ready)
                        cx.war(t, psname)
                        for ic in range(NIC):
                            i = t.matmul(ps[:, 0:512],
                                         qw[:, ic * 128:(ic + 1) * 128],
                                         xlnT[:, ic * G512:(ic + 1) * G512],
                                         start=(ic == 0), stop=(ic == NIC - 1))
                        mm.inc(i, 1)
                    cx.on("tensor", pmm)
                    cx.use(qwname, mm, mm.n)
                    mm_done = mm.n

                    tmp, tmpname = (T["ktmp"], "ktmp") if kv == 0 else (T["vtmp"], "vtmp")
                    def pev(v, ps=ps, mm_done=mm_done, tmp=tmp, tmpname=tmpname):
                        v.wait_ge(mm.h, mm_done)
                        cx.war(v, tmpname)
                        i = v.tensor_copy(tmp[:, :], ps[:, 0:512])
                        ev.inc(i, 1)
                    cx.on("vector", pev)
                    ev_done = ev.n

                    def kpl(y, kvh=kvh, kv=kv):
                        cx.war(y, "kvpT")
                        src = (kpT_d if kv == 0 else vpT_d)
                        i = y.dma_start(out=T["kvpT"][:, kv * 512:(kv + 1) * 512],
                                        in_=src[kvh, :, :])
                        wld.inc(i, 16)
                    cx.on("gpsimd", kpl) if DBG["cin"] else None
                    kp_done = wld.n

                    cin, cinname = (T["ckinr"], "ckinr") if kv == 0 else (T["cvinr"], "cvinr")
                    def cinf(v, ps=ps, mm_done=mm_done, kp_done=kp_done,
                             cin=cin, cinname=cinname, kvh=kvh, kv=kv):
                        v.wait_ge(mm.h, mm_done)
                        v.wait_ge(wld.h, kp_done)
                        cx.war(v, "rt12")
                        i = v.tensor_tensor(T["rt1"][:, :], ps[:, 0:512],
                                            T["kvpT"][:, kv * 512:(kv + 1) * 512],
                                            op=ALU.add)
                        chv.inc(i, 1); v.wait_ge(chv.h, chv.n)
                        cx.war(v, cinname)
                        dst = cin[:, :].rearrange("p (tb kv c) -> p tb kv c",
                                                  kv=KVH, c=32)[:, :, kvh, :]
                        src = T["rt1"][:, :].rearrange("p (c tb) -> p tb c", tb=16)
                        i = v.tensor_copy(dst, src)
                        ev.inc(i, 1)
                    cx.on("vector", cinf) if DBG["cin"] else None
                    cx.use("rt12", ev, ev.n)
                    cx.use_add(psname, ev, ev.n)
                    cx.use("kvpT", ev, ev.n)

                    if kv == 0:
                        def kr0(t, ev_done=ev_done, tmp=tmp):
                            t.wait_ge(ev.h, ev_done)
                            cx.war(t, "psU")
                            i = t.matmul(psU[:, 0:512], T["ropeP"][:, :],
                                         tmp[:, :], start=True, stop=True)
                            mm.inc(i, 1)
                        cx.on("tensor", kr0) if DBG["krope"] else None
                        r0_done = mm.n

                        def kr1(v, ev_done=ev_done, tmp=tmp, kvh=kvh, g=g,
                                r0_done=r0_done, gconst_ld=gconst_ld):
                            v.wait_ge(ev.h, ev_done)
                            v.wait_ge(pld.h, gconst_ld)
                            cx.war(v, "rt12")
                            i = v.tensor_tensor(T["rt1"][:, :], tmp[:, :],
                                                T["ropeCS"][:, 0:512], op=ALU.mult)
                            chv.inc(i, 1); v.wait_ge(chv.h, chv.n)
                            v.wait_ge(mm.h, r0_done)
                            i = v.tensor_tensor(T["rt2"][:, :], psU[:, 0:512],
                                                T["ropeCS"][:, 512:1024], op=ALU.mult)
                            chv.inc(i, 1); v.wait_ge(chv.h, chv.n)
                            cx.use_add("psU", chv, chv.n)
                            toff = kvh * S + g * G512
                            i = v.tensor_tensor(rkT[:, toff:toff + G512],
                                                T["rt1"][:, :], T["rt2"][:, :],
                                                op=ALU.add)
                            rp.inc(i, 1)
                            cx.use("rt12", rp, rp.n)
                        cx.on("vector", kr1) if DBG["krope"] else None
                        cx.use("ktmp", rp, rp.n)
                        cx.use_add("ktmp", mm, mm.n)
                    else:
                        for tt in range(NTT):
                            def vt(t, ev_done=ev_done, tt=tt, tmp=tmp):
                                t.wait_ge(ev.h, ev_done)
                                cx.war(t, "ptrA")
                                i = t.transpose(ptrA[:, 0:128],
                                                tmp[:, tt * 128:(tt + 1) * 128],
                                                ident[:, :])
                                lnt.inc(i, 1)
                            cx.on("tensor", vt) if DBG["vtr"] else None
                            t_done = lnt.n
                            def vtev(v, t_done=t_done, tt=tt, kvh=kvh, g=g):
                                v.wait_ge(lnt.h, t_done)
                                kb = g * NTT + tt
                                i = v.tensor_copy(
                                    vtok[:, kvh * S + kb * 128: kvh * S + (kb + 1) * 128],
                                    ptrA[:, 0:128])
                                rp.inc(i, 1)
                            cx.on("vector", vtev) if DBG["vtr"] else None
                            cx.use("ptrA", rp, rp.n)
                        if tt == NTT - 1:
                            cx.use("vtmp", lnt, lnt.n)
            kv_ready = rp.n
            cin_ready = ev.n

            for kv in range(2 if DBG["mlp"] else 0):
                cin = T["ckinr"] if kv == 0 else T["cvinr"]
                w1d = kw1T_d if kv == 0 else vw1T_d
                w2d = kw2T_d if kv == 0 else vw2T_d
                b1 = T["kb1r_s"] if kv == 0 else T["vb1r_s"]
                b2 = T["kb2c_s"] if kv == 0 else T["vb2c_s"]
                cT, cTname = (T["ckT"], "ckT") if kv == 0 else (T["cvT"], "cvT")

                for oc in range(16):
                    mw, mwname = T["mw0"], "mw0"
                    wld_done = stream_block(mw, mwname, w1d, oc * 128, 128, 16)
                    ps, psname = ((psA, "psA"), (psB, "psB"))[oc % 2]

                    def h1mm(t, mw=mw, ps=ps, psname=psname, cin=cin,
                             wld_done=wld_done, cin_ready=cin_ready):
                        t.wait_ge(wld.h, wld_done)
                        t.wait_ge(ev.h, cin_ready)
                        cx.war(t, psname)
                        for tb in range(16):
                            i = t.matmul(ps[:, 0:256],
                                         mw[:, tb * 128:(tb + 1) * 128],
                                         cin[:, tb * 256:(tb + 1) * 256],
                                         start=(tb == 0), stop=(tb == 15))
                        mm.inc(i, 1)
                    cx.on("tensor", h1mm)
                    cx.use(mwname, mm, mm.n)
                    h1_done = mm.n

                    def h1ev(v, oc=oc, ps=ps, h1_done=h1_done, b1=b1):
                        v.wait_ge(mm.h, h1_done)
                        cx.war(v, "h1a") if oc == 0 else None
                        i = v.tensor_scalar(T["h1a"][:, oc * 256:(oc + 1) * 256],
                                            ps[:, 0:256], b1[:, oc:oc + 1], 0.0,
                                            op0=ALU.add, op1=ALU.max)
                        ev.inc(i, 1)
                    cx.on("vector", h1ev)
                    cx.use(psname, ev, ev.n)
                cx.use("ckinr" if kv == 0 else "cvinr", mm, mm.n)
                h1_ready = ev.n

                wld2 = stream_block(T["mw0"], "mw0", w2d, 0, 128, 16)
                def h2mm(t, wld2=wld2, h1_ready=h1_ready):
                    t.wait_ge(wld.h, wld2)
                    t.wait_ge(ev.h, h1_ready)
                    cx.war(t, "psA")
                    for oc in range(16):
                        i = t.matmul(psA[:, 0:256],
                                     T["mw0"][:, oc * 128:(oc + 1) * 128],
                                     T["h1a"][:, oc * 256:(oc + 1) * 256],
                                     start=(oc == 0), stop=(oc == 15))
                    mm.inc(i, 1)
                cx.on("tensor", h2mm)
                cx.use("mw0", mm, mm.n)
                cx.use("h1a", mm, mm.n)
                h2_done = mm.n

                def h2ev(v, h2_done=h2_done, b2=b2, cT=cT, cTname=cTname, g=g):
                    v.wait_ge(mm.h, h2_done)
                    cx.war(v, cTname)
                    dst = cT[:, :].rearrange("p (kv c) -> p kv c", c=128)[
                        :, :, g * 32:(g + 1) * 32]
                    src = psA[:, 0:256].rearrange("p (kv c) -> p kv c", c=32)
                    i = v.tensor_scalar(dst, src, b2[:, 0:1], None, op0=ALU.add)
                    ev.inc(i, 1)
                cx.on("vector", h2ev)
                cx.use("psA", ev, ev.n)
            mlp_ready = ev.n

            for kvh in range(KVH if DBG["mlp"] else 0):
                def cvt(t, kvh=kvh, mlp_ready=mlp_ready):
                    t.wait_ge(ev.h, mlp_ready)
                    cx.war(t, "ptrA")
                    i = t.transpose(ptrA[:, 0:128],
                                    T["cvT"][:, kvh * 128:(kvh + 1) * 128],
                                    ident[:, :])
                    lnt.inc(i, 1)
                cx.on("tensor", cvt)
                t_done = lnt.n
                def cvtev(v, t_done=t_done, kvh=kvh):
                    v.wait_ge(lnt.h, t_done)
                    cx.war(v, "cvbm")
                    i = v.tensor_copy(T["cvbm"][:, kvh * 128:(kvh + 1) * 128],
                                      ptrA[:, 0:128])
                    rp.inc(i, 1)
                cx.on("vector", cvtev)
                cx.use("ptrA", rp, rp.n)
            cvbm_ready = rp.n

            for kvh in range(DBG["nkvh"]):
                for gh in range(GH):
                    h = kvh * GH + gh
                    qw, qwname = T["qw0"], "qw0"
                    wld_done = stream_block(qw, qwname, qkvwT, h * 128, 128, NIC)
                    ps, psname = ((psA, "psA"), (psB, "psB"))[gh % 2]
                    def qmm(t, qw=qw, ps=ps, psname=psname, wld_done=wld_done,
                            xln_ready=xln_ready):
                        t.wait_ge(wld.h, wld_done)
                        t.wait_ge(lne.h, xln_ready)
                        cx.war(t, psname)
                        for ic in range(NIC):
                            i = t.matmul(ps[:, 0:512],
                                         qw[:, ic * 128:(ic + 1) * 128],
                                         xlnT[:, ic * G512:(ic + 1) * G512],
                                         start=(ic == 0), stop=(ic == NIC - 1))
                        mm.inc(i, 1)
                    cx.on("tensor", qmm)
                    cx.use(qwname, mm, mm.n)
                    mm_done = mm.n
                    def qev(v, gh=gh, ps=ps, mm_done=mm_done):
                        v.wait_ge(mm.h, mm_done)
                        if gh == 0:
                            cx.war(v, "q4")
                        i = v.tensor_copy(T["q4"][:, gh * G512:(gh + 1) * G512],
                                          ps[:, 0:512])
                        ev.inc(i, 1)
                    cx.on("vector", qev)
                    cx.use(psname, ev, ev.n)
                q4_ready = ev.n

                for qc in range(NTT if DBG["sel"] else 0):
                    def selmm(t, kvh=kvh, qc=qc, q4_ready=q4_ready,
                              mlp_ready=mlp_ready, gconst_ld=gconst_ld):
                        t.wait_ge(ev.h, max(q4_ready, mlp_ready))
                        t.wait_ge(pld.h, gconst_ld)
                        cx.war(t, "psU")
                        for gh in range(GH):
                            i = t.matmul(
                                psU[:, 0:128],
                                T["q4"][:, gh * G512 + qc * 128: gh * G512 + (qc + 1) * 128],
                                T["ckT"][:, kvh * 128:(kvh + 1) * 128],
                                start=(gh == 0), stop=(gh == GH - 1))
                        mm.inc(i, 1)
                    cx.on("tensor", selmm)
                    selmm_done = mm.n

                    def selsoft(v, s_, qc=qc, selmm_done=selmm_done):
                        pass
                    def selv1(v, qc=qc, selmm_done=selmm_done,
                              gconst_wld=gconst_wld):
                        v.wait_ge(mm.h, selmm_done)
                        v.wait_ge(wld.h, gconst_wld)
                        cx.war(v, "pselw")
                        i = v.tensor_tensor(psU[:, 0:128], psU[:, 0:128],
                                            T["selvis_g"][:, qc * 128:(qc + 1) * 128],
                                            op=ALU.add)
                        chv.inc(i, 1); v.wait_ge(chv.h, chv.n)
                        i = v.tensor_reduce(T["stats"][:, 10:11], psU[:, 0:128],
                                            axis=AX.X, op=ALU.max)
                        chv.inc(i, 1); v.wait_ge(chv.h, chv.n)
                        i = v.tensor_scalar(T["stats"][:, 11:12], T["stats"][:, 10:11],
                                            SCSEL, -1000.0, op0=ALU.mult, op1=ALU.max)
                        chv.inc(i, 1); v.wait_ge(chv.h, chv.n)
                        i = v.tensor_scalar(T["stats"][:, 12:13], T["stats"][:, 11:12],
                                            -1.0, None, op0=ALU.mult)
                        ev.inc(i, 1)
                    cx.on("vector", selv1)
                    neg_done = ev.n

                    def sele(s_, qc=qc, neg_done=neg_done):
                        s_.wait_ge(ev.h, neg_done)
                        cx.war(s_, "pselw2")
                        i = s_.activation(T["psel"][:, :], psU[:, 0:128], AF.Exp,
                                          scale=SCSEL, bias=T["stats"][:, 12:13],
                                          accum_out=T["stats"][:, 13:14])
                        chs.inc(i, 1); s_.wait_ge(chs.h, chs.n)
                        i = s_.activation(T["stats"][:, 14:15], T["stats"][:, 12:13],
                                          AF.Exp, bias=T["n1000"][:, 0:1])
                        lns.inc(i, 1)
                    cx.on("scalar", sele)
                    cx.use("psU", lns, lns.n)
                    sele_done = lns.n

                    def selv2(v, qc=qc, sele_done=sele_done):
                        v.wait_ge(lns.h, sele_done)
                        i = v.tensor_tensor(T["stats"][:, 15:16], T["stats"][:, 13:14],
                                            T["stats"][:, 14:15], op=ALU.add)
                        chv.inc(i, 1); v.wait_ge(chv.h, chv.n)
                        i = v.tensor_scalar(T["stats"][:, 15:16], T["stats"][:, 15:16],
                                            1e-10, None, op0=ALU.mult)
                        chv.inc(i, 1); v.wait_ge(chv.h, chv.n)
                        i = v.max(out=T["m8"][:, :], in_=T["psel"][:, :])
                        chv.inc(i, 1); v.wait_ge(chv.h, chv.n)
                        i = v.memset(T["m8"][:, 4:8], -1.0)
                        chv.inc(i, 1); v.wait_ge(chv.h, chv.n)
                        i = v.match_replace(out=T["pscr"][:, :], in_to_replace=T["m8"][:, :],
                                            in_values=T["psel"][:, :], imm_value=-1.0)
                        chv.inc(i, 1); v.wait_ge(chv.h, chv.n)
                        i = v.tensor_tensor(T["selm"][:, :], T["psel"][:, :],
                                            T["pscr"][:, :], op=ALU.not_equal)
                        chv.inc(i, 1); v.wait_ge(chv.h, chv.n)
                        i = v.tensor_scalar(T["pscr"][:, :], T["psel"][:, :],
                                            T["stats"][:, 15:16], None, op0=ALU.is_gt)
                        chv.inc(i, 1); v.wait_ge(chv.h, chv.n)
                        i = v.tensor_tensor(T["selm"][:, :], T["selm"][:, :],
                                            T["pscr"][:, :], op=ALU.mult)
                        chv.inc(i, 1); v.wait_ge(chv.h, chv.n)
                        i = v.tensor_tensor(T["pscr"][:, :], T["selm"][:, :],
                                            T["ownm1_g"][:, qc * 128:(qc + 1) * 128],
                                            op=ALU.add)
                        ev.inc(i, 1)
                        chv.inc(i, 0) if False else None
                    cx.on("vector", selv2)
                    selm_done = ev.n

                    def selcast(v, qc=qc, selm_done=selm_done):
                        v.wait_ge(ev.h, selm_done)
                        cx.war(v, "pselbf")
                        i = v.tensor_copy(T["xatt_tok"][:, :], T["pscr"][:, :])
                        ev.inc(i, 1)
                    cx.on("vector", selcast)
                    selc_done = ev.n

                    def selt(t, qc=qc, selc_done=selc_done):
                        t.wait_ge(ev.h, selc_done)
                        cx.war(t, "ptrA")
                        i = t.transpose(ptrA[:, 0:128], T["xatt_tok"][:, :],
                                        ident[:, :])
                        lnt.inc(i, 1)
                    cx.on("tensor", selt)
                    cx.use("pselbf", lnt, lnt.n)
                    t_done = lnt.n

                    def seltev(v, qc=qc, t_done=t_done):
                        v.wait_ge(lnt.h, t_done)
                        cx.war(v, "selTw")
                        i = v.tensor_copy(T["selT"][:, qc * 128:(qc + 1) * 128],
                                          ptrA[:, 0:128])
                        rp.inc(i, 1)
                    cx.on("vector", seltev)
                    cx.use("ptrA", rp, rp.n)
                selT_ready = rp.n

                for gh in range(DBG["ngh"]):
                    h = kvh * GH + gh
                    qsl = slice(gh * G512, (gh + 1) * G512)

                    if True:
                      pass
                    def cmm(t, kvh=kvh, qsl=qsl, q4_ready=q4_ready,
                            mlp_ready=mlp_ready):
                        t.wait_ge(ev.h, max(q4_ready, mlp_ready))
                        cx.war(t, "psA")
                        i = t.matmul(psA[:, 0:512],
                                     T["ckT"][:, kvh * 128:(kvh + 1) * 128],
                                     T["q4"][:, qsl], start=True, stop=True)
                        mm.inc(i, 1)
                        cx.war(t, "psU")
                        i = t.matmul(psU[0:1, 0:512],
                                     T["memkT_s"][:, kvh:kvh + 1],
                                     T["q4"][:, qsl], start=True, stop=True)
                        mm.inc(i, 1)
                    cx.on("tensor", cmm) if DBG["cmp"] else None
                    cmm_done = mm.n

                    def cexp(s_, cmm_done=cmm_done):
                        s_.wait_ge(mm.h, cmm_done)
                        cx.war(s_, "exp0")
                        i = s_.activation(expsl[0][:, :], psA[:, 0:512], AF.Exp,
                                          scale=SC0)
                        chs.inc(i, 1)
                        cx.war(s_, "exp1")
                        i = s_.activation(T["exp1"][0:1, 0:512], psU[0:1, 0:512], AF.Exp,
                                          scale=SC0)
                        lns.inc(i, 1)
                    cx.on("scalar", cexp) if DBG["cmp"] else None
                    cx.use("psA", lns, lns.n)
                    cx.use_add("psU", lns, lns.n)
                    cexp_done = lns.n

                    def cmask(v, cexp_done=cexp_done, gconst_ld=gconst_ld):
                        v.wait_ge(lns.h, cexp_done)
                        v.wait_ge(pld.h, gconst_ld)
                        i = v.tensor_tensor(expsl[0][:, :], expsl[0][:, :],
                                            T["cmpvis_g"][:, :], op=ALU.mult)
                        chv.inc(i, 1); v.wait_ge(chv.h, chv.n)
                    cx.on("vector", cmask) if DBG["cmp"] else None
                    cmask_done = chv.n

                    def cpv(t, kvh=kvh, cmask_done=cmask_done, cvbm_ready=cvbm_ready):
                        t.wait_ge(chv.h, cmask_done)
                        t.wait_ge(rp.h, cvbm_ready)
                        cx.war(t, "psC")
                        cx.war(t, "psZ")
                        for qc in range(NTT):
                            t.matmul(psC[:, qc * 128:(qc + 1) * 128],
                                     expsl[0][:, qc * 128:(qc + 1) * 128],
                                     T["cvbm"][:, kvh * 128:(kvh + 1) * 128],
                                     start=(qc == 0), stop=False,
                                     skip_group_check=True)
                            t.matmul(psC[:, qc * 128:(qc + 1) * 128],
                                     T["exp1"][0:1, qc * 128:(qc + 1) * 128],
                                     T["memv_s"][:, kvh * 128:(kvh + 1) * 128],
                                     start=False, stop=True,
                                     skip_group_check=True)
                            t.matmul(psZ[:, qc * 3:qc * 3 + 1],
                                     expsl[0][:, qc * 128:(qc + 1) * 128],
                                     T["onescb"][:, :], start=(qc == 0), stop=False,
                                     skip_group_check=True)
                            i = t.matmul(psZ[:, qc * 3:qc * 3 + 1],
                                         T["exp1"][0:1, qc * 128:(qc + 1) * 128],
                                         T["onescb"][0:1, :], start=False, stop=True,
                                         skip_group_check=True)
                        mm.inc(i, 1)
                    cx.on("tensor", cpv) if DBG["cmp"] else None
                    cx.use("exp0", mm, mm.n)
                    cx.use_add("exp1", mm, mm.n)

                    def hr0(t, qsl=qsl, q4_ready=q4_ready):
                        t.wait_ge(ev.h, q4_ready)
                        cx.war(t, "psU")
                        i = t.matmul(psU[:, 0:512], T["ropeP"][:, :],
                                     T["q4"][:, qsl], start=True, stop=True)
                        mm.inc(i, 1)
                    cx.on("tensor", hr0)
                    hr0_done = mm.n

                    def hr1(v, qsl=qsl, hr0_done=hr0_done, q4_ready=q4_ready,
                            gconst_ld=gconst_ld):
                        v.wait_ge(ev.h, q4_ready)
                        v.wait_ge(pld.h, gconst_ld)
                        cx.war(v, "rt12")
                        i = v.tensor_tensor(T["rt1"][:, :], T["q4"][:, qsl],
                                            T["ropeCS"][:, 0:512], op=ALU.mult)
                        chv.inc(i, 1); v.wait_ge(chv.h, chv.n)
                        v.wait_ge(mm.h, hr0_done)
                        i = v.tensor_tensor(T["rt2"][:, :], psU[:, 0:512],
                                            T["ropeCS"][:, 512:1024], op=ALU.mult)
                        chv.inc(i, 1); v.wait_ge(chv.h, chv.n)
                        cx.use_add("psU", chv, chv.n)
                        cx.war(v, "rqh")
                        i = v.tensor_tensor(T["rqh"][:, :], T["rt1"][:, :],
                                            T["rt2"][:, :], op=ALU.add)
                        rp.inc(i, 1)
                        cx.use("rt12", rp, rp.n)
                    cx.on("vector", hr1)
                    rqh_ready = rp.n

                    NKB = 4 * (g + 1)
                    for kb in range(NKB if DBG["fine"] else 0):
                        sps, spsname = ((psA, "psA"), (psB, "psB"))[kb % 2]
                        slot = 1 + (kb % 2)
                        esb = expsl[slot]
                        def fmm(t, kvh=kvh, kb=kb, sps=sps, spsname=spsname,
                                rqh_ready=rqh_ready, kv_ready=kv_ready,
                                selT_ready=selT_ready):
                            t.wait_ge(rp.h, max(rqh_ready, kv_ready, selT_ready))
                            cx.war(t, spsname)
                            t.matmul(sps[:, 0:512],
                                     rkT[:, kvh * S + kb * 128: kvh * S + (kb + 1) * 128],
                                     T["rqh"][:, :], start=True, stop=False)
                            i = t.matmul(sps[:, 0:512],
                                         T["selE_s"][:, kb * 128:(kb + 1) * 128],
                                         T["selT"][:, :], start=False, stop=True)
                            mm.inc(i, 1)
                        cx.on("tensor", fmm)
                        fmm_done = mm.n
                        cx.use_add("selTw", mm, mm.n)

                        def fexp(s_, sps=sps, esb=esb, slot=slot, fmm_done=fmm_done):
                            s_.wait_ge(mm.h, fmm_done)
                            cx.war(s_, f"exp{slot}")
                            i = s_.activation(esb[:, :], sps[:, 0:512], AF.Exp,
                                              scale=SC0)
                            lns.inc(i, 1)
                        cx.on("scalar", fexp)
                        cx.use(spsname, lns, lns.n)
                        fexp_done = lns.n

                        diag = (kb >= 4 * g)
                        if diag:
                            off = kb - 4 * g
                            def fmask(v, esb=esb, off=off, fexp_done=fexp_done):
                                v.wait_ge(lns.h, fexp_done)
                                i = v.tensor_tensor(
                                    esb[:, :], esb[:, :],
                                    T["fdiag_s"][:, off * 512:(off + 1) * 512],
                                    op=ALU.mult)
                                chv.inc(i, 1); v.wait_ge(chv.h, chv.n)
                            cx.on("vector", fmask)
                        fm_done = chv.n

                        def fpv(t, kvh=kvh, kb=kb, esb=esb, slot=slot, NKB=NKB,
                                diag=diag, fm_done=fm_done, fexp_done=fexp_done):
                            if diag:
                                t.wait_ge(chv.h, fm_done)
                            else:
                                t.wait_ge(lns.h, fexp_done)
                            if kb == 0:
                                cx.war(t, "psF")
                            for qc in range(NTT):
                                t.matmul(psF[:, qc * 128:(qc + 1) * 128],
                                         esb[:, qc * 128:(qc + 1) * 128],
                                         vtok[:, kvh * S + kb * 128: kvh * S + (kb + 1) * 128],
                                         start=(kb == 0 and qc == 0),
                                         stop=(kb == NKB - 1),
                                         skip_group_check=True)
                                i = t.matmul(psZ[:, qc * 3 + 1:qc * 3 + 2],
                                             esb[:, qc * 128:(qc + 1) * 128],
                                             T["onescb"][:, :],
                                             start=False,
                                             stop=(kb == NKB - 1),
                                             skip_group_check=True)
                            mm.inc(i, 1)
                        cx.on("tensor", fpv)
                        cx.use(f"exp{slot}", mm, mm.n)

                    kb0 = max(0, 4 * g - 1)
                    for kb in range(kb0, 4 * g + 4) if DBG["slide"] else []:
                        sps, spsname = ((psA, "psA"), (psB, "psB"))[kb % 2]
                        slot = 1 + (kb % 2)
                        esb = expsl[slot]
                        first = (kb == kb0)
                        last = (kb == 4 * g + 3)
                        def smm2(t, kvh=kvh, kb=kb, sps=sps, spsname=spsname,
                                 rqh_ready=rqh_ready, kv_ready=kv_ready):
                            t.wait_ge(rp.h, max(rqh_ready, kv_ready))
                            cx.war(t, spsname)
                            i = t.matmul(sps[:, 0:512],
                                         rkT[:, kvh * S + kb * 128: kvh * S + (kb + 1) * 128],
                                         T["rqh"][:, :], start=True, stop=True)
                            mm.inc(i, 1)
                        cx.on("tensor", smm2)
                        smm2_done = mm.n

                        def sexp2(s_, sps=sps, esb=esb, slot=slot,
                                  smm2_done=smm2_done):
                            s_.wait_ge(mm.h, smm2_done)
                            cx.war(s_, f"exp{slot}")
                            i = s_.activation(esb[:, :], sps[:, 0:512], AF.Exp,
                                              scale=SC0)
                            lns.inc(i, 1)
                        cx.on("scalar", sexp2)
                        cx.use(spsname, lns, lns.n)
                        sexp2_done = lns.n

                        off = kb - 4 * g + 1
                        def smask(v, esb=esb, off=off, sexp2_done=sexp2_done):
                            v.wait_ge(lns.h, sexp2_done)
                            i = v.tensor_tensor(
                                esb[:, :], esb[:, :],
                                T["band_s"][:, off * 512:(off + 1) * 512],
                                op=ALU.mult)
                            chv.inc(i, 1); v.wait_ge(chv.h, chv.n)
                        cx.on("vector", smask)
                        sm_done = chv.n

                        def spv(t, kvh=kvh, kb=kb, esb=esb, slot=slot,
                                first=first, last=last, sm_done=sm_done):
                            t.wait_ge(chv.h, sm_done)
                            if first:
                                cx.war(t, "psS")
                            for qc in range(NTT):
                                t.matmul(psS[:, qc * 128:(qc + 1) * 128],
                                         esb[:, qc * 128:(qc + 1) * 128],
                                         vtok[:, kvh * S + kb * 128: kvh * S + (kb + 1) * 128],
                                         start=(first and qc == 0), stop=last,
                                         skip_group_check=True)
                                i = t.matmul(psZ[:, qc * 3 + 2:qc * 3 + 3],
                                             esb[:, qc * 128:(qc + 1) * 128],
                                             T["onescb"][:, :],
                                             start=False, stop=last,
                                             skip_group_check=True)
                            mm.inc(i, 1)
                        cx.on("tensor", spv)
                        cx.use(f"exp{slot}", mm, mm.n)
                    allpv_done = mm.n

                    def zev2(v, h=h, allpv_done=allpv_done):
                        v.wait_ge(mm.h, allpv_done)
                        v.wait_ge(rp.h, 0)
                        i = v.tensor_copy(T["Zsb"][:, :], psZ[:, 0:12])
                        chv.inc(i, 1); v.wait_ge(chv.h, chv.n)
                        cx.use("psZ", chv, chv.n)
                        i = v.reciprocal(T["Zinv"][:, :], T["Zsb"][:, :])
                        chv.inc(i, 1); v.wait_ge(chv.h, chv.n)
                        gv = T["gsig"][:, :].rearrange("p (qc x) -> p qc x", x=96)[
                            :, :, 3 * h: 3 * h + 3]
                        zv = T["Zinv"][:, :].rearrange("p (qc b) -> p qc b", b=3)
                        gzv = T["gz"][:, :].rearrange("p (qc b) -> p qc b", b=3)
                        i = v.tensor_tensor(gzv, gv, zv, op=ALU.mult)
                        chv.inc(i, 1); v.wait_ge(chv.h, chv.n)
                    cx.on("vector", zev2)
                    gz_done = chv.n

                    for qc in range(NTT):
                        def comb(v, qc=qc, gz_done=gz_done):
                            v.wait_ge(chv.h, gz_done)
                            cx.war(v, "rt12")
                            i = v.tensor_scalar(T["rt1"][:, 0:128], psC[:, qc * 128:(qc + 1) * 128],
                                                T["gz"][:, qc * 3:qc * 3 + 1], None,
                                                op0=ALU.mult)
                            chv.inc(i, 1); v.wait_ge(chv.h, chv.n)
                            i = v.scalar_tensor_tensor(
                                T["rt2"][:, 0:128], psF[:, qc * 128:(qc + 1) * 128],
                                T["gz"][:, qc * 3 + 1:qc * 3 + 2],
                                T["rt1"][:, 0:128], op0=ALU.mult, op1=ALU.add)
                            chv.inc(i, 1); v.wait_ge(chv.h, chv.n)
                            cx.war(v, "xatt_tok_w")
                            i = v.scalar_tensor_tensor(
                                T["xatt_tok"][:, :], psS[:, qc * 128:(qc + 1) * 128],
                                T["gz"][:, qc * 3 + 2:qc * 3 + 3],
                                T["rt2"][:, 0:128], op0=ALU.mult, op1=ALU.add)
                            ev.inc(i, 1)
                            cx.use("rt12", ev, ev.n)
                        cx.on("vector", comb)
                        comb_done = ev.n
                        if qc == NTT - 1:
                            cx.use("psC", ev, ev.n)
                            cx.use("psF", ev, ev.n)
                            cx.use("psS", ev, ev.n)

                        def xt(t, comb_done=comb_done):
                            t.wait_ge(ev.h, comb_done)
                            cx.war(t, "ptrA")
                            i = t.transpose(ptrA[:, 0:128], T["xatt_tok"][:, :],
                                            ident[:, :])
                            lnt.inc(i, 1)
                        cx.on("tensor", xt)
                        cx.use("xatt_tok_w", lnt, lnt.n)
                        t_done = lnt.n

                        def xtev(v, gh=gh, qc=qc, t_done=t_done):
                            v.wait_ge(lnt.h, t_done)
                            if qc == 0:
                                cx.war(v, "xattT_kvh")
                            i = v.tensor_copy(
                                T["xattT_kvh"][:, qc * 128:(qc + 1) * 128],
                                ptrA[:, 0:128])
                            rp.inc(i, 1)
                        cx.on("vector", xtev)
                        cx.use("ptrA", rp, rp.n)
                    xatt_done = rp.n

                    def xst(y, h=h, g=g, xatt_done=xatt_done):
                        y.wait_ge(rp.h, xatt_done)
                        i = y.dma_start(
                            out=xattT_out[h * 128:(h + 1) * 128,
                                          g * G512:(g + 1) * G512],
                            in_=T["xattT_kvh"][:, :])
                        st.inc(i, 16)
                    cx.on("sync", xst)
                    cx.use("xattT_kvh", st, st.n)
                cx.use("q4", mm, mm.n)
                cx.use_add("q4", rp, rp.n)
            cx.use("xlnT", mm, mm.n)
            cx.use("selvis_g", chv, chv.n)
            cx.use("ownm1_g", ev, ev.n)
            cx.use("cmpvis_g", chv, chv.n)
            cx.use("ropeCS", chv, chv.n)
            cx.use("ckT", mm, mm.n)
            cx.use("cvT", mm, mm.n)
            cx.use("cvbm", mm, mm.n)
            cx.use("gsig", chv, chv.n)
            cx.use("q4", mm, mm.n)
            cx.use_add("q4", rp, rp.n)

        if DBG["dump"]:
            def dmpf(y):
                y.wait_ge(rp.h, rp.n)
                y.wait_ge(ev.h, ev.n)
                y.wait_ge(mm.h, mm.n)
                for nm in dmp:
                    st.inc(y.dma_start(out=dmp[nm], in_=T[nm][:, :]), 16)
            cx.on("sync", dmpf)
        final_st = st.n
        def fin(y, final_st=final_st):
            y.wait_ge(st.h, final_st)
        cx.on("sync", fin)
        if fz is not None:
            fz["barrier"] = (st.h, st.n)
    return nc



def build_proj_residual(fz=None):
    nc, pre, io = fz_begin(fz)
    xres = io.get("xres") if fz else nc.dram_tensor("xres", [S, D], F32, kind="ExternalInput").ap()
    xT = io.get("xT") if fz else nc.dram_tensor("xT", [D, S], BF16, kind="ExternalInput").ap()
    wT = nc.dram_tensor(pre + "wT", [D, D], F32, kind="ExternalInput").ap()
    xout = io.get("xout") if fz else nc.dram_tensor("xout", [S, D], F32, kind="ExternalOutput").ap()

    OB = 256
    NOB = D // OB

    es = contextlib.ExitStack()
    with es:
        tens, sems, block = alloc(
            nc, es,
            sbufs=[("lh0", [128, NIC * 128], BF16), ("lh1", [128, NIC * 128], BF16),
                   ("wbf0", [128, NIC * OB], BF16), ("wbf1", [128, NIC * OB], BF16),
                   ("xr", [128, 2 * D], F32),
                   ("outsb", [128, 2 * OB], F32)],
            psums=[("psA", [128, 512], F32), ("psB", [128, 512], F32)],
            semnames=["ld", "lld", "wld", "mm", "ev", "st"],
            sem_es=(fz["es"] if fz else None), prefix=pre)
        cx = Ctx(nc, block, sems)
        fz_barrier(cx, fz)
        ld, lld, wld, mm, ev, st = (sems[k] for k in ("ld", "lld", "wld", "mm", "ev", "st"))
        lh = [tens["lh0"], tens["lh1"]]
        wb = [tens["wbf0"], tens["wbf1"]]
        xr = tens["xr"]; outsb = tens["outsb"]
        psA, psB = tens["psA"], tens["psB"]

        for ob in range(NOB):
            par = ob % 2
            wbf = wb[par]
            def wload(y, ob=ob, wbf=wbf, par=par):
                cx.war(y, f"wb{par}")
                src = wT.rearrange("(ic p) o -> p ic o", p=128)
                for quarter in range(4):
                    dst = wbf[:, quarter * 8 * OB:(quarter + 1) * 8 * OB]
                    dst = dst.rearrange("p (ic o) -> p ic o", o=OB)
                    i = y.dma_start(out=dst,
                                    in_=src[:, quarter * 8:(quarter + 1) * 8,
                                            ob * OB:(ob + 1) * OB])
                    wld.inc(i, 16)
            cx.on("gpsimd", wload)
            wld_done = wld.n

            for tt in range(16):
                tpar = tt % 2
                lhb = lh[tpar]
                def lhload(y, ob=ob, tt=tt, lhb=lhb, tpar=tpar):
                    cx.war(y, f"lh{tpar}")
                    src = xT.rearrange("(ic p) t -> p ic t", p=128)
                    i = y.dma_start(
                        out=lhb[:, :].rearrange("p (ic t) -> p ic t", t=128),
                        in_=src[:, :, tt * 128:(tt + 1) * 128])
                    lld.inc(i, 16)
                    cx.war(y, f"xr{tpar}")
                    i = y.dma_start(out=xr[:, tpar * D:tpar * D + D],
                                    in_=xres[tt * 128:(tt + 1) * 128, :])
                    ld.inc(i, 16)
                cx.on("sync", lhload)
                lh_done = lld.n
                xr_done = ld.n

                ps, psname = ((psA, "psA"), (psB, "psB"))[tpar]
                def mmf(t, ob=ob, tt=tt, lhb=lhb, wbf=wbf, ps=ps, psname=psname,
                        wld_done=wld_done, lh_done=lh_done):
                    t.wait_ge(wld.h, wld_done)
                    t.wait_ge(lld.h, lh_done)
                    cx.war(t, psname)
                    for ic in range(NIC):
                        i = t.matmul(ps[:, 0:OB],
                                     lhb[:, ic * 128:(ic + 1) * 128],
                                     wbf[:, ic * OB:(ic + 1) * OB],
                                     start=(ic == 0), stop=(ic == NIC - 1))
                    mm.inc(i, 1)
                cx.on("tensor", mmf)
                mm_done = mm.n
                cx.use(f"lh{tpar}", mm, mm.n)
                if tt == 15:
                    cx.use(f"wb{par}", mm, mm.n)

                slot = tt % 2
                def evf(v, ob=ob, tt=tt, ps=ps, mm_done=mm_done, slot=slot,
                        xr_done=xr_done, tpar=tpar):
                    v.wait_ge(mm.h, mm_done)
                    v.wait_ge(ld.h, xr_done)
                    cx.war(v, f"outsb{slot}")
                    i = v.tensor_tensor(
                        outsb[:, slot * OB:(slot + 1) * OB], ps[:, 0:OB],
                        xr[:, tpar * D + ob * OB: tpar * D + (ob + 1) * OB],
                        op=ALU.add)
                    ev.inc(i, 1)
                cx.on("vector", evf)
                cx.use(psname, ev, ev.n)
                cx.use_add(f"xr{tpar}", ev, ev.n)
                ev_done = ev.n

                def stf(y, ob=ob, tt=tt, ev_done=ev_done, slot=slot):
                    y.wait_ge(ev.h, ev_done)
                    i = y.dma_start(out=xout[tt * 128:(tt + 1) * 128,
                                             ob * OB:(ob + 1) * OB],
                                    in_=outsb[:, slot * OB:(slot + 1) * OB])
                    st.inc(i, 16)
                cx.on("gpsimd", stf)
                cx.use(f"outsb{slot}", st, st.n)

        final_st = st.n
        def fin(y, final_st=final_st):
            y.wait_ge(st.h, final_st)
        cx.on("gpsimd", fin)
        if fz is not None:
            fz["barrier"] = (st.h, st.n)
    return nc


def prep_l0_attn_inputs(inputs, consts, xcur, pre=""):
    f32 = np.float32
    g = lambda k: np.asarray(inputs[k], dtype=f32)
    qkvwT = np.ascontiguousarray(g("L0_attn_w_qkv").T)
    wgT = np.ascontiguousarray(g("L0_attn_w_gate").T)
    bgrow = np.ascontiguousarray(g("L0_attn_b_gate").reshape(1, 96))
    kw1T = np.ascontiguousarray(g("L0_attn_k_w1").T)
    vw1T = np.ascontiguousarray(g("L0_attn_v_w1").T)
    kw2T = np.ascontiguousarray(g("L0_attn_k_w2").T)
    vw2T = np.ascontiguousarray(g("L0_attn_v_w2").T)
    kb1r = np.ascontiguousarray(g("L0_attn_k_b1").reshape(16, 128).T)
    vb1r = np.ascontiguousarray(g("L0_attn_v_b1").reshape(16, 128).T)
    kb2c = np.ascontiguousarray(g("L0_attn_k_b2").reshape(128, 1))
    vb2c = np.ascontiguousarray(g("L0_attn_v_b2").reshape(128, 1))
    kp = g("L0_attn_k_pos")
    vp = g("L0_attn_v_pos")
    kpT = np.ascontiguousarray(np.stack(
        [np.tile(kp[h], (32, 1)).T for h in range(8)]))
    vpT = np.ascontiguousarray(np.stack(
        [np.tile(vp[h], (32, 1)).T for h in range(8)]))
    mem = g("L0_attn_mem_kv")
    memkT = np.ascontiguousarray(mem[0, :, 0, :].T.astype(BF))
    memv = np.ascontiguousarray(mem[1, :, 0, :].reshape(1, 8 * 128).astype(BF))
    maps = []
    for core in range(8):
        maps.append({
            "xin": np.ascontiguousarray(xcur[core]),
            "qkvwT": qkvwT, "wgT": wgT, "bgrow": bgrow,
            "kw1T": kw1T, "vw1T": vw1T, "kw2T": kw2T, "vw2T": vw2T,
            "kb1r": kb1r, "vb1r": vb1r, "kb2c": kb2c, "vb2c": vb2c,
            "kpT": kpT, "vpT": vpT, "memkT": memkT, "memv": memv,
            "ident_bf": consts["ident_bf"],
            "ones_col_bf": consts["ones_col_bf"],
            "ones_row_bf": consts["ones_row_bf"],
            "ropeC0": consts["rope0_C"], "ropeS0": consts["rope0_S"],
            "ropeP0": consts["rope0_P"],
            "selvis": consts["sel_vis_add"],
            "ownm1": (consts["own_oh"] - 1.0).astype(np.float32),
            "cmpvis": consts["cmp_vis01"],
            "fdiag": consts["fine_diag01"], "band": consts["band01"],
            "selE": consts["selE"],
        })
    return maps



def build_l1_attn(fz=None):
    nc, pre, io = fz_begin(fz)
    xin = io.get("xin") if fz else nc.dram_tensor("xin", [S, D], F32, kind="ExternalInput").ap()
    qkvwT = nc.dram_tensor(pre + "qkvwT", [D, 4608], F32, kind="ExternalInput").ap()
    qkvbT = nc.dram_tensor(pre + "qkvbT", [128, 36], F32, kind="ExternalInput").ap()
    dwT = nc.dram_tensor(pre + "dwT", [D, D], F32, kind="ExternalInput").ap()
    ident_in = nc.dram_tensor(pre + "ident_bf", [128, 128], BF16, kind="ExternalInput").ap()
    onesc_in = nc.dram_tensor(pre + "ones_col_f", [128, 1], F32, kind="ExternalInput").ap()
    onesr_in = nc.dram_tensor(pre + "ones_row_f", [1, 128], F32, kind="ExternalInput").ap()
    ropeC_in = nc.dram_tensor(pre + "ropeC", [64, S], F32, kind="ExternalInput").ap()
    ropeS_in = nc.dram_tensor(pre + "ropeS", [64, S], F32, kind="ExternalInput").ap()
    ropeP_in = nc.dram_tensor(pre + "ropeP", [128, 64], BF16, kind="ExternalInput").ap()
    diag_in = nc.dram_tensor(pre + "diag01", [4, 128, 512], BF16, kind="ExternalInput").ap()
    xout = io.get("xout") if fz else nc.dram_tensor("xout", [S, D], F32, kind="ExternalOutput").ap()

    NH = 32
    GQ = 16
    SCALE = 1.0 / np.sqrt(128.0)

    es = contextlib.ExitStack()
    with es:
        tens, sems, block = alloc(
            nc, es,
            sbufs=[
                ("x_g", [128, D], F32), ("xlnT", [128, NIC * G512], BF16),
                ("bs", [128, 48], F32), ("stats", [128, 16], F32),
                ("xln_tok", [128, D], BF16), ("ident", [128, 128], BF16),
                ("eps_col", [128, 1], F32),
                ("onesc", [128, 1], F32), ("onesr", [1, 128], F32),
                ("biasT", [128, 36], F32),
                ("ropeCS", [128, S], F32),
                ("ropeP_s", [128, 64], BF16), ("diag01_s", [128, 4 * 512], BF16),
                ("qw0", [128, NIC * 128], BF16), ("qw1", [128, NIC * 128], BF16),
                ("rq", [128, NH * G512], BF16),
                ("ktmp", [128, 2 * G512], BF16),
                ("krT", [128, 2 * S], BF16),
                ("vtmp", [128, 2 * G512], BF16),
                ("vtok", [128, 2 * S], BF16),
                ("ctxT", [128, NH * G512], BF16),
                ("expS0", [128, 512], BF16), ("expS1", [128, 512], BF16),
                ("expS2", [128, 512], BF16),
                ("Zacc", [128, 512], F32), ("zrow", [1, 512], F32),
                ("zinv", [1, 512], F32), ("zb_sb", [128, 512], F32),
                ("rt1", [64, 512], F32), ("rt2", [64, 512], F32),
                ("dw0", [128, NIC * 128], BF16), ("dw1", [128, NIC * 128], BF16),
                ("outsb", [128, 2 * 128], F32), ("xres", [128, NTT * 128], F32),
            ],
            psums=[("psA", [128, 512], F32), ("psB", [128, 512], F32),
                   ("psC", [128, 512], F32), ("psD", [128, 512], F32),
                   ("psE", [128, 512], F32),
                   ("ptrA", [128, 1024], BF16), ("ptrB", [128, 1024], BF16)],
            semnames=["ld", "lnv", "lns", "lnt", "lne", "wld", "mm",
                      "ev", "st", "chv", "chs", "rp"],
            sem_es=(fz["es"] if fz else None), prefix=pre)
        cx = Ctx(nc, block, sems)
        fz_barrier(cx, fz)
        ld, lnv, lns, lnt, lne, wld, mm, ev, st, chv, chs, rp = (
            sems[k] for k in ("ld", "lnv", "lns", "lnt", "lne", "wld", "mm",
                              "ev", "st", "chv", "chs", "rp"))
        x_g = tens["x_g"]; xlnT = tens["xlnT"]; ident = tens["ident"]
        rq = tens["rq"]; krT = tens["krT"]; vtok = tens["vtok"]
        ktmp = tens["ktmp"]; vtmp = tens["vtmp"]; ctxT = tens["ctxT"]
        Zacc = tens["Zacc"]; zrow = tens["zrow"]; zinv = tens["zinv"]
        zb_sb = tens["zb_sb"]; rt1 = tens["rt1"]; rt2 = tens["rt2"]
        psA, psB, psC, psD, psE = (tens[k] for k in ("psA", "psB", "psC", "psD", "psE"))
        ptrA, ptrB = tens["ptrA"], tens["ptrB"]
        outsb = tens["outsb"]; xres = tens["xres"]
        expS = [tens["expS0"], tens["expS1"], tens["expS2"]]

        sb = {"x_g": x_g, "xlnT": xlnT, "bs": tens["bs"], "stats": tens["stats"],
              "xln_tok": tens["xln_tok"], "ident": ident, "ptrA": ptrA,
              "ptrB": ptrB, "eps_col": tens["eps_col"]}

        def init_eps(v):
            v.memset(tens["eps_col"][:, :], EPS)
        cx.on("vector", init_eps)

        def prolog_dma(y):
            for dst, src in ((ident, ident_in), (tens["onesc"], onesc_in),
                             (tens["onesr"], onesr_in), (tens["biasT"], qkvbT),
                             (tens["ropeP_s"], ropeP_in)):
                ld.inc(y.dma_start(out=dst[:, :], in_=src[:, :]), 16)
            ld.inc(y.dma_start(out=tens["ropeCS"][0:64, :], in_=ropeC_in[:, :]), 16)
            ld.inc(y.dma_start(out=tens["ropeCS"][64:128, :], in_=ropeS_in[:, :]), 16)
            for j in range(4):
                ld.inc(y.dma_start(out=tens["diag01_s"][:, j * 512:(j + 1) * 512],
                                   in_=diag_in[j, :, :]), 16)
        cx.on("sync", prolog_dma)
        pro_ld = ld.n

        for eng in ("tensor", "vector", "scalar"):
            def gate(e, pro_ld=pro_ld):
                e.wait_ge(ld.h, pro_ld)
            cx.on(eng, gate)

        for g in range(NG):
            emit_ln_group(cx, g, xin, sb, xg_tiled=False)
            xln_ready = lne.n

            for ob in range(36):
                par = ob % 2
                qw, qwname = ((tens["qw0"], "qw0"), (tens["qw1"], "qw1"))[par]

                def wload(y, ob=ob, qw=qw, qwname=qwname):
                    cx.war(y, qwname)
                    src = qkvwT.rearrange("(ic p) o -> p ic o", p=128)
                    for half in range(2):
                        dst = qw[:, half * 16 * 128:(half + 1) * 16 * 128]
                        dst = dst.rearrange("p (ic o) -> p ic o", o=128)
                        i = y.dma_start(
                            out=dst,
                            in_=src[:, half * 16:(half + 1) * 16,
                                    ob * 128:(ob + 1) * 128])
                        wld.inc(i, 16)
                cx.on("gpsimd", wload)
                wld_done = wld.n

                ps, psname = ((psA, "psA"), (psB, "psB"))[par]
                def pmm(t, ob=ob, qw=qw, ps=ps, psname=psname,
                        wld_done=wld_done, xln_ready=xln_ready):
                    t.wait_ge(wld.h, wld_done)
                    t.wait_ge(lne.h, xln_ready)
                    cx.war(t, psname)
                    for ic in range(NIC):
                        i = t.matmul(ps[:, 0:512],
                                     qw[:, ic * 128:(ic + 1) * 128],
                                     xlnT[:, ic * G512:(ic + 1) * G512],
                                     start=(ic == 0), stop=(ic == NIC - 1))
                    mm.inc(i, 1)
                cx.on("tensor", pmm)
                cx.use(qwname, mm, mm.n)
                mm_done = mm.n

                if ob < 32:
                    dst, dname, doff = rq, "rq", ob * G512
                elif ob < 34:
                    dst, dname, doff = ktmp, "ktmp", (ob - 32) * G512
                else:
                    dst, dname, doff = vtmp, "vtmp", (ob - 34) * G512

                def pev(v, ob=ob, ps=ps, mm_done=mm_done, dst=dst,
                        dname=dname, doff=doff):
                    v.wait_ge(mm.h, mm_done)
                    cx.war(v, dname)
                    i = v.tensor_scalar(dst[:, doff:doff + G512], ps[:, 0:512],
                                        tens["biasT"][:, ob:ob + 1], None,
                                        op0=ALU.add)
                    ev.inc(i, 1)
                cx.on("vector", pev)
                cx.use(psname, ev, ev.n)
                ev_done = ev.n

                if ob < 34:
                    if ob < 32:
                        tgt, toff = rq, ob * G512
                    else:
                        tgt, toff = krT, (ob - 32) * S + g * G512
                    def r0(t, ev_done=ev_done, dst=dst, doff=doff):
                        t.wait_ge(ev.h, ev_done)
                        cx.war(t, "psD")
                        i = t.matmul(psD[0:64, 0:512], tens["ropeP_s"][:, :],
                                     dst[:, doff:doff + G512], start=True, stop=True)
                        mm.inc(i, 1)
                    cx.on("tensor", r0)
                    r0_done = mm.n

                    def r1(v, ev_done=ev_done, dst=dst, doff=doff, tgt=tgt,
                           toff=toff, g=g, r0_done=r0_done):
                        gsl = slice(g * G512, (g + 1) * G512)
                        v.wait_ge(ev.h, ev_done)
                        cx.war(v, "rt12")
                        i = v.tensor_tensor(rt1[:, :], dst[0:64, doff:doff + G512],
                                            tens["ropeCS"][0:64, gsl], op=ALU.mult)
                        chv.inc(i, 1); v.wait_ge(chv.h, chv.n)
                        v.wait_ge(mm.h, r0_done)
                        i = v.tensor_tensor(rt2[:, :], psD[0:64, 0:512],
                                            tens["ropeCS"][64:128, gsl], op=ALU.mult)
                        chv.inc(i, 1); v.wait_ge(chv.h, chv.n)
                        cx.use_add("psD", chv, chv.n)
                        i = v.tensor_tensor(tgt[0:64, toff:toff + G512],
                                            rt1[:, :], rt2[:, :], op=ALU.add)
                        rp.inc(i, 1)
                        cx.use("rt12", rp, rp.n)
                        if tgt is not dst:
                            i = v.tensor_copy(tgt[64:128, toff:toff + G512],
                                              dst[64:128, doff:doff + G512])
                            rp.inc(i, 1)
                    cx.on("vector", r1)
                else:
                    kvh = ob - 34
                    for tt in range(NTT):
                        bank, bn = ((ptrA, "ptrA"), (ptrB, "ptrB"))[tt % 2]
                        def vt(t, ev_done=ev_done, tt=tt, bank=bank, bn=bn,
                               kvh=kvh):
                            t.wait_ge(ev.h, ev_done)
                            cx.war(t, bn)
                            i = t.transpose(bank[:, 0:128],
                                            vtmp[:, kvh * G512 + tt * 128:
                                                 kvh * G512 + (tt + 1) * 128],
                                            ident[:, :])
                            lnt.inc(i, 1)
                        cx.on("tensor", vt) if DBG["vtr"] else None
                        t_done = lnt.n
                        def vtev(v, t_done=t_done, tt=tt, bank=bank, kvh=kvh,
                                 g=g):
                            v.wait_ge(lnt.h, t_done)
                            kb = g * NTT + tt
                            i = v.tensor_copy(
                                vtok[:, kvh * S + kb * 128: kvh * S + (kb + 1) * 128],
                                bank[:, 0:128])
                            rp.inc(i, 1)
                        cx.on("vector", vtev) if DBG["vtr"] else None
                        cx.use(bn, rp, rp.n)
            cx.use("xlnT", mm, mm.n)
            rope_ready = rp.n

            NKB = 4 * (g + 1)
            for h in range(NH):
                kvh = h // GQ
                zacc_done = 0
                for kb in range(NKB):
                    sps, spsname = ((psA, "psA"), (psB, "psB"))[kb % 2]
                    slot = kb % 3
                    esb = expS[slot]

                    def smm(t, kb=kb, h=h, kvh=kvh, sps=sps, spsname=spsname,
                            rope_ready=rope_ready):
                        t.wait_ge(rp.h, rope_ready)
                        cx.war(t, spsname)
                        i = t.matmul(
                            sps[:, 0:512],
                            krT[:, kvh * S + kb * 128: kvh * S + (kb + 1) * 128],
                            rq[:, h * G512:(h + 1) * G512],
                            start=True, stop=True)
                        mm.inc(i, 1)
                    cx.on("tensor", smm)
                    smm_done = mm.n

                    def sexp(s_, sps=sps, esb=esb, slot=slot, smm_done=smm_done):
                        s_.wait_ge(mm.h, smm_done)
                        cx.war(s_, f"expS{slot}")
                        i = s_.activation(esb[:, :], sps[:, 0:512], AF.Exp,
                                          scale=SCALE)
                        lns.inc(i, 1)
                    cx.on("scalar", sexp)
                    cx.use(spsname, lns, lns.n)
                    exp_done = lns.n

                    diag = (kb >= 4 * g)
                    if diag:
                        off = kb - 4 * g
                        def dmask(v, esb=esb, off=off, exp_done=exp_done):
                            v.wait_ge(lns.h, exp_done)
                            i = v.tensor_tensor(
                                esb[:, :], esb[:, :],
                                tens["diag01_s"][:, off * 512:(off + 1) * 512],
                                op=ALU.mult)
                            chv.inc(i, 1); v.wait_ge(chv.h, chv.n)
                        cx.on("vector", dmask)

                    def zadd(v, kb=kb, esb=esb, exp_done=exp_done, diag=diag):
                        if not diag:
                            v.wait_ge(lns.h, exp_done)
                        if kb == 0:
                            i = v.tensor_copy(Zacc[:, :], esb[:, :])
                        else:
                            i = v.tensor_tensor(Zacc[:, :], Zacc[:, :],
                                                esb[:, :], op=ALU.add)
                        chv.inc(i, 1); v.wait_ge(chv.h, chv.n)
                    cx.on("vector", zadd)
                    zacc_done = chv.n

                    def pv(t, kb=kb, kvh=kvh, esb=esb, NKB=NKB,
                           zacc_done=zacc_done):
                        t.wait_ge(chv.h, zacc_done)
                        if kb == 0:
                            cx.war(t, "psC")
                        i = t.matmul(
                            psC[:, 0:512],
                            vtok[:, kvh * S + kb * 128: kvh * S + (kb + 1) * 128],
                            esb[:, :],
                            start=(kb == 0), stop=(kb == NKB - 1))
                        mm.inc(i, 1)
                    cx.on("tensor", pv)
                    cx.use(f"expS{slot}", mm, mm.n)
                pv_done = mm.n

                def zred(t, zacc_done=zacc_done):
                    t.wait_ge(chv.h, zacc_done)
                    cx.war(t, "psE")
                    i = t.matmul(psE[0:1, 0:512], tens["onesc"][:, :],
                                 Zacc[:, :], start=True, stop=True)
                    mm.inc(i, 1)
                cx.on("tensor", zred)
                zred_done = mm.n
                cx.use("Zacc", mm, zred_done)

                def zfin(v, zred_done=zred_done):
                    v.wait_ge(mm.h, zred_done)
                    i = v.tensor_copy(zrow[:, :], psE[0:1, 0:512])
                    chv.inc(i, 1); v.wait_ge(chv.h, chv.n)
                    i = v.reciprocal(zinv[:, :], zrow[:, :])
                    ev.inc(i, 1)
                cx.on("vector", zfin)
                cx.use("psE", ev, ev.n)
                zinv_done = ev.n

                def zbc(t, zinv_done=zinv_done):
                    t.wait_ge(ev.h, zinv_done)
                    cx.war(t, "psE")
                    i = t.matmul(psE[:, 0:512], tens["onesr"][:, :],
                                 zinv[:, :], start=True, stop=True)
                    mm.inc(i, 1)
                cx.on("tensor", zbc)
                zbc_done = mm.n

                def ctxev(v, h=h, zbc_done=zbc_done, pv_done=pv_done):
                    v.wait_ge(mm.h, max(zbc_done, pv_done))
                    i = v.tensor_copy(zb_sb[:, :], psE[:, 0:512])
                    chv.inc(i, 1); v.wait_ge(chv.h, chv.n)
                    cx.use("psE", chv, chv.n)
                    cx.war(v, "ctxT")
                    i = v.tensor_tensor(ctxT[:, h * G512:(h + 1) * G512],
                                        psC[:, 0:512], zb_sb[:, :], op=ALU.mult)
                    rp.inc(i, 1)
                cx.on("vector", ctxev)
                cx.use("psC", rp, rp.n)
            ctx_ready = rp.n

            for ob in range(32):
                par = ob % 2
                dw, dwname = ((tens["dw0"], "dw0"), (tens["dw1"], "dw1"))[par]

                def dload(y, ob=ob, dw=dw, dwname=dwname):
                    cx.war(y, dwname)
                    src = dwT.rearrange("(ic p) o -> p ic o", p=128)
                    for half in range(2):
                        dsx = dw[:, half * 16 * 128:(half + 1) * 16 * 128]
                        dsx = dsx.rearrange("p (ic o) -> p ic o", o=128)
                        i = y.dma_start(
                            out=dsx,
                            in_=src[:, half * 16:(half + 1) * 16,
                                    ob * 128:(ob + 1) * 128])
                        wld.inc(i, 16)
                cx.on("gpsimd", dload)
                dld_done = wld.n

                def xload(y, ob=ob, g=g):
                    cx.war(y, "xres")
                    i = y.dma_start(
                        out=xres[:, :].rearrange("p (tt o) -> p tt o", o=128),
                        in_=xin[g * G512:(g + 1) * G512,
                                ob * 128:(ob + 1) * 128].rearrange(
                                    "(tt p) o -> p tt o", p=128))
                    ld.inc(i, 16)
                cx.on("sync", xload)
                xres_done = ld.n

                for tt in range(NTT):
                    ps, psname = ((psA, "psA"), (psB, "psB"))[tt % 2]
                    def dmm(t, ob=ob, tt=tt, dw=dw, ps=ps, psname=psname,
                            dld_done=dld_done, ctx_ready=ctx_ready):
                        t.wait_ge(wld.h, dld_done)
                        t.wait_ge(rp.h, ctx_ready)
                        cx.war(t, psname)
                        for hc in range(NIC):
                            i = t.matmul(
                                ps[:, 0:128],
                                ctxT[:, hc * G512 + tt * 128: hc * G512 + (tt + 1) * 128],
                                dw[:, hc * 128:(hc + 1) * 128],
                                start=(hc == 0), stop=(hc == NIC - 1))
                        mm.inc(i, 1)
                    cx.on("tensor", dmm)
                    mm_done = mm.n
                    if tt == NTT - 1:
                        cx.use(dwname, mm, mm.n)

                    slot = (ob * NTT + tt) % 2
                    def evf(v, tt=tt, ps=ps, mm_done=mm_done, slot=slot,
                            xres_done=xres_done):
                        v.wait_ge(mm.h, mm_done)
                        v.wait_ge(ld.h, xres_done)
                        cx.war(v, f"outsb{slot}")
                        i = v.tensor_tensor(
                            outsb[:, slot * 128:(slot + 1) * 128], ps[:, 0:128],
                            xres[:, tt * 128:(tt + 1) * 128], op=ALU.add)
                        ev.inc(i, 1)
                    cx.on("vector", evf)
                    cx.use(psname, ev, ev.n)
                    cx.use_add("xres", ev, ev.n)
                    ev_done = ev.n

                    def stf(y, ob=ob, tt=tt, ev_done=ev_done, g=g, slot=slot):
                        y.wait_ge(ev.h, ev_done)
                        t0 = g * G512 + tt * 128
                        i = y.dma_start(
                            out=xout[t0:t0 + 128, ob * 128:(ob + 1) * 128],
                            in_=outsb[:, slot * 128:(slot + 1) * 128])
                        st.inc(i, 16)
                    cx.on("gpsimd", stf)
                    cx.use(f"outsb{slot}", st, st.n)

            cx.use("ctxT", mm, mm.n)
            cx.use("rq", mm, mm.n)
            cx.use("ktmp", mm, mm.n)
            cx.use("vtmp", lnt, lnt.n)

        final_st = st.n
        def fin(y, final_st=final_st):
            y.wait_ge(st.h, final_st)
        cx.on("gpsimd", fin)
        if fz is not None:
            fz["barrier"] = (st.h, st.n)

    return nc


def prep_l1_attn_inputs(inputs, consts, xcur, pre=""):
    qkvwT = np.ascontiguousarray(np.asarray(inputs["L1_attn_qkv_w"], dtype=np.float32).T)
    qkvbT = np.ascontiguousarray(
        np.asarray(inputs["L1_attn_qkv_b"], dtype=np.float32).reshape(36, 128).T)
    dwT = np.ascontiguousarray(np.asarray(inputs["L1_attn_dense_w"], dtype=np.float32).T)
    maps = []
    for core in range(8):
        maps.append({
            "xin": np.ascontiguousarray(xcur[core]),
            "qkvwT": qkvwT, "qkvbT": qkvbT, "dwT": dwT,
            "ident_bf": consts["ident_bf"],
            "ones_col_f": consts["ones_col_f"],
            "ones_row_f": consts["ones_row_f"],
            "ropeC": np.ascontiguousarray(consts["rope1_C"][:64]),
            "ropeS": np.ascontiguousarray(consts["rope1_S"][:64]),
            "ropeP": consts["rope1_P"],
            "diag01": consts["l1_diag01"],
        })
    return maps



def build_l1_ffn(fz=None):
    nc, pre, io = fz_begin(fz)
    xin = io.get("xin") if fz else nc.dram_tensor("xin", [S, D], F32, kind="ExternalInput").ap()
    wupT = nc.dram_tensor(pre + "wupT", [D, D], F32, kind="ExternalInput").ap()
    atgw = nc.dram_tensor(pre + "atgw", [D, 36], F32, kind="ExternalInput").ap()
    btc = nc.dram_tensor(pre + "btc", [8, 4 * D], F32, kind="ExternalInput").ap()
    ebias = nc.dram_tensor(pre + "ebias", [1, 4], F32, kind="ExternalInput").ap()
    ident_in = nc.dram_tensor(pre + "ident_bf", [128, 128], BF16, kind="ExternalInput").ap()
    onesr_in = nc.dram_tensor(pre + "ones_row_f", [1, 128], F32, kind="ExternalInput").ap()
    xout = io.get("xout") if fz else nc.dram_tensor("xout", [S, D], F32, kind="ExternalOutput").ap()

    OB = 256
    NOB = D // OB

    es = contextlib.ExitStack()
    with es:
        tens, sems, block = alloc(
            nc, es,
            sbufs=[("x_g", [128, NTT * D], F32), ("xlnT", [128, NIC * G512], BF16),
                   ("bs", [128, 48], F32), ("stats", [128, 16], F32),
                   ("xln_tok", [128, D], BF16), ("ident", [128, 128], BF16),
                   ("onesr", [1, 128], F32), ("ebrow", [1, 4], F32),
                   ("ebb", [128, 4], F32), ("atgw_bf", [128, NIC * 36], BF16),
                   ("bt_bf", [8, 4 * D], BF16),
                   ("wbf0", [128, NIC * OB], BF16), ("wbf1", [128, NIC * OB], BF16),
                   ("z_sb", [128, 32], F32), ("lg", [128, 8], F32),
                   ("oh", [128, 8], F32), ("zm_bf", [128, 32], BF16),
                   ("zmT", [8, NTT * 4 * 128], BF16), ("outsb", [128, 2 * OB], F32),
                   ("eps_col", [128, 1], F32)],
            psums=[("psA", [128, 512], F32), ("psB", [128, 512], F32),
                   ("psZ", [128, 512], F32),
                   ("ptrA", [128, 1024], BF16), ("ptrB", [128, 1024], BF16)],
            semnames=["ld", "lnv", "lns", "lnt", "lne", "wld", "wdq", "mm", "ev", "st", "chv", "chs"],
            sem_es=(fz["es"] if fz else None), prefix=pre)
        (x_g, xlnT, stats, xln_tok, ident, onesr, ebrow, ebb, atgw_bf,
         bt_bf, wbf0, wbf1, z_sb, lg, oh, zm_bf, zmT, outsb,
         psA, psB, psZ, ptrA, ptrB) = (
            tens[k] for k in ("x_g", "xlnT", "stats", "xln_tok", "ident",
                              "onesr", "ebrow", "ebb", "atgw_bf", "bt_bf",
                              "wbf0", "wbf1", "z_sb", "lg", "oh", "zm_bf",
                              "zmT", "outsb", "psA", "psB", "psZ", "ptrA", "ptrB"))
        cx = Ctx(nc, block, sems)
        fz_barrier(cx, fz)
        ld, lnv, lns, lnt, lne, wld, wdq, mm, ev, st = (
            sems[k] for k in ("ld", "lnv", "lns", "lnt", "lne", "wld", "wdq",
                              "mm", "ev", "st"))

        sb = {"x_g": x_g, "xlnT": xlnT, "bs": tens["bs"], "stats": stats,
              "xln_tok": xln_tok, "ident": ident, "ptrA": ptrA, "ptrB": ptrB,
              "eps_col": tens["eps_col"]}
        def init_eps(v):
            v.memset(tens["eps_col"][:, :], EPS)
        cx.on("vector", init_eps)

        def prolog_dma(y):
            ld.inc(y.dma_start(out=ident[:, :], in_=ident_in[:, :]), 16)
            ld.inc(y.dma_start(out=onesr[:, :], in_=onesr_in[:, :]), 16)
            ld.inc(y.dma_start(out=ebrow[:, :], in_=ebias[:, :]), 16)
        cx.on("sync", prolog_dma)
        def prolog_dma2(y):
            src = atgw.rearrange("(ic p) c -> p ic c", p=128)
            wld.inc(y.dma_start(out=atgw_bf[:, :].rearrange("p (ic c) -> p ic c", c=36),
                                in_=src), 16)
            wld.inc(y.dma_start(out=bt_bf[:, :], in_=btc[:, :]), 16)
        cx.on("gpsimd", prolog_dma2)
        pro_ld = ld.n
        pro_wld = wld.n

        def prolog_mm(t, pro_ld=pro_ld):
            t.wait_ge(ld.h, pro_ld)
            i = t.matmul(psZ[:, 0:4], onesr[:, :], ebrow[:, :], start=True, stop=True)
            mm.inc(i, 1)
        cx.on("tensor", prolog_mm)
        pro_mm = mm.n

        def prolog_ev(v, pro_mm=pro_mm):
            v.wait_ge(mm.h, pro_mm)
            i = v.tensor_copy(ebb[:, :], psZ[:, 0:4])
            ev.inc(i, 1)
        cx.on("vector", prolog_ev)
        pro_ev = ev.n
        cx.use("psZ", ev, pro_ev)

        for g in range(NG):
            emit_ln_group(cx, g, xin, sb)
            xln_ready = lne.n

            for tt in range(NTT):
                def zmm(t, tt=tt, xln_ready=xln_ready, pro_wld=pro_wld):
                    t.wait_ge(lne.h, xln_ready)
                    t.wait_ge(wld.h, pro_wld)
                    cx.war(t, "psZ")
                    for ic in range(NIC):
                        i = t.matmul(psZ[:, 0:36],
                                     xlnT[:, ic * G512 + tt * 128: ic * G512 + (tt + 1) * 128],
                                     atgw_bf[:, ic * 36:(ic + 1) * 36],
                                     start=(ic == 0), stop=(ic == NIC - 1))
                    mm.inc(i, 1)
                cx.on("tensor", zmm)
                z_done = mm.n

                def zev(v, tt=tt, z_done=z_done):
                    chv = sems["chv"]
                    def ch(i):
                        chv.inc(i, 1); v.wait_ge(chv.h, chv.n)
                    v.wait_ge(mm.h, z_done)
                    v.tensor_copy(z_sb[:, :], psZ[:, 0:32])
                    ch(v.tensor_tensor(lg[:, 0:4], psZ[:, 32:36], ebb[:, :], op=ALU.add))
                    ch(v.tensor_reduce(lg[:, 4:5], lg[:, 0:4], axis=AX.X, op=ALU.max))
                    ch(v.tensor_scalar(oh[:, 0:4], lg[:, 0:4], lg[:, 4:5], None, op0=ALU.is_ge))
                    ch(v.tensor_scalar(oh[:, 4:5], oh[:, 0:1], -1.0, 1.0, op0=ALU.mult, op1=ALU.add))
                    for e in range(1, 4):
                        ch(v.tensor_tensor(oh[:, e:e + 1], oh[:, e:e + 1], oh[:, 4:5], op=ALU.mult))
                        if e < 3:
                            ch(v.tensor_scalar(oh[:, 5:6], oh[:, e:e + 1], -1.0, 1.0, op0=ALU.mult, op1=ALU.add))
                            ch(v.tensor_tensor(oh[:, 4:5], oh[:, 4:5], oh[:, 5:6], op=ALU.mult))
                    for e in range(4):
                        i = v.tensor_scalar(zm_bf[:, e * 8:(e + 1) * 8],
                                            z_sb[:, e * 8:(e + 1) * 8],
                                            oh[:, e:e + 1], None, op0=ALU.mult)
                    ev.inc(i, 1)
                cx.on("vector", zev)
                cx.use("psZ", ev, ev.n)
                zm_done = ev.n

                def ztr(t, tt=tt, zm_done=zm_done):
                    t.wait_ge(ev.h, zm_done)
                    cx.war(t, "ptrA")
                    for e in range(4):
                        i = t.transpose(ptrA[0:8, e * 128:(e + 1) * 128],
                                        zm_bf[:, e * 8:(e + 1) * 8], ident[:, :])
                    lnt.inc(i, 1)
                cx.on("tensor", ztr)
                ztr_done = lnt.n

                def ztev(v, tt=tt, ztr_done=ztr_done):
                    v.wait_ge(lnt.h, ztr_done)
                    i = v.tensor_copy(zmT[:, tt * 512:(tt + 1) * 512],
                                      ptrA[0:8, 0:512])
                    lne.inc(i, 1)
                cx.on("vector", ztev)
                cx.use("ptrA", lne, lne.n)
            zmT_ready = lne.n

            for ob in range(NOB):
                par = ob % 2
                wbf, wbname = ((wbf0, "wb0"), (wbf1, "wb1"))[par]

                def wload(y, ob=ob, wbf=wbf, wbname=wbname):
                    cx.war(y, wbname)
                    src = wupT.rearrange("(ic p) o -> p ic o", p=128)
                    for quarter in range(4):
                        dst = wbf[:, quarter * 8 * OB:(quarter + 1) * 8 * OB]
                        dst = dst.rearrange("p (ic o) -> p ic o", o=OB)
                        i = y.dma_start(
                            out=dst,
                            in_=src[:, quarter * 8:(quarter + 1) * 8,
                                    ob * OB:(ob + 1) * OB])
                        wld.inc(i, 16)
                cx.on("gpsimd", wload)
                wld_done = wld.n

                for tt in range(NTT):
                    ps, psname = ((psA, "psA"), (psB, "psB"))[tt % 2]
                    def mmf(t, ob=ob, tt=tt, ps=ps, psname=psname, wbf=wbf,
                            wld_done=wld_done, xln_ready=xln_ready,
                            zmT_ready=zmT_ready):
                        t.wait_ge(wld.h, wld_done)
                        t.wait_ge(lne.h, max(xln_ready, zmT_ready))
                        cx.war(t, psname)
                        for ic in range(NIC):
                            t.matmul(ps[:, 0:OB],
                                     xlnT[:, ic * G512 + tt * 128: ic * G512 + (tt + 1) * 128],
                                     wbf[:, ic * OB:(ic + 1) * OB],
                                     start=(ic == 0), stop=False)
                        for e in range(4):
                            i = t.matmul(ps[:, 0:OB],
                                         zmT[0:8, tt * 512 + e * 128: tt * 512 + (e + 1) * 128],
                                         bt_bf[0:8, e * D + ob * OB: e * D + (ob + 1) * OB],
                                         start=False, stop=(e == 3))
                        mm.inc(i, 1)
                    cx.on("tensor", mmf)
                    mm_done = mm.n
                    if tt == NTT - 1:
                        cx.use(wbname, mm, mm.n)

                    slot = (ob * NTT + tt) % 2
                    def evf(v, ob=ob, tt=tt, ps=ps, mm_done=mm_done, slot=slot):
                        v.wait_ge(mm.h, mm_done)
                        cx.war(v, f"outsb{slot}")
                        i = v.tensor_tensor(
                            outsb[:, slot * OB:(slot + 1) * OB], ps[:, 0:OB],
                            x_g[:, tt * D + ob * OB: tt * D + (ob + 1) * OB],
                            op=ALU.add)
                        ev.inc(i, 1)
                    cx.on("vector", evf)
                    cx.use(psname, ev, ev.n)
                    ev_done = ev.n

                    def stf(y, ob=ob, tt=tt, ev_done=ev_done, g=g, slot=slot):
                        y.wait_ge(ev.h, ev_done)
                        t0 = g * G512 + tt * 128
                        i = y.dma_start(out=xout[t0:t0 + 128, ob * OB:(ob + 1) * OB],
                                        in_=outsb[:, slot * OB:(slot + 1) * OB])
                        st.inc(i, 16)
                    cx.on("sync", stf)
                    cx.use(f"outsb{slot}", st, st.n)

            cx.use("x_g", ev, ev.n)
            cx.use("xlnT", mm, mm.n)
            cx.use("wb0", mm, mm.n)
            cx.use("wb1", mm, mm.n)

        final_st = st.n
        def fin(y, final_st=final_st):
            y.wait_ge(st.h, final_st)
        cx.on("sync", fin)
        if fz is not None:
            fz["barrier"] = (st.h, st.n)

    return nc


def prep_l1_ffn_inputs(inputs, consts, xcur, pre=""):
    wupT = np.ascontiguousarray(np.asarray(inputs["L1_ffn_W_up"], dtype=np.float32).T)
    A = np.asarray(inputs["L1_ffn_lora_A"], dtype=np.float32)
    gw = np.asarray(inputs["L1_ffn_gate_w"], dtype=np.float32)
    atgw = np.concatenate([A[e].T for e in range(4)] + [gw.T], axis=1)
    atgw = np.ascontiguousarray(atgw)
    B = np.asarray(inputs["L1_ffn_lora_B"], dtype=np.float32)
    btc = np.ascontiguousarray(
        np.concatenate([B[e].T for e in range(4)], axis=1))
    ebias = np.ascontiguousarray(
        np.asarray(inputs["L1_ffn_expert_bias"], dtype=np.float32).reshape(1, 4))
    maps = []
    for core in range(8):
        maps.append({
            "xin": np.ascontiguousarray(xcur[core]),
            "wupT": wupT, "atgw": atgw, "btc": btc, "ebias": ebias,
            "ident_bf": consts["ident_bf"],
            "ones_row_f": consts["ones_row_f"],
        })
    return maps



def prep_l0_ffn_inputs(inputs, consts, xcur, pre=""):
    wT = np.ascontiguousarray(inputs["L0_ffn_weight"].T)
    scflat = np.ascontiguousarray(
        np.asarray(inputs["L0_ffn_weight_scale"], dtype=np.float32).reshape(1, 1024))
    maps = []
    for core in range(8):
        maps.append({
            "xin": np.ascontiguousarray(xcur[core]),
            "wT": wT,
            "scflat": scflat,
            "ident_bf": consts["ident_bf"],
            "ones_row_f": consts["ones_row_f"],
        })
    return maps


def build_fused():
    nc = bass.Bass("TRN2", target_bir_lowering=False)
    x0 = nc.dram_tensor("x0", [S, D], F32, kind="ExternalInput").ap()
    xout = nc.dram_tensor("xout", [S, D], F32, kind="ExternalOutput").ap()
    xattT = nc.dram_tensor("t_xattT", [D, S], BF16).ap()
    x1 = nc.dram_tensor("t_x1", [S, D], F32).ap()
    x2 = nc.dram_tensor("t_x2", [S, D], F32).ap()
    x3 = nc.dram_tensor("t_x3", [S, D], F32).ap()
    es = contextlib.ExitStack()
    with es:
        fz = {"nc": nc, "es": es, "barrier": None,
              "prefix": "a_", "io": {"xin": x0, "xattT": xattT}}
        build_l0_attn(fz)
        fz["prefix"] = "b_"; fz["io"] = {"xres": x0, "xT": xattT, "xout": x1}
        build_proj_residual(fz)
        fz["prefix"] = "f_"; fz["io"] = {"xin": x1, "xout": x2}
        build_l0_ffn(fz)
        fz["prefix"] = "c_"; fz["io"] = {"xin": x2, "xout": x3}
        build_l1_attn(fz)
        fz["prefix"] = "d_"; fz["io"] = {"xin": x3, "xout": xout}
        build_l1_ffn(fz)
    return nc


def kernel(**inputs):
    consts = host_consts()
    x = np.ascontiguousarray(np.asarray(inputs["L0_attn_hidden_states"], dtype=np.float32))

    nc = build_fused()
    woT = np.ascontiguousarray(np.asarray(inputs["L0_attn_w_o"], dtype=np.float32).T)
    shared = {}
    for pre, prep in (("a_", prep_l0_attn_inputs), ("f_", prep_l0_ffn_inputs),
                      ("c_", prep_l1_attn_inputs), ("d_", prep_l1_ffn_inputs)):
        m = prep(inputs, consts, x)[0]
        shared.update({pre + k: v for k, v in m.items() if k != "xin"})
    shared["b_wT"] = woT
    maps = [dict(shared, x0=np.ascontiguousarray(x[core])) for core in range(8)]
    res = run_bass_kernel_spmd(nc, maps, core_ids=list(range(8)))
    return np.stack([res.results[i]["xout"] for i in range(8)], axis=0)


# version 06
# speedup vs baseline: 1.0093x; 1.0082x over previous
import numpy as np
import ml_dtypes
import concourse.bass as bass
import concourse.mybir as mybir
from concourse.bass_utils import run_bass_kernel_spmd

F32 = mybir.dt.float32
BF16 = mybir.dt.bfloat16
AF = mybir.ActivationFunctionType
ALU = mybir.AluOpType
AX = mybir.AxisListType
BF = ml_dtypes.bfloat16

S = 2048
D = 4096
G512 = 512
NG = S // G512
NTT = G512 // 128
NIC = D // 128
EPS = 1e-5
NEGBIG = -1e30


class Sem:
    def __init__(self, h):
        self.h = h
        self.n = 0
    def inc(self, instr, k):
        instr.then_inc(self.h, k)
        self.n += k
        return self.n

class Ctx:
    def __init__(self, nc, block, sems):
        self.nc = nc
        self.block = block
        self.sems = sems
        self.last_use = {}

    def on(self, eng, fn):
        getattr(self.block, eng)(fn)

    def use(self, res, sem, count):
        self.last_use[res] = [(sem, count)]

    def use_add(self, res, sem, count):
        self.last_use.setdefault(res, []).append((sem, count))

    def war(self, eng_handle, res):
        for sem, count in self.last_use.get(res, ()):
            eng_handle.wait_ge(sem.h, count)




import contextlib

def alloc(nc, es, sbufs, psums, semnames, sem_es=None, prefix=""):
    tens = {}
    for name, shape, dt in sbufs:
        tens[name] = es.enter_context(nc.sbuf_tensor(prefix + name, shape, dt))
    for name, shape, dt in psums:
        tens[name] = es.enter_context(nc.psum_tensor(prefix + name, shape, dt))
    ses = sem_es if sem_es is not None else es
    sems = {name: Sem(ses.enter_context(nc.semaphore(prefix + name)))
            for name in semnames}
    block = es.enter_context(nc.Block())
    return tens, sems, block


def fz_begin(fz):
    if fz is None:
        return bass.Bass("TRN2", target_bir_lowering=False), "", {}
    return fz["nc"], fz["prefix"], fz["io"]


def fz_barrier(cx, fz):
    if fz and fz.get("barrier"):
        bsem, bcnt = fz["barrier"]
        for eng in ("tensor", "vector", "scalar", "sync", "gpsimd"):
            def _bar(e, bsem=bsem, bcnt=bcnt):
                e.wait_ge(bsem, bcnt)
            cx.on(eng, _bar)



def host_consts():
    c = {}
    c["ident_bf"] = np.eye(128, dtype=BF)
    c["ones_col_f"] = np.ones((128, 1), dtype=np.float32)
    c["ones_row_f"] = np.ones((1, 128), dtype=np.float32)
    c["ones_col_bf"] = np.ones((128, 1), dtype=BF)
    c["ones_row_bf"] = np.ones((1, 128), dtype=BF)

    ROT = 64
    t = np.arange(S, dtype=np.float64)
    inv1 = 1.0 / (10000.0 ** (np.arange(0, ROT, 2, dtype=np.float64) / ROT))
    ang1 = t[:, None] * inv1[None, :]
    C1 = np.ones((128, S), dtype=np.float32)
    S1 = np.zeros((128, S), dtype=np.float32)
    cos1 = np.cos(ang1).astype(np.float32)
    sin1 = np.sin(ang1).astype(np.float32)
    for d in range(ROT):
        i = d // 2
        C1[d, :] = cos1[:, i]
        S1[d, :] = (-sin1[:, i] if d % 2 == 0 else sin1[:, i])
    c["rope1_C"] = C1
    c["rope1_S"] = S1
    P1 = np.zeros((128, 64), dtype=BF)
    for d in range(ROT):
        P1[d + 1 if d % 2 == 0 else d - 1, d] = 1
    c["rope1_P"] = P1

    inv0 = 1.0 / (10000.0 ** (np.arange(0, 128, 2, dtype=np.float64) / 128))
    ang0 = t[:, None] * inv0[None, :]
    C0 = np.zeros((128, S), dtype=np.float32)
    S0 = np.zeros((128, S), dtype=np.float32)
    cos0 = np.cos(ang0).astype(np.float32)
    sin0 = np.sin(ang0).astype(np.float32)
    for d in range(128):
        i = d // 2
        C0[d, :] = cos0[:, i]
        S0[d, :] = (-sin0[:, i] if d % 2 == 0 else sin0[:, i])
    c["rope0_C"] = C0
    c["rope0_S"] = S0
    P0 = np.zeros((128, 128), dtype=BF)
    for d in range(128):
        P0[d ^ 1, d] = 1
    c["rope0_P"] = P0

    dm = np.zeros((4, 128, 512), dtype=BF)
    for j, off in enumerate((0, 128, 256, 384)):
        k = np.arange(128)[:, None] + off
        q = np.arange(512)[None, :]
        dm[j] = (k <= q).astype(BF)
    c["l1_diag01"] = dm

    qa = np.arange(S)[:, None]
    cc = np.arange(128)[None, :]
    c["sel_vis_add"] = np.where(16 * (cc + 1) <= qa, 0.0, NEGBIG).astype(np.float32)
    c["own_oh"] = (cc == (qa // 16)).astype(np.float32)

    cv = np.zeros((NG, 128, G512), dtype=BF)
    for g in range(NG):
        qg = g * G512 + np.arange(G512)[None, :]
        cv[g] = (16 * (np.arange(128)[:, None] + 1) <= qg).astype(BF)
    c["cmp_vis01"] = cv

    fd = np.ones((4, 128, 512), dtype=BF)
    for j, off in enumerate((0, 128, 256, 384)):
        ka = np.arange(128)[:, None] + off
        qg = np.arange(512)[None, :]
        same = (ka // 16) == (qg // 16)
        future = same & (ka > qg)
        fd[j] = np.where(future, 0.0, 1.0).astype(BF)
    c["fine_diag01"] = fd

    sb = np.zeros((5, 128, 512), dtype=BF)
    for j, off in enumerate((-1, 0, 1, 2, 3)):
        ka = np.arange(128)[:, None] + 128 * off
        qg = np.arange(512)[None, :]
        sb[j] = ((qg - ka >= 0) & (qg - ka <= 64)).astype(BF)
    c["band01"] = sb

    BIGC = 1e30
    E = np.zeros((16, 128, 128), dtype=BF)
    for kb in range(16):
        for k in range(128):
            E[kb, 8 * kb + k // 16, k] = BIGC
    c["selE"] = E
    return c



def emit_ln_group(cx, g, xin_ap, sb, xg_tiled=True):
    sems = cx.sems
    ld, lnv, lns, lnt, lne = (sems[k] for k in ("ld", "lnv", "lns", "lnt", "lne"))
    chv, chs = sems["chv"], sems["chs"]
    x_g = sb["x_g"]; xlnT = sb["xlnT"]; bs = sb["bs"]
    st = sb["stats"]; xln_tok = sb["xln_tok"]
    ptrA = sb["ptrA"]; ptrB = sb["ptrB"]; ident = sb["ident"]

    for tt in range(NTT):
        t0 = g * G512 + tt * 128
        xo = tt * D if xg_tiled else 0

        def ldf(y, tt=tt, t0=t0, g=g, xo=xo):
            if (tt == 0) or (not xg_tiled):
                cx.war(y, "x_g")
            i = y.dma_start(out=x_g[:, xo:xo + D], in_=xin_ap[t0:t0 + 128, :])
            ld.inc(i, 16)
        cx.on("sync", ldf)
        ld_done = ld.n

        def vstats(v, tt=tt, ld_done=ld_done, xo=xo):
            v.wait_ge(ld.h, ld_done)
            cx.war(v, "stats")
            for i8 in range(D // 512):
                i = v.bn_stats(bs[:, i8 * 6:(i8 + 1) * 6],
                               x_g[:, xo + i8 * 512: xo + (i8 + 1) * 512])
            chv.inc(i, 1); v.wait_ge(chv.h, chv.n)
            i = v.bn_aggr(st[:, 0:2], bs[:, :])
            lnv.inc(i, 1)
        cx.on("vector", vstats)
        v_done = lnv.n

        def sstats2(s, v_done=v_done):
            s.wait_ge(lnv.h, v_done)
            i = s.activation(st[:, 6:7], st[:, 1:2], AF.Ln, bias=sb["eps_col"][:, 0:1])
            chs.inc(i, 1); s.wait_ge(chs.h, chs.n)
            i = s.activation(st[:, 7:8], st[:, 6:7], AF.Exp, scale=-0.5)
            lns.inc(i, 1)
        cx.on("scalar", sstats2)
        s2_done = lns.n
        cx.use("stats", lns, s2_done)

        def vnorm(v, tt=tt, s2_done=s2_done, xo=xo):
            v.wait_ge(lns.h, s2_done)
            cx.war(v, "xln_tok")
            i = v.tensor_tensor(st[:, 8:9], st[:, 0:1], st[:, 7:8], op=ALU.mult)
            chv.inc(i, 1); v.wait_ge(chv.h, chv.n)
            i = v.tensor_scalar(st[:, 9:10], st[:, 8:9], -1.0, None, op0=ALU.mult)
            chv.inc(i, 1); v.wait_ge(chv.h, chv.n)
            i = v.tensor_scalar(xln_tok[:, :], x_g[:, xo:xo + D],
                                st[:, 7:8], st[:, 9:10], op0=ALU.mult, op1=ALU.add)
            lnv.inc(i, 1)
        cx.on("vector", vnorm)
        n_done = lnv.n
        if not xg_tiled:
            cx.use("x_g", lnv, lnv.n)

        same_bank = ptrA is ptrB
        for half in range(4):
            bank, bankname = ((ptrA, "ptrA"), (ptrB, "ptrB"))[0 if same_bank else half % 2]
            def ttr(t, half=half, bank=bank, bankname=bankname, n_done=n_done):
                t.wait_ge(lnv.h, n_done)
                cx.war(t, bankname)
                for j in range(8):
                    ic = half * 8 + j
                    i = t.transpose(bank[:, j * 128:(j + 1) * 128],
                                    xln_tok[:, ic * 128:(ic + 1) * 128], ident[:, :])
                lnt.inc(i, 1)
            cx.on("tensor", ttr)
            t_done = lnt.n
            if half == 3:
                cx.use("xln_tok", lnt, t_done)

            def vev(v, tt=tt, half=half, bank=bank, t_done=t_done):
                v.wait_ge(lnt.h, t_done)
                cx.war(v, "xlnT")
                for j in range(8):
                    ic = half * 8 + j
                    i = v.tensor_copy(
                        xlnT[:, ic * G512 + tt * 128: ic * G512 + (tt + 1) * 128],
                        bank[:, j * 128:(j + 1) * 128])
                lne.inc(i, 1)
            cx.on("vector", vev)
            cx.use(bankname, lne, lne.n)
    return lne.n



def build_l0_ffn(fz=None):
    nc, pre, io = fz_begin(fz)
    xin = io.get("xin") if fz else nc.dram_tensor("xin", [S, D], F32, kind="ExternalInput").ap()
    wT = nc.dram_tensor(pre + "wT", [D, D], F32, kind="ExternalInput").ap()
    scflat = nc.dram_tensor(pre + "scflat", [1, 1024], F32, kind="ExternalInput").ap()
    ident_in = nc.dram_tensor(pre + "ident_bf", [128, 128], BF16, kind="ExternalInput").ap()
    onesr_in = nc.dram_tensor(pre + "ones_row_f", [1, 128], F32, kind="ExternalInput").ap()
    xout = io.get("xout") if fz else nc.dram_tensor("xout", [S, D], F32, kind="ExternalOutput").ap()

    OB = 128
    NOB = D // OB

    es = contextlib.ExitStack()
    with es:
        tens, sems, block = alloc(
            nc, es,
            sbufs=[("x_g", [128, NTT * D], F32), ("xlnT", [128, NIC * G512], BF16),
                   ("bs", [128, 48], F32), ("stats", [128, 16], F32),
                   ("xln_tok", [128, D], BF16), ("ident", [128, 128], BF16),
                   ("onesr", [1, 128], F32), ("scrow", [1, 1024], F32),
                   ("SC", [128, 1024], F32),
                   ("wstage0", [128, NIC * OB], F32), ("wstage1", [128, NIC * OB], F32),
                   ("wbf0", [128, NIC * OB], BF16), ("wbf1", [128, NIC * OB], BF16),
                   ("outsb", [128, 2 * OB], F32), ("eps_col", [128, 1], F32)],
            psums=[("psA", [128, 512], F32), ("psB", [128, 512], F32),
                   ("psC", [128, 512], F32), ("psD", [128, 512], F32),
                   ("ptrA", [128, 1024], BF16), ("ptrB", [128, 1024], BF16)],
            semnames=["ld", "lnv", "lns", "lnt", "lne", "wld", "wdq", "mm", "ev", "st", "chv", "chs"],
            sem_es=(fz["es"] if fz else None), prefix=pre)
        (x_g, xlnT, stats, xln_tok, ident, onesr, scrow, SC, wstage0,
         wstage1, wbf0, wbf1, outsb, psA, psB, psC, ptrA, ptrB) = (
            tens[k] for k in ("x_g", "xlnT", "stats", "xln_tok", "ident",
                              "onesr", "scrow", "SC", "wstage0", "wstage1",
                              "wbf0", "wbf1", "outsb", "psA", "psB", "psC",
                              "ptrA", "ptrB"))
        cx = Ctx(nc, block, sems)
        fz_barrier(cx, fz)
        ld, lnv, lns, lnt, lne, wld, wdq, mm, ev, st = (
            sems[k] for k in ("ld", "lnv", "lns", "lnt", "lne", "wld", "wdq",
                              "mm", "ev", "st"))

        sb = {"x_g": x_g, "xlnT": xlnT, "bs": tens["bs"], "stats": stats,
              "xln_tok": xln_tok, "ident": ident, "ptrA": ptrA, "ptrB": ptrB,
              "eps_col": tens["eps_col"]}
        def init_eps(v):
            v.memset(tens["eps_col"][:, :], EPS)
        cx.on("vector", init_eps)

        def prolog_dma(y):
            ld.inc(y.dma_start(out=ident[:, :], in_=ident_in[:, :]), 16)
            ld.inc(y.dma_start(out=onesr[:, :], in_=onesr_in[:, :]), 16)
            ld.inc(y.dma_start(out=scrow[:, :], in_=scflat[:, :]), 16)
        cx.on("sync", prolog_dma)
        pro_ld = ld.n

        def prolog_mm(t, pro_ld=pro_ld):
            t.wait_ge(ld.h, pro_ld)
            t.matmul(psC[:, 0:512], onesr[:, :], scrow[:, 0:512], start=True, stop=True)
            i = t.matmul(psB[:, 0:512], onesr[:, :], scrow[:, 512:1024], start=True, stop=True)
            mm.inc(i, 1)
        cx.on("tensor", prolog_mm)
        pro_mm = mm.n

        def prolog_ev(v, pro_mm=pro_mm):
            v.wait_ge(mm.h, pro_mm)
            v.tensor_copy(SC[:, 0:512], psC[:, 0:512])
            i = v.tensor_copy(SC[:, 512:1024], psB[:, 0:512])
            ev.inc(i, 1)
        cx.on("vector", prolog_ev)
        pro_ev = ev.n
        cx.use("psB", ev, pro_ev)
        cx.use("psC", ev, pro_ev)

        for g in range(NG):
            emit_ln_group(cx, g, xin, sb)
            xln_ready = lne.n

            for ob in range(NOB):
                par = ob % 2
                wstage, wsname = ((wstage0, "ws0"), (wstage1, "ws1"))[par]
                wbf, wbname = ((wbf0, "wb0"), (wbf1, "wb1"))[par]

                def wload(y, ob=ob, wstage=wstage, wsname=wsname):
                    cx.war(y, wsname)
                    src = wT.rearrange("(ic p) o -> p ic o", p=128)
                    for quarter in range(4):
                        dst = wstage[:, quarter * 8 * OB:(quarter + 1) * 8 * OB]
                        dst = dst.rearrange("p (ic o) -> p ic o", o=OB)
                        i = y.dma_start(
                            out=dst,
                            in_=src[:, quarter * 8:(quarter + 1) * 8,
                                    ob * OB:(ob + 1) * OB])
                        wld.inc(i, 16)
                cx.on("sync", wload)
                wld_done = wld.n

                def wdeq(v, ob=ob, wstage=wstage, wbf=wbf, wbname=wbname,
                         wld_done=wld_done, pro_ev=pro_ev):
                    v.wait_ge(wld.h, wld_done)
                    v.wait_ge(ev.h, pro_ev)
                    cx.war(v, wbname)
                    oc = ob
                    for ic in range(NIC):
                        i = v.tensor_scalar(
                            wbf[:, ic * OB:(ic + 1) * OB],
                            wstage[:, ic * OB:(ic + 1) * OB],
                            SC[:, oc * 32 + ic: oc * 32 + ic + 1], None,
                            op0=ALU.mult)
                    wdq.inc(i, 1)
                cx.on("vector", wdeq)
                cx.use(wsname, wdq, wdq.n)
                wdq_done = wdq.n

                for tt in range(NTT):
                    ps, psname = ((psA, "psA"), (psB, "psB"), (psC, "psC"),
                                  (tens["psD"], "psD"))[tt % 4]
                    def mmf(t, ob=ob, tt=tt, ps=ps, psname=psname,
                            wbf=wbf, wdq_done=wdq_done, xln_ready=xln_ready):
                        t.wait_ge(wdq.h, wdq_done)
                        t.wait_ge(lne.h, xln_ready)
                        cx.war(t, psname)
                        for ic in range(NIC):
                            i = t.matmul(
                                ps[:, 0:OB],
                                xlnT[:, ic * G512 + tt * 128: ic * G512 + (tt + 1) * 128],
                                wbf[:, ic * OB:(ic + 1) * OB],
                                start=(ic == 0), stop=(ic == NIC - 1))
                        mm.inc(i, 1)
                    cx.on("tensor", mmf)
                    mm_done = mm.n
                    if tt == NTT - 1:
                        cx.use(wbname, mm, mm.n)

                    slot = (ob * NTT + tt) % 2
                    def evf(v, ob=ob, tt=tt, ps=ps, mm_done=mm_done, slot=slot):
                        v.wait_ge(mm.h, mm_done)
                        cx.war(v, f"outsb{slot}")
                        i = v.tensor_tensor(
                            outsb[:, slot * OB:(slot + 1) * OB], ps[:, 0:OB],
                            x_g[:, tt * D + ob * OB: tt * D + (ob + 1) * OB],
                            op=ALU.add)
                        ev.inc(i, 1)
                    cx.on("vector", evf)
                    cx.use(psname, ev, ev.n)
                    ev_done = ev.n

                    def stf(y, ob=ob, tt=tt, ev_done=ev_done, g=g, slot=slot):
                        y.wait_ge(ev.h, ev_done)
                        t0 = g * G512 + tt * 128
                        i = y.dma_start(out=xout[t0:t0 + 128, ob * OB:(ob + 1) * OB],
                                        in_=outsb[:, slot * OB:(slot + 1) * OB])
                        st.inc(i, 16)
                    cx.on("gpsimd", stf)
                    cx.use(f"outsb{slot}", st, st.n)

            cx.use("x_g", ev, ev.n)
            cx.use("xlnT", mm, mm.n)
            cx.use("wb0", mm, mm.n)
            cx.use("wb1", mm, mm.n)

        final_st = st.n
        def fin(y, final_st=final_st):
            y.wait_ge(st.h, final_st)
        cx.on("gpsimd", fin)
        if fz is not None:
            fz["barrier"] = (st.h, st.n)

    return nc



DBG = {"ng": NG, "nkvh": 8, "ngh": 4, "fine": True, "slide": True,
       "cmp": True, "sel": True, "gates": True, "mlp": True,
       "kvproj": 2, "gconst": True, "krope": True, "vtr": True, "cin": True,
       "dump": False}

def build_l0_attn(fz=None):
    nc, pre, io = fz_begin(fz)
    xin = io.get("xin") if fz else nc.dram_tensor("xin", [S, D], F32, kind="ExternalInput").ap()
    qkvwT = nc.dram_tensor(pre + "qkvwT", [D, 6144], F32, kind="ExternalInput").ap()
    wgT_d = nc.dram_tensor(pre + "wgT", [D, 96], F32, kind="ExternalInput").ap()
    bg_d = nc.dram_tensor(pre + "bgrow", [1, 96], F32, kind="ExternalInput").ap()
    kw1T_d = nc.dram_tensor(pre + "kw1T", [2048, 2048], F32, kind="ExternalInput").ap()
    vw1T_d = nc.dram_tensor(pre + "vw1T", [2048, 2048], F32, kind="ExternalInput").ap()
    kw2T_d = nc.dram_tensor(pre + "kw2T", [2048, 128], F32, kind="ExternalInput").ap()
    vw2T_d = nc.dram_tensor(pre + "vw2T", [2048, 128], F32, kind="ExternalInput").ap()
    kb1r_d = nc.dram_tensor(pre + "kb1r", [128, 16], F32, kind="ExternalInput").ap()
    vb1r_d = nc.dram_tensor(pre + "vb1r", [128, 16], F32, kind="ExternalInput").ap()
    kb2c_d = nc.dram_tensor(pre + "kb2c", [128, 1], F32, kind="ExternalInput").ap()
    vb2c_d = nc.dram_tensor(pre + "vb2c", [128, 1], F32, kind="ExternalInput").ap()
    kpT_d = nc.dram_tensor(pre + "kpT", [8, 128, 512], F32, kind="ExternalInput").ap()
    vpT_d = nc.dram_tensor(pre + "vpT", [8, 128, 512], F32, kind="ExternalInput").ap()
    memkT_d = nc.dram_tensor(pre + "memkT", [128, 8], BF16, kind="ExternalInput").ap()
    memv_d = nc.dram_tensor(pre + "memv", [1, 8 * 128], BF16, kind="ExternalInput").ap()
    ident_in = nc.dram_tensor(pre + "ident_bf", [128, 128], BF16, kind="ExternalInput").ap()
    onescb_in = nc.dram_tensor(pre + "ones_col_bf", [128, 1], BF16, kind="ExternalInput").ap()
    onesrb_in = nc.dram_tensor(pre + "ones_row_bf", [1, 128], BF16, kind="ExternalInput").ap()
    ropeC_in = nc.dram_tensor(pre + "ropeC0", [128, S], F32, kind="ExternalInput").ap()
    ropeS_in = nc.dram_tensor(pre + "ropeS0", [128, S], F32, kind="ExternalInput").ap()
    ropeP_in = nc.dram_tensor(pre + "ropeP0", [128, 128], BF16, kind="ExternalInput").ap()
    selvis_d = nc.dram_tensor(pre + "selvis", [S, 128], F32, kind="ExternalInput").ap()
    ownm1_d = nc.dram_tensor(pre + "ownm1", [S, 128], F32, kind="ExternalInput").ap()
    cmpvis_d = nc.dram_tensor(pre + "cmpvis", [NG, 128, 512], BF16, kind="ExternalInput").ap()
    fdiag_d = nc.dram_tensor(pre + "fdiag", [4, 128, 512], BF16, kind="ExternalInput").ap()
    band_d = nc.dram_tensor(pre + "band", [5, 128, 512], BF16, kind="ExternalInput").ap()
    selE_d = nc.dram_tensor(pre + "selE", [16, 128, 128], BF16, kind="ExternalInput").ap()
    xattT_out = io.get("xattT") if fz else nc.dram_tensor("xattT", [D, S], BF16, kind="ExternalOutput").ap()
    if DBG["dump"]:
        dmp = {nm: nc.dram_tensor(pre + "dmp_" + nm, shp, dt, kind="ExternalOutput").ap()
               for nm, shp, dt in (
                   ("rkT", [128, 8 * S], BF16), ("vtok", [128, 8 * S], BF16),
                   ("ckT", [128, 8 * 128], BF16), ("cvT", [128, 8 * 128], BF16),
                   ("selT", [128, 512], BF16), ("gsig", [128, 4 * 96], F32),
                   ("q4", [128, 4 * 512], BF16), ("Zsb", [128, 12], F32),
                   ("gz", [128, 12], F32), ("xlnT", [128, NIC * G512], BF16))}

    KVH = 8
    GH = 4
    SC0 = 1.0 / np.sqrt(128.0)
    SCSEL = SC0 / 4.0

    es = contextlib.ExitStack()
    with es:
        tens, sems, block = alloc(
            nc, es,
            sbufs=[
                ("x_g", [128, D], F32), ("xlnT", [128, NIC * G512], BF16),
                ("bs", [128, 48], F32), ("stats", [128, 16], F32),
                ("xln_tok", [128, D], BF16), ("ident", [128, 128], BF16),
                ("eps_col", [128, 1], F32), ("n1000", [128, 1], F32),
                ("onescb", [128, 1], BF16), ("onesrb", [1, 128], BF16),
                ("ropeCS", [128, 2 * G512], F32),
                ("ropeP", [128, 128], BF16),
                ("rkT", [128, KVH * S], BF16),
                ("vtok", [128, KVH * S], BF16),
                ("q4", [128, GH * G512], BF16),
                ("rqh", [128, G512], BF16),
                ("ktmp", [128, G512], BF16), ("vtmp", [128, G512], BF16),
                ("kvpT", [128, 2 * G512], BF16),
                ("ckinr", [128, 16 * 256], BF16),
                ("cvinr", [128, 16 * 256], BF16),
                ("h1a", [128, 16 * 256], BF16),
                ("ckT", [128, KVH * 128], BF16), ("cvT", [128, KVH * 128], BF16),
                ("cvbm", [128, KVH * 128], BF16),
                ("selT", [128, G512], BF16),
                ("gsig", [128, NTT * 96], F32),
                ("selvis_g", [128, NTT * 128], BF16),
                ("ownm1_g", [128, NTT * 128], BF16),
                ("cmpvis_g", [128, G512], BF16),
                ("fdiag_s", [128, 4 * 512], BF16),
                ("band_s", [128, 5 * 512], BF16),
                ("selE_s", [128, 16 * 128], BF16),
                ("memkT_s", [128, 8], BF16), ("memv_s", [1, 8 * 128], BF16),
                ("kb1r_s", [128, 16], F32), ("vb1r_s", [128, 16], F32),
                ("kb2c_s", [128, 1], F32), ("vb2c_s", [128, 1], F32),
                ("qw0", [128, NIC * 128], BF16),
                ("mw0", [128, 16 * 128], BF16),
                ("bg_bf", [1, 96], BF16),
                ("exp0", [128, 512], BF16), ("exp1", [128, 512], BF16),
                ("exp2", [128, 512], BF16),
                ("rt1", [128, 512], F32), ("rt2", [128, 512], F32),
                ("psel", [128, 128], F32), ("pscr", [128, 128], F32),
                ("m8", [128, 8], F32), ("selm", [128, 128], F32),
                ("Zsb", [128, 12], F32), ("Zinv", [128, 12], F32),
                ("gz", [128, 12], F32),
                ("xatt_tok", [128, 128], BF16),
                ("xattT_kvh", [128, G512], BF16),
            ],
            psums=[("psA", [128, 512], F32), ("psB", [128, 512], F32),
                   ("psC", [128, 512], F32), ("psF", [128, 512], F32),
                   ("psS", [128, 512], F32), ("psU", [128, 512], F32),
                   ("psZ", [128, 512], F32),
                   ("ptrA", [128, 1024], BF16)],
            semnames=["ld", "pld", "lnv", "lns", "lnt", "lne", "wld", "mm",
                      "ev", "st", "chv", "chs", "rp"],
            sem_es=(fz["es"] if fz else None), prefix=pre)
        cx = Ctx(nc, block, sems)
        fz_barrier(cx, fz)
        ld, pld, lnv, lns, lnt, lne, wld, mm, ev, st, chv, chs, rp = (
            sems[k] for k in ("ld", "pld", "lnv", "lns", "lnt", "lne", "wld", "mm",
                              "ev", "st", "chv", "chs", "rp"))
        T = tens
        x_g = T["x_g"]; xlnT = T["xlnT"]; ident = T["ident"]
        rkT = T["rkT"]; vtok = T["vtok"]; q4 = T["q4"]; rqh = T["rqh"]
        psA, psB, psC, psF, psS, psU, psZ = (
            T[k] for k in ("psA", "psB", "psC", "psF", "psS", "psU", "psZ"))
        ptrA = T["ptrA"]
        expsl = [T["exp0"], T["exp1"], T["exp2"]]

        sb = {"x_g": x_g, "xlnT": xlnT, "bs": T["bs"], "stats": T["stats"],
              "xln_tok": T["xln_tok"], "ident": ident, "ptrA": ptrA,
              "ptrB": ptrA, "eps_col": T["eps_col"]}

        def init_consts(v):
            v.memset(T["eps_col"][:, :], EPS)
            v.memset(T["n1000"][:, :], -1000.0)
            v.memset(T["ckT"][:, :], 0.0)
            v.memset(T["cvT"][:, :], 0.0)
            v.memset(T["cvbm"][:, :], 0.0)
        cx.on("vector", init_consts)

        def prolog_dma(y):
            for dst, src in ((ident, ident_in), (T["onescb"], onescb_in),
                             (T["onesrb"], onesrb_in),
                             (T["ropeP"], ropeP_in), (T["memkT_s"], memkT_d),
                             (T["memv_s"], memv_d), (T["kb1r_s"], kb1r_d),
                             (T["vb1r_s"], vb1r_d), (T["kb2c_s"], kb2c_d),
                             (T["vb2c_s"], vb2c_d)):
                pld.inc(y.dma_start(out=dst[:, :], in_=src[:, :]), 16)
            for j in range(4):
                pld.inc(y.dma_start(out=T["fdiag_s"][:, j * 512:(j + 1) * 512],
                                   in_=fdiag_d[j, :, :]), 16)
            for j in range(5):
                pld.inc(y.dma_start(out=T["band_s"][:, j * 512:(j + 1) * 512],
                                   in_=band_d[j, :, :]), 16)
            for j in range(16):
                pld.inc(y.dma_start(out=T["selE_s"][:, j * 128:(j + 1) * 128],
                                   in_=selE_d[j, :, :]), 16)
        cx.on("sync", prolog_dma)
        def prolog_dmag(y):
            wld.inc(y.dma_start(out=T["bg_bf"][:, :], in_=bg_d[:, :]), 16)
        cx.on("gpsimd", prolog_dmag)
        pro_ld = pld.n

        for eng in ("tensor", "vector", "scalar"):
            def gate_(e, pro_ld=pro_ld):
                e.wait_ge(pld.h, pro_ld)
            cx.on(eng, gate_)

        def stream_block(buf, bufname, src_ap, col0, ncols, nic, blk=16):
            def wl(y, buf=buf, bufname=bufname, src_ap=src_ap, col0=col0,
                   ncols=ncols, nic=nic, blk=blk):
                cx.war(y, bufname)
                src = src_ap.rearrange("(ic p) o -> p ic o", p=128)
                nq = max(1, nic // blk)
                per = nic // nq
                for qq in range(nq):
                    dst = buf[:, qq * per * ncols:(qq + 1) * per * ncols]
                    dst = dst.rearrange("p (ic o) -> p ic o", o=ncols)
                    i = y.dma_start(out=dst,
                                    in_=src[:, qq * per:(qq + 1) * per,
                                            col0:col0 + ncols])
                    wld.inc(i, 16)
            cx.on("gpsimd", wl)
            return wld.n

        for g in range(DBG["ng"]):
            emit_ln_group(cx, g, xin, sb, xg_tiled=False)
            xln_ready = lne.n

            def gconsts(y, g=g):
                cx.war(y, "ropeCS")
                pld.inc(y.dma_start(out=T["ropeCS"][:, 0:512],
                                   in_=ropeC_in[:, g * G512:(g + 1) * G512]), 16)
                pld.inc(y.dma_start(out=T["ropeCS"][:, 512:1024],
                                   in_=ropeS_in[:, g * G512:(g + 1) * G512]), 16)
                cx.war(y, "cmpvis_g")
                pld.inc(y.dma_start(out=T["cmpvis_g"][:, :], in_=cmpvis_d[g, :, :]), 16)

            if DBG["gconst"]:
                cx.on("sync", gconsts)
            def gconsts2(y, g=g):
                cx.war(y, "selvis_g")
                cx.war(y, "ownm1_g")
                for qc in range(NTT):
                    q0 = g * G512 + qc * 128
                    pld.inc(y.dma_start(out=T["selvis_g"][:, qc * 128:(qc + 1) * 128],
                                       in_=selvis_d[q0:q0 + 128, :]), 16)
                    pld.inc(y.dma_start(out=T["ownm1_g"][:, qc * 128:(qc + 1) * 128],
                                       in_=ownm1_d[q0:q0 + 128, :]), 16)
            if DBG["gconst"]:
                cx.on("gpsimd", gconsts2)
            gconst_ld = pld.n
            gconst_wld = wld.n

            wg_done = stream_block(T["qw0"], "qw0", wgT_d, 0, 96, NIC) if DBG["gates"] else 0
            for qc in range(NTT if DBG["gates"] else 0):
                def gmm(t, qc=qc, xln_ready=xln_ready, wg_done=wg_done):
                    t.wait_ge(lne.h, xln_ready)
                    t.wait_ge(wld.h, wg_done)
                    cx.war(t, "psU")
                    for ic in range(NIC):
                        t.matmul(psU[:, 0:96],
                                 xlnT[:, ic * G512 + qc * 128: ic * G512 + (qc + 1) * 128],
                                 T["qw0"][:, ic * 96:(ic + 1) * 96],
                                 start=(ic == 0), stop=False)
                    i = t.matmul(psU[:, 0:96], T["onesrb"][:, :],
                                 T["bg_bf"][:, :], start=False, stop=True)
                    mm.inc(i, 1)
                cx.on("tensor", gmm)
                gmm_done = mm.n
                cx.use_add("qw0", mm, mm.n)

                def gsigf(s_, qc=qc, gmm_done=gmm_done):
                    s_.wait_ge(mm.h, gmm_done)
                    cx.war(s_, "rt12")
                    i = s_.activation(T["rt1"][:, 0:96], psU[:, 0:96], AF.Exp,
                                      scale=-1.0)
                    lns.inc(i, 1)
                cx.on("scalar", gsigf)
                cx.use("psU", lns, lns.n)
                gs_done = lns.n

                def gsig2(v, qc=qc, gs_done=gs_done):
                    v.wait_ge(lns.h, gs_done)
                    if qc == 0:
                        cx.war(v, "gsig")
                    i = v.tensor_scalar(T["rt1"][:, 0:96], T["rt1"][:, 0:96], 1.0,
                                        None, op0=ALU.add)
                    chv.inc(i, 1); v.wait_ge(chv.h, chv.n)
                    i = v.reciprocal(T["gsig"][:, qc * 96:(qc + 1) * 96],
                                     T["rt1"][:, 0:96])
                    rp.inc(i, 1)
                    cx.use("rt12", rp, rp.n)
                cx.on("vector", gsig2)

            for kv in range(DBG["kvproj"]):
                for kvh in range(KVH):
                    ob = 32 + kv * 8 + kvh
                    qw, qwname = T["qw0"], "qw0"
                    wld_done = stream_block(qw, qwname, qkvwT, ob * 128, 128, NIC)

                    ps, psname = ((psA, "psA"), (psB, "psB"))[ob % 2]
                    def pmm(t, qw=qw, ps=ps, psname=psname, wld_done=wld_done,
                            xln_ready=xln_ready):
                        t.wait_ge(wld.h, wld_done)
                        t.wait_ge(lne.h, xln_ready)
                        cx.war(t, psname)
                        for ic in range(NIC):
                            i = t.matmul(ps[:, 0:512],
                                         qw[:, ic * 128:(ic + 1) * 128],
                                         xlnT[:, ic * G512:(ic + 1) * G512],
                                         start=(ic == 0), stop=(ic == NIC - 1))
                        mm.inc(i, 1)
                    cx.on("tensor", pmm)
                    cx.use(qwname, mm, mm.n)
                    mm_done = mm.n

                    tmp, tmpname = (T["ktmp"], "ktmp") if kv == 0 else (T["vtmp"], "vtmp")
                    def pev(v, ps=ps, mm_done=mm_done, tmp=tmp, tmpname=tmpname):
                        v.wait_ge(mm.h, mm_done)
                        cx.war(v, tmpname)
                        i = v.tensor_copy(tmp[:, :], ps[:, 0:512])
                        ev.inc(i, 1)
                    cx.on("vector", pev)
                    ev_done = ev.n

                    def kpl(y, kvh=kvh, kv=kv):
                        cx.war(y, "kvpT")
                        src = (kpT_d if kv == 0 else vpT_d)
                        i = y.dma_start(out=T["kvpT"][:, kv * 512:(kv + 1) * 512],
                                        in_=src[kvh, :, :])
                        wld.inc(i, 16)
                    cx.on("gpsimd", kpl) if DBG["cin"] else None
                    kp_done = wld.n

                    cin, cinname = (T["ckinr"], "ckinr") if kv == 0 else (T["cvinr"], "cvinr")
                    def cinf(v, ps=ps, mm_done=mm_done, kp_done=kp_done,
                             cin=cin, cinname=cinname, kvh=kvh, kv=kv):
                        v.wait_ge(mm.h, mm_done)
                        v.wait_ge(wld.h, kp_done)
                        cx.war(v, "rt12")
                        i = v.tensor_tensor(T["rt1"][:, :], ps[:, 0:512],
                                            T["kvpT"][:, kv * 512:(kv + 1) * 512],
                                            op=ALU.add)
                        chv.inc(i, 1); v.wait_ge(chv.h, chv.n)
                        cx.war(v, cinname)
                        dst = cin[:, :].rearrange("p (tb kv c) -> p tb kv c",
                                                  kv=KVH, c=32)[:, :, kvh, :]
                        src = T["rt1"][:, :].rearrange("p (c tb) -> p tb c", tb=16)
                        i = v.tensor_copy(dst, src)
                        ev.inc(i, 1)
                    cx.on("vector", cinf) if DBG["cin"] else None
                    cx.use("rt12", ev, ev.n)
                    cx.use_add(psname, ev, ev.n)
                    cx.use("kvpT", ev, ev.n)

                    if kv == 0:
                        def kr0(t, ev_done=ev_done, tmp=tmp):
                            t.wait_ge(ev.h, ev_done)
                            cx.war(t, "psU")
                            i = t.matmul(psU[:, 0:512], T["ropeP"][:, :],
                                         tmp[:, :], start=True, stop=True)
                            mm.inc(i, 1)
                        cx.on("tensor", kr0) if DBG["krope"] else None
                        r0_done = mm.n

                        def kr1(v, ev_done=ev_done, tmp=tmp, kvh=kvh, g=g,
                                r0_done=r0_done, gconst_ld=gconst_ld):
                            v.wait_ge(ev.h, ev_done)
                            v.wait_ge(pld.h, gconst_ld)
                            cx.war(v, "rt12")
                            i = v.tensor_tensor(T["rt1"][:, :], tmp[:, :],
                                                T["ropeCS"][:, 0:512], op=ALU.mult)
                            chv.inc(i, 1); v.wait_ge(chv.h, chv.n)
                            v.wait_ge(mm.h, r0_done)
                            i = v.tensor_tensor(T["rt2"][:, :], psU[:, 0:512],
                                                T["ropeCS"][:, 512:1024], op=ALU.mult)
                            chv.inc(i, 1); v.wait_ge(chv.h, chv.n)
                            cx.use_add("psU", chv, chv.n)
                            toff = kvh * S + g * G512
                            i = v.tensor_tensor(rkT[:, toff:toff + G512],
                                                T["rt1"][:, :], T["rt2"][:, :],
                                                op=ALU.add)
                            rp.inc(i, 1)
                            cx.use("rt12", rp, rp.n)
                        cx.on("vector", kr1) if DBG["krope"] else None
                        cx.use("ktmp", rp, rp.n)
                        cx.use_add("ktmp", mm, mm.n)
                    else:
                        for tt in range(NTT):
                            def vt(t, ev_done=ev_done, tt=tt, tmp=tmp):
                                t.wait_ge(ev.h, ev_done)
                                cx.war(t, "ptrA")
                                i = t.transpose(ptrA[:, 0:128],
                                                tmp[:, tt * 128:(tt + 1) * 128],
                                                ident[:, :])
                                lnt.inc(i, 1)
                            cx.on("tensor", vt) if DBG["vtr"] else None
                            t_done = lnt.n
                            def vtev(v, t_done=t_done, tt=tt, kvh=kvh, g=g):
                                v.wait_ge(lnt.h, t_done)
                                kb = g * NTT + tt
                                i = v.tensor_copy(
                                    vtok[:, kvh * S + kb * 128: kvh * S + (kb + 1) * 128],
                                    ptrA[:, 0:128])
                                rp.inc(i, 1)
                            cx.on("vector", vtev) if DBG["vtr"] else None
                            cx.use("ptrA", rp, rp.n)
                        if tt == NTT - 1:
                            cx.use("vtmp", lnt, lnt.n)
            kv_ready = rp.n
            cin_ready = ev.n

            for kv in range(2 if DBG["mlp"] else 0):
                cin = T["ckinr"] if kv == 0 else T["cvinr"]
                w1d = kw1T_d if kv == 0 else vw1T_d
                w2d = kw2T_d if kv == 0 else vw2T_d
                b1 = T["kb1r_s"] if kv == 0 else T["vb1r_s"]
                b2 = T["kb2c_s"] if kv == 0 else T["vb2c_s"]
                cT, cTname = (T["ckT"], "ckT") if kv == 0 else (T["cvT"], "cvT")

                for oc in range(16):
                    mw, mwname = T["mw0"], "mw0"
                    wld_done = stream_block(mw, mwname, w1d, oc * 128, 128, 16)
                    ps, psname = ((psA, "psA"), (psB, "psB"))[oc % 2]

                    def h1mm(t, mw=mw, ps=ps, psname=psname, cin=cin,
                             wld_done=wld_done, cin_ready=cin_ready):
                        t.wait_ge(wld.h, wld_done)
                        t.wait_ge(ev.h, cin_ready)
                        cx.war(t, psname)
                        for tb in range(16):
                            i = t.matmul(ps[:, 0:256],
                                         mw[:, tb * 128:(tb + 1) * 128],
                                         cin[:, tb * 256:(tb + 1) * 256],
                                         start=(tb == 0), stop=(tb == 15))
                        mm.inc(i, 1)
                    cx.on("tensor", h1mm)
                    cx.use(mwname, mm, mm.n)
                    h1_done = mm.n

                    def h1ev(v, oc=oc, ps=ps, h1_done=h1_done, b1=b1):
                        v.wait_ge(mm.h, h1_done)
                        cx.war(v, "h1a") if oc == 0 else None
                        i = v.tensor_scalar(T["h1a"][:, oc * 256:(oc + 1) * 256],
                                            ps[:, 0:256], b1[:, oc:oc + 1], 0.0,
                                            op0=ALU.add, op1=ALU.max)
                        ev.inc(i, 1)
                    cx.on("vector", h1ev)
                    cx.use(psname, ev, ev.n)
                cx.use("ckinr" if kv == 0 else "cvinr", mm, mm.n)
                h1_ready = ev.n

                wld2 = stream_block(T["mw0"], "mw0", w2d, 0, 128, 16)
                def h2mm(t, wld2=wld2, h1_ready=h1_ready):
                    t.wait_ge(wld.h, wld2)
                    t.wait_ge(ev.h, h1_ready)
                    cx.war(t, "psA")
                    for oc in range(16):
                        i = t.matmul(psA[:, 0:256],
                                     T["mw0"][:, oc * 128:(oc + 1) * 128],
                                     T["h1a"][:, oc * 256:(oc + 1) * 256],
                                     start=(oc == 0), stop=(oc == 15))
                    mm.inc(i, 1)
                cx.on("tensor", h2mm)
                cx.use("mw0", mm, mm.n)
                cx.use("h1a", mm, mm.n)
                h2_done = mm.n

                def h2ev(v, h2_done=h2_done, b2=b2, cT=cT, cTname=cTname, g=g):
                    v.wait_ge(mm.h, h2_done)
                    cx.war(v, cTname)
                    dst = cT[:, :].rearrange("p (kv c) -> p kv c", c=128)[
                        :, :, g * 32:(g + 1) * 32]
                    src = psA[:, 0:256].rearrange("p (kv c) -> p kv c", c=32)
                    i = v.tensor_scalar(dst, src, b2[:, 0:1], None, op0=ALU.add)
                    ev.inc(i, 1)
                cx.on("vector", h2ev)
                cx.use("psA", ev, ev.n)
            mlp_ready = ev.n

            for kvh in range(KVH if DBG["mlp"] else 0):
                def cvt(t, kvh=kvh, mlp_ready=mlp_ready):
                    t.wait_ge(ev.h, mlp_ready)
                    cx.war(t, "ptrA")
                    i = t.transpose(ptrA[:, 0:128],
                                    T["cvT"][:, kvh * 128:(kvh + 1) * 128],
                                    ident[:, :])
                    lnt.inc(i, 1)
                cx.on("tensor", cvt)
                t_done = lnt.n
                def cvtev(v, t_done=t_done, kvh=kvh):
                    v.wait_ge(lnt.h, t_done)
                    cx.war(v, "cvbm")
                    i = v.tensor_copy(T["cvbm"][:, kvh * 128:(kvh + 1) * 128],
                                      ptrA[:, 0:128])
                    rp.inc(i, 1)
                cx.on("vector", cvtev)
                cx.use("ptrA", rp, rp.n)
            cvbm_ready = rp.n

            for kvh in range(DBG["nkvh"]):
                for gh in range(GH):
                    h = kvh * GH + gh
                    qw, qwname = T["qw0"], "qw0"
                    wld_done = stream_block(qw, qwname, qkvwT, h * 128, 128, NIC)
                    ps, psname = ((psA, "psA"), (psB, "psB"))[gh % 2]
                    def qmm(t, qw=qw, ps=ps, psname=psname, wld_done=wld_done,
                            xln_ready=xln_ready):
                        t.wait_ge(wld.h, wld_done)
                        t.wait_ge(lne.h, xln_ready)
                        cx.war(t, psname)
                        for ic in range(NIC):
                            i = t.matmul(ps[:, 0:512],
                                         qw[:, ic * 128:(ic + 1) * 128],
                                         xlnT[:, ic * G512:(ic + 1) * G512],
                                         start=(ic == 0), stop=(ic == NIC - 1))
                        mm.inc(i, 1)
                    cx.on("tensor", qmm)
                    cx.use(qwname, mm, mm.n)
                    mm_done = mm.n
                    def qev(v, gh=gh, ps=ps, mm_done=mm_done):
                        v.wait_ge(mm.h, mm_done)
                        if gh == 0:
                            cx.war(v, "q4")
                        i = v.tensor_copy(T["q4"][:, gh * G512:(gh + 1) * G512],
                                          ps[:, 0:512])
                        ev.inc(i, 1)
                    cx.on("vector", qev)
                    cx.use(psname, ev, ev.n)
                q4_ready = ev.n

                for qc in range(NTT if DBG["sel"] else 0):
                    def selmm(t, kvh=kvh, qc=qc, q4_ready=q4_ready,
                              mlp_ready=mlp_ready, gconst_ld=gconst_ld):
                        t.wait_ge(ev.h, max(q4_ready, mlp_ready))
                        t.wait_ge(pld.h, gconst_ld)
                        cx.war(t, "psU")
                        for gh in range(GH):
                            i = t.matmul(
                                psU[:, 0:128],
                                T["q4"][:, gh * G512 + qc * 128: gh * G512 + (qc + 1) * 128],
                                T["ckT"][:, kvh * 128:(kvh + 1) * 128],
                                start=(gh == 0), stop=(gh == GH - 1))
                        mm.inc(i, 1)
                    cx.on("tensor", selmm)
                    selmm_done = mm.n

                    def selsoft(v, s_, qc=qc, selmm_done=selmm_done):
                        pass
                    def selv1(v, qc=qc, selmm_done=selmm_done,
                              gconst_wld=gconst_wld):
                        v.wait_ge(mm.h, selmm_done)
                        v.wait_ge(wld.h, gconst_wld)
                        cx.war(v, "pselw")
                        i = v.tensor_tensor(psU[:, 0:128], psU[:, 0:128],
                                            T["selvis_g"][:, qc * 128:(qc + 1) * 128],
                                            op=ALU.add)
                        chv.inc(i, 1); v.wait_ge(chv.h, chv.n)
                        i = v.tensor_reduce(T["stats"][:, 10:11], psU[:, 0:128],
                                            axis=AX.X, op=ALU.max)
                        chv.inc(i, 1); v.wait_ge(chv.h, chv.n)
                        i = v.tensor_scalar(T["stats"][:, 11:12], T["stats"][:, 10:11],
                                            SCSEL, -1000.0, op0=ALU.mult, op1=ALU.max)
                        chv.inc(i, 1); v.wait_ge(chv.h, chv.n)
                        i = v.tensor_scalar(T["stats"][:, 12:13], T["stats"][:, 11:12],
                                            -1.0, None, op0=ALU.mult)
                        ev.inc(i, 1)
                    cx.on("vector", selv1)
                    neg_done = ev.n

                    def sele(s_, qc=qc, neg_done=neg_done):
                        s_.wait_ge(ev.h, neg_done)
                        cx.war(s_, "pselw2")
                        i = s_.activation(T["psel"][:, :], psU[:, 0:128], AF.Exp,
                                          scale=SCSEL, bias=T["stats"][:, 12:13],
                                          accum_out=T["stats"][:, 13:14])
                        chs.inc(i, 1); s_.wait_ge(chs.h, chs.n)
                        i = s_.activation(T["stats"][:, 14:15], T["stats"][:, 12:13],
                                          AF.Exp, bias=T["n1000"][:, 0:1])
                        lns.inc(i, 1)
                    cx.on("scalar", sele)
                    cx.use("psU", lns, lns.n)
                    sele_done = lns.n

                    def selv2(v, qc=qc, sele_done=sele_done):
                        v.wait_ge(lns.h, sele_done)
                        i = v.tensor_tensor(T["stats"][:, 15:16], T["stats"][:, 13:14],
                                            T["stats"][:, 14:15], op=ALU.add)
                        chv.inc(i, 1); v.wait_ge(chv.h, chv.n)
                        i = v.tensor_scalar(T["stats"][:, 15:16], T["stats"][:, 15:16],
                                            1e-10, None, op0=ALU.mult)
                        chv.inc(i, 1); v.wait_ge(chv.h, chv.n)
                        i = v.max(out=T["m8"][:, :], in_=T["psel"][:, :])
                        chv.inc(i, 1); v.wait_ge(chv.h, chv.n)
                        i = v.memset(T["m8"][:, 4:8], -1.0)
                        chv.inc(i, 1); v.wait_ge(chv.h, chv.n)
                        i = v.match_replace(out=T["pscr"][:, :], in_to_replace=T["m8"][:, :],
                                            in_values=T["psel"][:, :], imm_value=-1.0)
                        chv.inc(i, 1); v.wait_ge(chv.h, chv.n)
                        i = v.tensor_tensor(T["selm"][:, :], T["psel"][:, :],
                                            T["pscr"][:, :], op=ALU.not_equal)
                        chv.inc(i, 1); v.wait_ge(chv.h, chv.n)
                        i = v.tensor_scalar(T["pscr"][:, :], T["psel"][:, :],
                                            T["stats"][:, 15:16], None, op0=ALU.is_gt)
                        chv.inc(i, 1); v.wait_ge(chv.h, chv.n)
                        i = v.tensor_tensor(T["selm"][:, :], T["selm"][:, :],
                                            T["pscr"][:, :], op=ALU.mult)
                        chv.inc(i, 1); v.wait_ge(chv.h, chv.n)
                        i = v.tensor_tensor(T["pscr"][:, :], T["selm"][:, :],
                                            T["ownm1_g"][:, qc * 128:(qc + 1) * 128],
                                            op=ALU.add)
                        ev.inc(i, 1)
                        chv.inc(i, 0) if False else None
                    cx.on("vector", selv2)
                    selm_done = ev.n

                    def selcast(v, qc=qc, selm_done=selm_done):
                        v.wait_ge(ev.h, selm_done)
                        cx.war(v, "pselbf")
                        i = v.tensor_copy(T["xatt_tok"][:, :], T["pscr"][:, :])
                        ev.inc(i, 1)
                    cx.on("vector", selcast)
                    selc_done = ev.n

                    def selt(t, qc=qc, selc_done=selc_done):
                        t.wait_ge(ev.h, selc_done)
                        cx.war(t, "ptrA")
                        i = t.transpose(ptrA[:, 0:128], T["xatt_tok"][:, :],
                                        ident[:, :])
                        lnt.inc(i, 1)
                    cx.on("tensor", selt)
                    cx.use("pselbf", lnt, lnt.n)
                    t_done = lnt.n

                    def seltev(v, qc=qc, t_done=t_done):
                        v.wait_ge(lnt.h, t_done)
                        cx.war(v, "selTw")
                        i = v.tensor_copy(T["selT"][:, qc * 128:(qc + 1) * 128],
                                          ptrA[:, 0:128])
                        rp.inc(i, 1)
                    cx.on("vector", seltev)
                    cx.use("ptrA", rp, rp.n)
                selT_ready = rp.n

                for gh in range(DBG["ngh"]):
                    h = kvh * GH + gh
                    qsl = slice(gh * G512, (gh + 1) * G512)

                    if True:
                      pass
                    def cmm(t, kvh=kvh, qsl=qsl, q4_ready=q4_ready,
                            mlp_ready=mlp_ready):
                        t.wait_ge(ev.h, max(q4_ready, mlp_ready))
                        cx.war(t, "psA")
                        i = t.matmul(psA[:, 0:512],
                                     T["ckT"][:, kvh * 128:(kvh + 1) * 128],
                                     T["q4"][:, qsl], start=True, stop=True)
                        mm.inc(i, 1)
                        cx.war(t, "psU")
                        i = t.matmul(psU[0:1, 0:512],
                                     T["memkT_s"][:, kvh:kvh + 1],
                                     T["q4"][:, qsl], start=True, stop=True)
                        mm.inc(i, 1)
                    cx.on("tensor", cmm) if DBG["cmp"] else None
                    cmm_done = mm.n

                    def cexp(s_, cmm_done=cmm_done):
                        s_.wait_ge(mm.h, cmm_done)
                        cx.war(s_, "exp0")
                        i = s_.activation(expsl[0][:, :], psA[:, 0:512], AF.Exp,
                                          scale=SC0)
                        chs.inc(i, 1)
                        cx.war(s_, "exp1")
                        i = s_.activation(T["exp1"][0:1, 0:512], psU[0:1, 0:512], AF.Exp,
                                          scale=SC0)
                        lns.inc(i, 1)
                    cx.on("scalar", cexp) if DBG["cmp"] else None
                    cx.use("psA", lns, lns.n)
                    cx.use_add("psU", lns, lns.n)
                    cexp_done = lns.n

                    def cmask(v, cexp_done=cexp_done, gconst_ld=gconst_ld):
                        v.wait_ge(lns.h, cexp_done)
                        v.wait_ge(pld.h, gconst_ld)
                        i = v.tensor_tensor(expsl[0][:, :], expsl[0][:, :],
                                            T["cmpvis_g"][:, :], op=ALU.mult)
                        chv.inc(i, 1); v.wait_ge(chv.h, chv.n)
                    cx.on("vector", cmask) if DBG["cmp"] else None
                    cmask_done = chv.n

                    def cpv(t, kvh=kvh, cmask_done=cmask_done, cvbm_ready=cvbm_ready):
                        t.wait_ge(chv.h, cmask_done)
                        t.wait_ge(rp.h, cvbm_ready)
                        cx.war(t, "psC")
                        cx.war(t, "psZ")
                        for qc in range(NTT):
                            t.matmul(psC[:, qc * 128:(qc + 1) * 128],
                                     expsl[0][:, qc * 128:(qc + 1) * 128],
                                     T["cvbm"][:, kvh * 128:(kvh + 1) * 128],
                                     start=(qc == 0), stop=False,
                                     skip_group_check=True)
                            t.matmul(psC[:, qc * 128:(qc + 1) * 128],
                                     T["exp1"][0:1, qc * 128:(qc + 1) * 128],
                                     T["memv_s"][:, kvh * 128:(kvh + 1) * 128],
                                     start=False, stop=True,
                                     skip_group_check=True)
                            t.matmul(psZ[:, qc * 3:qc * 3 + 1],
                                     expsl[0][:, qc * 128:(qc + 1) * 128],
                                     T["onescb"][:, :], start=(qc == 0), stop=False,
                                     skip_group_check=True)
                            i = t.matmul(psZ[:, qc * 3:qc * 3 + 1],
                                         T["exp1"][0:1, qc * 128:(qc + 1) * 128],
                                         T["onescb"][0:1, :], start=False, stop=True,
                                         skip_group_check=True)
                        mm.inc(i, 1)
                    cx.on("tensor", cpv) if DBG["cmp"] else None
                    cx.use("exp0", mm, mm.n)
                    cx.use_add("exp1", mm, mm.n)

                    def hr0(t, qsl=qsl, q4_ready=q4_ready):
                        t.wait_ge(ev.h, q4_ready)
                        cx.war(t, "psU")
                        i = t.matmul(psU[:, 0:512], T["ropeP"][:, :],
                                     T["q4"][:, qsl], start=True, stop=True)
                        mm.inc(i, 1)
                    cx.on("tensor", hr0)
                    hr0_done = mm.n

                    def hr1(v, qsl=qsl, hr0_done=hr0_done, q4_ready=q4_ready,
                            gconst_ld=gconst_ld):
                        v.wait_ge(ev.h, q4_ready)
                        v.wait_ge(pld.h, gconst_ld)
                        cx.war(v, "rt12")
                        i = v.tensor_tensor(T["rt1"][:, :], T["q4"][:, qsl],
                                            T["ropeCS"][:, 0:512], op=ALU.mult)
                        chv.inc(i, 1); v.wait_ge(chv.h, chv.n)
                        v.wait_ge(mm.h, hr0_done)
                        i = v.tensor_tensor(T["rt2"][:, :], psU[:, 0:512],
                                            T["ropeCS"][:, 512:1024], op=ALU.mult)
                        chv.inc(i, 1); v.wait_ge(chv.h, chv.n)
                        cx.use_add("psU", chv, chv.n)
                        cx.war(v, "rqh")
                        i = v.tensor_tensor(T["rqh"][:, :], T["rt1"][:, :],
                                            T["rt2"][:, :], op=ALU.add)
                        rp.inc(i, 1)
                        cx.use("rt12", rp, rp.n)
                    cx.on("vector", hr1)
                    rqh_ready = rp.n

                    NKB = 4 * (g + 1)
                    for kb in range(NKB if DBG["fine"] else 0):
                        sps, spsname = ((psA, "psA"), (psB, "psB"))[kb % 2]
                        slot = 1 + (kb % 2)
                        esb = expsl[slot]
                        def fmm(t, kvh=kvh, kb=kb, sps=sps, spsname=spsname,
                                rqh_ready=rqh_ready, kv_ready=kv_ready,
                                selT_ready=selT_ready):
                            t.wait_ge(rp.h, max(rqh_ready, kv_ready, selT_ready))
                            cx.war(t, spsname)
                            t.matmul(sps[:, 0:512],
                                     rkT[:, kvh * S + kb * 128: kvh * S + (kb + 1) * 128],
                                     T["rqh"][:, :], start=True, stop=False)
                            i = t.matmul(sps[:, 0:512],
                                         T["selE_s"][:, kb * 128:(kb + 1) * 128],
                                         T["selT"][:, :], start=False, stop=True)
                            mm.inc(i, 1)
                        cx.on("tensor", fmm)
                        fmm_done = mm.n
                        cx.use_add("selTw", mm, mm.n)

                        def fexp(s_, sps=sps, esb=esb, slot=slot, fmm_done=fmm_done):
                            s_.wait_ge(mm.h, fmm_done)
                            cx.war(s_, f"exp{slot}")
                            i = s_.activation(esb[:, :], sps[:, 0:512], AF.Exp,
                                              scale=SC0)
                            lns.inc(i, 1)
                        cx.on("scalar", fexp)
                        cx.use(spsname, lns, lns.n)
                        fexp_done = lns.n

                        diag = (kb >= 4 * g)
                        if diag:
                            off = kb - 4 * g
                            def fmask(v, esb=esb, off=off, fexp_done=fexp_done):
                                v.wait_ge(lns.h, fexp_done)
                                i = v.tensor_tensor(
                                    esb[:, :], esb[:, :],
                                    T["fdiag_s"][:, off * 512:(off + 1) * 512],
                                    op=ALU.mult)
                                chv.inc(i, 1); v.wait_ge(chv.h, chv.n)
                            cx.on("vector", fmask)
                        fm_done = chv.n

                        def fpv(t, kvh=kvh, kb=kb, esb=esb, slot=slot, NKB=NKB,
                                diag=diag, fm_done=fm_done, fexp_done=fexp_done):
                            if diag:
                                t.wait_ge(chv.h, fm_done)
                            else:
                                t.wait_ge(lns.h, fexp_done)
                            if kb == 0:
                                cx.war(t, "psF")
                            for qc in range(NTT):
                                t.matmul(psF[:, qc * 128:(qc + 1) * 128],
                                         esb[:, qc * 128:(qc + 1) * 128],
                                         vtok[:, kvh * S + kb * 128: kvh * S + (kb + 1) * 128],
                                         start=(kb == 0 and qc == 0),
                                         stop=(kb == NKB - 1),
                                         skip_group_check=True)
                                i = t.matmul(psZ[:, qc * 3 + 1:qc * 3 + 2],
                                             esb[:, qc * 128:(qc + 1) * 128],
                                             T["onescb"][:, :],
                                             start=False,
                                             stop=(kb == NKB - 1),
                                             skip_group_check=True)
                            mm.inc(i, 1)
                        cx.on("tensor", fpv)
                        cx.use(f"exp{slot}", mm, mm.n)

                    kb0 = max(0, 4 * g - 1)
                    for kb in range(kb0, 4 * g + 4) if DBG["slide"] else []:
                        sps, spsname = ((psA, "psA"), (psB, "psB"))[kb % 2]
                        slot = 1 + (kb % 2)
                        esb = expsl[slot]
                        first = (kb == kb0)
                        last = (kb == 4 * g + 3)
                        def smm2(t, kvh=kvh, kb=kb, sps=sps, spsname=spsname,
                                 rqh_ready=rqh_ready, kv_ready=kv_ready):
                            t.wait_ge(rp.h, max(rqh_ready, kv_ready))
                            cx.war(t, spsname)
                            i = t.matmul(sps[:, 0:512],
                                         rkT[:, kvh * S + kb * 128: kvh * S + (kb + 1) * 128],
                                         T["rqh"][:, :], start=True, stop=True)
                            mm.inc(i, 1)
                        cx.on("tensor", smm2)
                        smm2_done = mm.n

                        def sexp2(s_, sps=sps, esb=esb, slot=slot,
                                  smm2_done=smm2_done):
                            s_.wait_ge(mm.h, smm2_done)
                            cx.war(s_, f"exp{slot}")
                            i = s_.activation(esb[:, :], sps[:, 0:512], AF.Exp,
                                              scale=SC0)
                            lns.inc(i, 1)
                        cx.on("scalar", sexp2)
                        cx.use(spsname, lns, lns.n)
                        sexp2_done = lns.n

                        off = kb - 4 * g + 1
                        def smask(v, esb=esb, off=off, sexp2_done=sexp2_done):
                            v.wait_ge(lns.h, sexp2_done)
                            i = v.tensor_tensor(
                                esb[:, :], esb[:, :],
                                T["band_s"][:, off * 512:(off + 1) * 512],
                                op=ALU.mult)
                            chv.inc(i, 1); v.wait_ge(chv.h, chv.n)
                        cx.on("vector", smask)
                        sm_done = chv.n

                        def spv(t, kvh=kvh, kb=kb, esb=esb, slot=slot,
                                first=first, last=last, sm_done=sm_done):
                            t.wait_ge(chv.h, sm_done)
                            if first:
                                cx.war(t, "psS")
                            for qc in range(NTT):
                                t.matmul(psS[:, qc * 128:(qc + 1) * 128],
                                         esb[:, qc * 128:(qc + 1) * 128],
                                         vtok[:, kvh * S + kb * 128: kvh * S + (kb + 1) * 128],
                                         start=(first and qc == 0), stop=last,
                                         skip_group_check=True)
                                i = t.matmul(psZ[:, qc * 3 + 2:qc * 3 + 3],
                                             esb[:, qc * 128:(qc + 1) * 128],
                                             T["onescb"][:, :],
                                             start=False, stop=last,
                                             skip_group_check=True)
                            mm.inc(i, 1)
                        cx.on("tensor", spv)
                        cx.use(f"exp{slot}", mm, mm.n)
                    allpv_done = mm.n

                    def zev2(v, h=h, allpv_done=allpv_done):
                        v.wait_ge(mm.h, allpv_done)
                        v.wait_ge(rp.h, 0)
                        i = v.tensor_copy(T["Zsb"][:, :], psZ[:, 0:12])
                        chv.inc(i, 1); v.wait_ge(chv.h, chv.n)
                        cx.use("psZ", chv, chv.n)
                        i = v.reciprocal(T["Zinv"][:, :], T["Zsb"][:, :])
                        chv.inc(i, 1); v.wait_ge(chv.h, chv.n)
                        gv = T["gsig"][:, :].rearrange("p (qc x) -> p qc x", x=96)[
                            :, :, 3 * h: 3 * h + 3]
                        zv = T["Zinv"][:, :].rearrange("p (qc b) -> p qc b", b=3)
                        gzv = T["gz"][:, :].rearrange("p (qc b) -> p qc b", b=3)
                        i = v.tensor_tensor(gzv, gv, zv, op=ALU.mult)
                        chv.inc(i, 1); v.wait_ge(chv.h, chv.n)
                    cx.on("vector", zev2)
                    gz_done = chv.n

                    for qc in range(NTT):
                        def comb(v, qc=qc, gz_done=gz_done):
                            v.wait_ge(chv.h, gz_done)
                            cx.war(v, "rt12")
                            i = v.tensor_scalar(T["rt1"][:, 0:128], psC[:, qc * 128:(qc + 1) * 128],
                                                T["gz"][:, qc * 3:qc * 3 + 1], None,
                                                op0=ALU.mult)
                            chv.inc(i, 1); v.wait_ge(chv.h, chv.n)
                            i = v.scalar_tensor_tensor(
                                T["rt2"][:, 0:128], psF[:, qc * 128:(qc + 1) * 128],
                                T["gz"][:, qc * 3 + 1:qc * 3 + 2],
                                T["rt1"][:, 0:128], op0=ALU.mult, op1=ALU.add)
                            chv.inc(i, 1); v.wait_ge(chv.h, chv.n)
                            cx.war(v, "xatt_tok_w")
                            i = v.scalar_tensor_tensor(
                                T["xatt_tok"][:, :], psS[:, qc * 128:(qc + 1) * 128],
                                T["gz"][:, qc * 3 + 2:qc * 3 + 3],
                                T["rt2"][:, 0:128], op0=ALU.mult, op1=ALU.add)
                            ev.inc(i, 1)
                            cx.use("rt12", ev, ev.n)
                        cx.on("vector", comb)
                        comb_done = ev.n
                        if qc == NTT - 1:
                            cx.use("psC", ev, ev.n)
                            cx.use("psF", ev, ev.n)
                            cx.use("psS", ev, ev.n)

                        def xt(t, comb_done=comb_done):
                            t.wait_ge(ev.h, comb_done)
                            cx.war(t, "ptrA")
                            i = t.transpose(ptrA[:, 0:128], T["xatt_tok"][:, :],
                                            ident[:, :])
                            lnt.inc(i, 1)
                        cx.on("tensor", xt)
                        cx.use("xatt_tok_w", lnt, lnt.n)
                        t_done = lnt.n

                        def xtev(v, gh=gh, qc=qc, t_done=t_done):
                            v.wait_ge(lnt.h, t_done)
                            if qc == 0:
                                cx.war(v, "xattT_kvh")
                            i = v.tensor_copy(
                                T["xattT_kvh"][:, qc * 128:(qc + 1) * 128],
                                ptrA[:, 0:128])
                            rp.inc(i, 1)
                        cx.on("vector", xtev)
                        cx.use("ptrA", rp, rp.n)
                    xatt_done = rp.n

                    def xst(y, h=h, g=g, xatt_done=xatt_done):
                        y.wait_ge(rp.h, xatt_done)
                        i = y.dma_start(
                            out=xattT_out[h * 128:(h + 1) * 128,
                                          g * G512:(g + 1) * G512],
                            in_=T["xattT_kvh"][:, :])
                        st.inc(i, 16)
                    cx.on("sync", xst)
                    cx.use("xattT_kvh", st, st.n)
                cx.use("q4", mm, mm.n)
                cx.use_add("q4", rp, rp.n)
            cx.use("xlnT", mm, mm.n)
            cx.use("selvis_g", chv, chv.n)
            cx.use("ownm1_g", ev, ev.n)
            cx.use("cmpvis_g", chv, chv.n)
            cx.use("ropeCS", chv, chv.n)
            cx.use("ckT", mm, mm.n)
            cx.use("cvT", mm, mm.n)
            cx.use("cvbm", mm, mm.n)
            cx.use("gsig", chv, chv.n)
            cx.use("q4", mm, mm.n)
            cx.use_add("q4", rp, rp.n)

        if DBG["dump"]:
            def dmpf(y):
                y.wait_ge(rp.h, rp.n)
                y.wait_ge(ev.h, ev.n)
                y.wait_ge(mm.h, mm.n)
                for nm in dmp:
                    st.inc(y.dma_start(out=dmp[nm], in_=T[nm][:, :]), 16)
            cx.on("sync", dmpf)
        final_st = st.n
        def fin(y, final_st=final_st):
            y.wait_ge(st.h, final_st)
        cx.on("sync", fin)
        if fz is not None:
            fz["barrier"] = (st.h, st.n)
    return nc



def build_proj_residual(fz=None):
    nc, pre, io = fz_begin(fz)
    xres = io.get("xres") if fz else nc.dram_tensor("xres", [S, D], F32, kind="ExternalInput").ap()
    xT = io.get("xT") if fz else nc.dram_tensor("xT", [D, S], BF16, kind="ExternalInput").ap()
    wT = nc.dram_tensor(pre + "wT", [D, D], F32, kind="ExternalInput").ap()
    xout = io.get("xout") if fz else nc.dram_tensor("xout", [S, D], F32, kind="ExternalOutput").ap()

    OB = 256
    NOB = D // OB

    es = contextlib.ExitStack()
    with es:
        tens, sems, block = alloc(
            nc, es,
            sbufs=[("lh0", [128, NIC * 128], BF16), ("lh1", [128, NIC * 128], BF16),
                   ("wbf0", [128, NIC * OB], BF16), ("wbf1", [128, NIC * OB], BF16),
                   ("xr", [128, 2 * D], F32),
                   ("outsb", [128, 2 * OB], F32)],
            psums=[("psA", [128, 512], F32), ("psB", [128, 512], F32),
                   ("psC", [128, 512], F32), ("psD", [128, 512], F32)],
            semnames=["ld", "lld", "wld", "mm", "ev", "st"],
            sem_es=(fz["es"] if fz else None), prefix=pre)
        cx = Ctx(nc, block, sems)
        fz_barrier(cx, fz)
        ld, lld, wld, mm, ev, st = (sems[k] for k in ("ld", "lld", "wld", "mm", "ev", "st"))
        lh = [tens["lh0"], tens["lh1"]]
        wb = [tens["wbf0"], tens["wbf1"]]
        xr = tens["xr"]; outsb = tens["outsb"]
        psA, psB = tens["psA"], tens["psB"]

        for ob in range(NOB):
            par = ob % 2
            wbf = wb[par]
            def wload(y, ob=ob, wbf=wbf, par=par):
                cx.war(y, f"wb{par}")
                src = wT.rearrange("(ic p) o -> p ic o", p=128)
                for quarter in range(4):
                    dst = wbf[:, quarter * 8 * OB:(quarter + 1) * 8 * OB]
                    dst = dst.rearrange("p (ic o) -> p ic o", o=OB)
                    i = y.dma_start(out=dst,
                                    in_=src[:, quarter * 8:(quarter + 1) * 8,
                                            ob * OB:(ob + 1) * OB])
                    wld.inc(i, 16)
            cx.on("gpsimd", wload)
            wld_done = wld.n

            for tt in range(16):
                tpar = tt % 2
                lhb = lh[tpar]
                def lhload(y, ob=ob, tt=tt, lhb=lhb, tpar=tpar):
                    cx.war(y, f"lh{tpar}")
                    src = xT.rearrange("(ic p) t -> p ic t", p=128)
                    i = y.dma_start(
                        out=lhb[:, :].rearrange("p (ic t) -> p ic t", t=128),
                        in_=src[:, :, tt * 128:(tt + 1) * 128])
                    lld.inc(i, 16)
                    cx.war(y, f"xr{tpar}")
                    i = y.dma_start(out=xr[:, tpar * D:tpar * D + D],
                                    in_=xres[tt * 128:(tt + 1) * 128, :])
                    ld.inc(i, 16)
                cx.on("sync", lhload)
                lh_done = lld.n
                xr_done = ld.n

                ps, psname = ((psA, "psA"), (psB, "psB"), (tens["psC"], "psC"),
                              (tens["psD"], "psD"))[tt % 4]
                def mmf(t, ob=ob, tt=tt, lhb=lhb, wbf=wbf, ps=ps, psname=psname,
                        wld_done=wld_done, lh_done=lh_done):
                    t.wait_ge(wld.h, wld_done)
                    t.wait_ge(lld.h, lh_done)
                    cx.war(t, psname)
                    for ic in range(NIC):
                        i = t.matmul(ps[:, 0:OB],
                                     lhb[:, ic * 128:(ic + 1) * 128],
                                     wbf[:, ic * OB:(ic + 1) * OB],
                                     start=(ic == 0), stop=(ic == NIC - 1))
                    mm.inc(i, 1)
                cx.on("tensor", mmf)
                mm_done = mm.n
                cx.use(f"lh{tpar}", mm, mm.n)
                if tt == 15:
                    cx.use(f"wb{par}", mm, mm.n)

                slot = tt % 2
                def evf(v, ob=ob, tt=tt, ps=ps, mm_done=mm_done, slot=slot,
                        xr_done=xr_done, tpar=tpar):
                    v.wait_ge(mm.h, mm_done)
                    v.wait_ge(ld.h, xr_done)
                    cx.war(v, f"outsb{slot}")
                    i = v.tensor_tensor(
                        outsb[:, slot * OB:(slot + 1) * OB], ps[:, 0:OB],
                        xr[:, tpar * D + ob * OB: tpar * D + (ob + 1) * OB],
                        op=ALU.add)
                    ev.inc(i, 1)
                cx.on("vector", evf)
                cx.use(psname, ev, ev.n)
                cx.use_add(f"xr{tpar}", ev, ev.n)
                ev_done = ev.n

                def stf(y, ob=ob, tt=tt, ev_done=ev_done, slot=slot):
                    y.wait_ge(ev.h, ev_done)
                    i = y.dma_start(out=xout[tt * 128:(tt + 1) * 128,
                                             ob * OB:(ob + 1) * OB],
                                    in_=outsb[:, slot * OB:(slot + 1) * OB])
                    st.inc(i, 16)
                cx.on("gpsimd", stf)
                cx.use(f"outsb{slot}", st, st.n)

        final_st = st.n
        def fin(y, final_st=final_st):
            y.wait_ge(st.h, final_st)
        cx.on("gpsimd", fin)
        if fz is not None:
            fz["barrier"] = (st.h, st.n)
    return nc


def prep_l0_attn_inputs(inputs, consts, xcur, pre=""):
    f32 = np.float32
    g = lambda k: np.asarray(inputs[k], dtype=f32)
    qkvwT = np.ascontiguousarray(g("L0_attn_w_qkv").T)
    wgT = np.ascontiguousarray(g("L0_attn_w_gate").T)
    bgrow = np.ascontiguousarray(g("L0_attn_b_gate").reshape(1, 96))
    kw1T = np.ascontiguousarray(g("L0_attn_k_w1").T)
    vw1T = np.ascontiguousarray(g("L0_attn_v_w1").T)
    kw2T = np.ascontiguousarray(g("L0_attn_k_w2").T)
    vw2T = np.ascontiguousarray(g("L0_attn_v_w2").T)
    kb1r = np.ascontiguousarray(g("L0_attn_k_b1").reshape(16, 128).T)
    vb1r = np.ascontiguousarray(g("L0_attn_v_b1").reshape(16, 128).T)
    kb2c = np.ascontiguousarray(g("L0_attn_k_b2").reshape(128, 1))
    vb2c = np.ascontiguousarray(g("L0_attn_v_b2").reshape(128, 1))
    kp = g("L0_attn_k_pos")
    vp = g("L0_attn_v_pos")
    kpT = np.ascontiguousarray(np.stack(
        [np.tile(kp[h], (32, 1)).T for h in range(8)]))
    vpT = np.ascontiguousarray(np.stack(
        [np.tile(vp[h], (32, 1)).T for h in range(8)]))
    mem = g("L0_attn_mem_kv")
    memkT = np.ascontiguousarray(mem[0, :, 0, :].T.astype(BF))
    memv = np.ascontiguousarray(mem[1, :, 0, :].reshape(1, 8 * 128).astype(BF))
    maps = []
    for core in range(8):
        maps.append({
            "xin": np.ascontiguousarray(xcur[core]),
            "qkvwT": qkvwT, "wgT": wgT, "bgrow": bgrow,
            "kw1T": kw1T, "vw1T": vw1T, "kw2T": kw2T, "vw2T": vw2T,
            "kb1r": kb1r, "vb1r": vb1r, "kb2c": kb2c, "vb2c": vb2c,
            "kpT": kpT, "vpT": vpT, "memkT": memkT, "memv": memv,
            "ident_bf": consts["ident_bf"],
            "ones_col_bf": consts["ones_col_bf"],
            "ones_row_bf": consts["ones_row_bf"],
            "ropeC0": consts["rope0_C"], "ropeS0": consts["rope0_S"],
            "ropeP0": consts["rope0_P"],
            "selvis": consts["sel_vis_add"],
            "ownm1": (consts["own_oh"] - 1.0).astype(np.float32),
            "cmpvis": consts["cmp_vis01"],
            "fdiag": consts["fine_diag01"], "band": consts["band01"],
            "selE": consts["selE"],
        })
    return maps



def build_l1_attn(fz=None):
    nc, pre, io = fz_begin(fz)
    xin = io.get("xin") if fz else nc.dram_tensor("xin", [S, D], F32, kind="ExternalInput").ap()
    qkvwT = nc.dram_tensor(pre + "qkvwT", [D, 4608], F32, kind="ExternalInput").ap()
    qkvbT = nc.dram_tensor(pre + "qkvbT", [128, 36], F32, kind="ExternalInput").ap()
    dwT = nc.dram_tensor(pre + "dwT", [D, D], F32, kind="ExternalInput").ap()
    ident_in = nc.dram_tensor(pre + "ident_bf", [128, 128], BF16, kind="ExternalInput").ap()
    onesc_in = nc.dram_tensor(pre + "ones_col_f", [128, 1], F32, kind="ExternalInput").ap()
    onesr_in = nc.dram_tensor(pre + "ones_row_f", [1, 128], F32, kind="ExternalInput").ap()
    ropeC_in = nc.dram_tensor(pre + "ropeC", [64, S], F32, kind="ExternalInput").ap()
    ropeS_in = nc.dram_tensor(pre + "ropeS", [64, S], F32, kind="ExternalInput").ap()
    ropeP_in = nc.dram_tensor(pre + "ropeP", [128, 64], BF16, kind="ExternalInput").ap()
    diag_in = nc.dram_tensor(pre + "diag01", [4, 128, 512], BF16, kind="ExternalInput").ap()
    xout = io.get("xout") if fz else nc.dram_tensor("xout", [S, D], F32, kind="ExternalOutput").ap()

    NH = 32
    GQ = 16
    SCALE = 1.0 / np.sqrt(128.0)

    es = contextlib.ExitStack()
    with es:
        tens, sems, block = alloc(
            nc, es,
            sbufs=[
                ("x_g", [128, D], F32), ("xlnT", [128, NIC * G512], BF16),
                ("bs", [128, 48], F32), ("stats", [128, 16], F32),
                ("xln_tok", [128, D], BF16), ("ident", [128, 128], BF16),
                ("eps_col", [128, 1], F32),
                ("onesc", [128, 1], F32), ("onesr", [1, 128], F32),
                ("biasT", [128, 36], F32),
                ("ropeCS", [128, S], F32),
                ("ropeP_s", [128, 64], BF16), ("diag01_s", [128, 4 * 512], BF16),
                ("qw0", [128, NIC * 128], BF16), ("qw1", [128, NIC * 128], BF16),
                ("rq", [128, NH * G512], BF16),
                ("ktmp", [128, 2 * G512], BF16),
                ("krT", [128, 2 * S], BF16),
                ("vtmp", [128, 2 * G512], BF16),
                ("vtok", [128, 2 * S], BF16),
                ("ctxT", [128, NH * G512], BF16),
                ("expS0", [128, 512], BF16), ("expS1", [128, 512], BF16),
                ("expS2", [128, 512], BF16),
                ("Zacc", [128, 512], F32), ("zrow", [1, 512], F32),
                ("zinv", [1, 512], F32), ("zb_sb", [128, 512], F32),
                ("rt1", [64, 512], F32), ("rt2", [64, 512], F32),
                ("dw0", [128, NIC * 128], BF16), ("dw1", [128, NIC * 128], BF16),
                ("outsb", [128, 2 * 128], F32), ("xres", [128, NTT * 128], F32),
            ],
            psums=[("psA", [128, 512], F32), ("psB", [128, 512], F32),
                   ("psC", [128, 512], F32), ("psD", [128, 512], F32),
                   ("psE", [128, 512], F32),
                   ("ptrA", [128, 1024], BF16), ("ptrB", [128, 1024], BF16)],
            semnames=["ld", "lnv", "lns", "lnt", "lne", "wld", "mm",
                      "ev", "st", "chv", "chs", "rp"],
            sem_es=(fz["es"] if fz else None), prefix=pre)
        cx = Ctx(nc, block, sems)
        fz_barrier(cx, fz)
        ld, lnv, lns, lnt, lne, wld, mm, ev, st, chv, chs, rp = (
            sems[k] for k in ("ld", "lnv", "lns", "lnt", "lne", "wld", "mm",
                              "ev", "st", "chv", "chs", "rp"))
        x_g = tens["x_g"]; xlnT = tens["xlnT"]; ident = tens["ident"]
        rq = tens["rq"]; krT = tens["krT"]; vtok = tens["vtok"]
        ktmp = tens["ktmp"]; vtmp = tens["vtmp"]; ctxT = tens["ctxT"]
        Zacc = tens["Zacc"]; zrow = tens["zrow"]; zinv = tens["zinv"]
        zb_sb = tens["zb_sb"]; rt1 = tens["rt1"]; rt2 = tens["rt2"]
        psA, psB, psC, psD, psE = (tens[k] for k in ("psA", "psB", "psC", "psD", "psE"))
        ptrA, ptrB = tens["ptrA"], tens["ptrB"]
        outsb = tens["outsb"]; xres = tens["xres"]
        expS = [tens["expS0"], tens["expS1"], tens["expS2"]]

        sb = {"x_g": x_g, "xlnT": xlnT, "bs": tens["bs"], "stats": tens["stats"],
              "xln_tok": tens["xln_tok"], "ident": ident, "ptrA": ptrA,
              "ptrB": ptrB, "eps_col": tens["eps_col"]}

        def init_eps(v):
            v.memset(tens["eps_col"][:, :], EPS)
        cx.on("vector", init_eps)

        def prolog_dma(y):
            for dst, src in ((ident, ident_in), (tens["onesc"], onesc_in),
                             (tens["onesr"], onesr_in), (tens["biasT"], qkvbT),
                             (tens["ropeP_s"], ropeP_in)):
                ld.inc(y.dma_start(out=dst[:, :], in_=src[:, :]), 16)
            ld.inc(y.dma_start(out=tens["ropeCS"][0:64, :], in_=ropeC_in[:, :]), 16)
            ld.inc(y.dma_start(out=tens["ropeCS"][64:128, :], in_=ropeS_in[:, :]), 16)
            for j in range(4):
                ld.inc(y.dma_start(out=tens["diag01_s"][:, j * 512:(j + 1) * 512],
                                   in_=diag_in[j, :, :]), 16)
        cx.on("sync", prolog_dma)
        pro_ld = ld.n

        for eng in ("tensor", "vector", "scalar"):
            def gate(e, pro_ld=pro_ld):
                e.wait_ge(ld.h, pro_ld)
            cx.on(eng, gate)

        for g in range(NG):
            emit_ln_group(cx, g, xin, sb, xg_tiled=False)
            xln_ready = lne.n

            for ob in range(36):
                par = ob % 2
                qw, qwname = ((tens["qw0"], "qw0"), (tens["qw1"], "qw1"))[par]

                def wload(y, ob=ob, qw=qw, qwname=qwname):
                    cx.war(y, qwname)
                    src = qkvwT.rearrange("(ic p) o -> p ic o", p=128)
                    for half in range(2):
                        dst = qw[:, half * 16 * 128:(half + 1) * 16 * 128]
                        dst = dst.rearrange("p (ic o) -> p ic o", o=128)
                        i = y.dma_start(
                            out=dst,
                            in_=src[:, half * 16:(half + 1) * 16,
                                    ob * 128:(ob + 1) * 128])
                        wld.inc(i, 16)
                cx.on("gpsimd", wload)
                wld_done = wld.n

                ps, psname = ((psA, "psA"), (psB, "psB"))[par]
                def pmm(t, ob=ob, qw=qw, ps=ps, psname=psname,
                        wld_done=wld_done, xln_ready=xln_ready):
                    t.wait_ge(wld.h, wld_done)
                    t.wait_ge(lne.h, xln_ready)
                    cx.war(t, psname)
                    for ic in range(NIC):
                        i = t.matmul(ps[:, 0:512],
                                     qw[:, ic * 128:(ic + 1) * 128],
                                     xlnT[:, ic * G512:(ic + 1) * G512],
                                     start=(ic == 0), stop=(ic == NIC - 1))
                    mm.inc(i, 1)
                cx.on("tensor", pmm)
                cx.use(qwname, mm, mm.n)
                mm_done = mm.n

                if ob < 32:
                    dst, dname, doff = rq, "rq", ob * G512
                elif ob < 34:
                    dst, dname, doff = ktmp, "ktmp", (ob - 32) * G512
                else:
                    dst, dname, doff = vtmp, "vtmp", (ob - 34) * G512

                def pev(v, ob=ob, ps=ps, mm_done=mm_done, dst=dst,
                        dname=dname, doff=doff):
                    v.wait_ge(mm.h, mm_done)
                    cx.war(v, dname)
                    i = v.tensor_scalar(dst[:, doff:doff + G512], ps[:, 0:512],
                                        tens["biasT"][:, ob:ob + 1], None,
                                        op0=ALU.add)
                    ev.inc(i, 1)
                cx.on("vector", pev)
                cx.use(psname, ev, ev.n)
                ev_done = ev.n

                if ob < 34:
                    if ob < 32:
                        tgt, toff = rq, ob * G512
                    else:
                        tgt, toff = krT, (ob - 32) * S + g * G512
                    def r0(t, ev_done=ev_done, dst=dst, doff=doff):
                        t.wait_ge(ev.h, ev_done)
                        cx.war(t, "psD")
                        i = t.matmul(psD[0:64, 0:512], tens["ropeP_s"][:, :],
                                     dst[:, doff:doff + G512], start=True, stop=True)
                        mm.inc(i, 1)
                    cx.on("tensor", r0)
                    r0_done = mm.n

                    def r1(v, ev_done=ev_done, dst=dst, doff=doff, tgt=tgt,
                           toff=toff, g=g, r0_done=r0_done):
                        gsl = slice(g * G512, (g + 1) * G512)
                        v.wait_ge(ev.h, ev_done)
                        cx.war(v, "rt12")
                        i = v.tensor_tensor(rt1[:, :], dst[0:64, doff:doff + G512],
                                            tens["ropeCS"][0:64, gsl], op=ALU.mult)
                        chv.inc(i, 1); v.wait_ge(chv.h, chv.n)
                        v.wait_ge(mm.h, r0_done)
                        i = v.tensor_tensor(rt2[:, :], psD[0:64, 0:512],
                                            tens["ropeCS"][64:128, gsl], op=ALU.mult)
                        chv.inc(i, 1); v.wait_ge(chv.h, chv.n)
                        cx.use_add("psD", chv, chv.n)
                        i = v.tensor_tensor(tgt[0:64, toff:toff + G512],
                                            rt1[:, :], rt2[:, :], op=ALU.add)
                        rp.inc(i, 1)
                        cx.use("rt12", rp, rp.n)
                        if tgt is not dst:
                            i = v.tensor_copy(tgt[64:128, toff:toff + G512],
                                              dst[64:128, doff:doff + G512])
                            rp.inc(i, 1)
                    cx.on("vector", r1)
                else:
                    kvh = ob - 34
                    for tt in range(NTT):
                        bank, bn = ((ptrA, "ptrA"), (ptrB, "ptrB"))[tt % 2]
                        def vt(t, ev_done=ev_done, tt=tt, bank=bank, bn=bn,
                               kvh=kvh):
                            t.wait_ge(ev.h, ev_done)
                            cx.war(t, bn)
                            i = t.transpose(bank[:, 0:128],
                                            vtmp[:, kvh * G512 + tt * 128:
                                                 kvh * G512 + (tt + 1) * 128],
                                            ident[:, :])
                            lnt.inc(i, 1)
                        cx.on("tensor", vt) if DBG["vtr"] else None
                        t_done = lnt.n
                        def vtev(v, t_done=t_done, tt=tt, bank=bank, kvh=kvh,
                                 g=g):
                            v.wait_ge(lnt.h, t_done)
                            kb = g * NTT + tt
                            i = v.tensor_copy(
                                vtok[:, kvh * S + kb * 128: kvh * S + (kb + 1) * 128],
                                bank[:, 0:128])
                            rp.inc(i, 1)
                        cx.on("vector", vtev) if DBG["vtr"] else None
                        cx.use(bn, rp, rp.n)
            cx.use("xlnT", mm, mm.n)
            rope_ready = rp.n

            NKB = 4 * (g + 1)
            for h in range(NH):
                kvh = h // GQ
                zacc_done = 0
                for kb in range(NKB):
                    sps, spsname = ((psA, "psA"), (psB, "psB"))[kb % 2]
                    slot = kb % 3
                    esb = expS[slot]

                    def smm(t, kb=kb, h=h, kvh=kvh, sps=sps, spsname=spsname,
                            rope_ready=rope_ready):
                        t.wait_ge(rp.h, rope_ready)
                        cx.war(t, spsname)
                        i = t.matmul(
                            sps[:, 0:512],
                            krT[:, kvh * S + kb * 128: kvh * S + (kb + 1) * 128],
                            rq[:, h * G512:(h + 1) * G512],
                            start=True, stop=True)
                        mm.inc(i, 1)
                    cx.on("tensor", smm)
                    smm_done = mm.n

                    def sexp(s_, sps=sps, esb=esb, slot=slot, smm_done=smm_done):
                        s_.wait_ge(mm.h, smm_done)
                        cx.war(s_, f"expS{slot}")
                        i = s_.activation(esb[:, :], sps[:, 0:512], AF.Exp,
                                          scale=SCALE)
                        lns.inc(i, 1)
                    cx.on("scalar", sexp)
                    cx.use(spsname, lns, lns.n)
                    exp_done = lns.n

                    diag = (kb >= 4 * g)
                    if diag:
                        off = kb - 4 * g
                        def dmask(v, esb=esb, off=off, exp_done=exp_done):
                            v.wait_ge(lns.h, exp_done)
                            i = v.tensor_tensor(
                                esb[:, :], esb[:, :],
                                tens["diag01_s"][:, off * 512:(off + 1) * 512],
                                op=ALU.mult)
                            chv.inc(i, 1); v.wait_ge(chv.h, chv.n)
                        cx.on("vector", dmask)

                    def zadd(v, kb=kb, esb=esb, exp_done=exp_done, diag=diag):
                        if not diag:
                            v.wait_ge(lns.h, exp_done)
                        if kb == 0:
                            i = v.tensor_copy(Zacc[:, :], esb[:, :])
                        else:
                            i = v.tensor_tensor(Zacc[:, :], Zacc[:, :],
                                                esb[:, :], op=ALU.add)
                        chv.inc(i, 1); v.wait_ge(chv.h, chv.n)
                    cx.on("vector", zadd)
                    zacc_done = chv.n

                    def pv(t, kb=kb, kvh=kvh, esb=esb, NKB=NKB,
                           zacc_done=zacc_done):
                        t.wait_ge(chv.h, zacc_done)
                        if kb == 0:
                            cx.war(t, "psC")
                        i = t.matmul(
                            psC[:, 0:512],
                            vtok[:, kvh * S + kb * 128: kvh * S + (kb + 1) * 128],
                            esb[:, :],
                            start=(kb == 0), stop=(kb == NKB - 1))
                        mm.inc(i, 1)
                    cx.on("tensor", pv)
                    cx.use(f"expS{slot}", mm, mm.n)
                pv_done = mm.n

                def zred(t, zacc_done=zacc_done):
                    t.wait_ge(chv.h, zacc_done)
                    cx.war(t, "psE")
                    i = t.matmul(psE[0:1, 0:512], tens["onesc"][:, :],
                                 Zacc[:, :], start=True, stop=True)
                    mm.inc(i, 1)
                cx.on("tensor", zred)
                zred_done = mm.n
                cx.use("Zacc", mm, zred_done)

                def zfin(v, zred_done=zred_done):
                    v.wait_ge(mm.h, zred_done)
                    i = v.tensor_copy(zrow[:, :], psE[0:1, 0:512])
                    chv.inc(i, 1); v.wait_ge(chv.h, chv.n)
                    i = v.reciprocal(zinv[:, :], zrow[:, :])
                    ev.inc(i, 1)
                cx.on("vector", zfin)
                cx.use("psE", ev, ev.n)
                zinv_done = ev.n

                def zbc(t, zinv_done=zinv_done):
                    t.wait_ge(ev.h, zinv_done)
                    cx.war(t, "psE")
                    i = t.matmul(psE[:, 0:512], tens["onesr"][:, :],
                                 zinv[:, :], start=True, stop=True)
                    mm.inc(i, 1)
                cx.on("tensor", zbc)
                zbc_done = mm.n

                def ctxev(v, h=h, zbc_done=zbc_done, pv_done=pv_done):
                    v.wait_ge(mm.h, max(zbc_done, pv_done))
                    i = v.tensor_copy(zb_sb[:, :], psE[:, 0:512])
                    chv.inc(i, 1); v.wait_ge(chv.h, chv.n)
                    cx.use("psE", chv, chv.n)
                    cx.war(v, "ctxT")
                    i = v.tensor_tensor(ctxT[:, h * G512:(h + 1) * G512],
                                        psC[:, 0:512], zb_sb[:, :], op=ALU.mult)
                    rp.inc(i, 1)
                cx.on("vector", ctxev)
                cx.use("psC", rp, rp.n)
            ctx_ready = rp.n

            for ob in range(32):
                par = ob % 2
                dw, dwname = ((tens["dw0"], "dw0"), (tens["dw1"], "dw1"))[par]

                def dload(y, ob=ob, dw=dw, dwname=dwname):
                    cx.war(y, dwname)
                    src = dwT.rearrange("(ic p) o -> p ic o", p=128)
                    for half in range(2):
                        dsx = dw[:, half * 16 * 128:(half + 1) * 16 * 128]
                        dsx = dsx.rearrange("p (ic o) -> p ic o", o=128)
                        i = y.dma_start(
                            out=dsx,
                            in_=src[:, half * 16:(half + 1) * 16,
                                    ob * 128:(ob + 1) * 128])
                        wld.inc(i, 16)
                cx.on("gpsimd", dload)
                dld_done = wld.n

                def xload(y, ob=ob, g=g):
                    cx.war(y, "xres")
                    i = y.dma_start(
                        out=xres[:, :].rearrange("p (tt o) -> p tt o", o=128),
                        in_=xin[g * G512:(g + 1) * G512,
                                ob * 128:(ob + 1) * 128].rearrange(
                                    "(tt p) o -> p tt o", p=128))
                    ld.inc(i, 16)
                cx.on("sync", xload)
                xres_done = ld.n

                for tt in range(NTT):
                    ps, psname = ((psA, "psA"), (psB, "psB"), (psC, "psC"),
                                  (psD, "psD"))[tt % 4]
                    def dmm(t, ob=ob, tt=tt, dw=dw, ps=ps, psname=psname,
                            dld_done=dld_done, ctx_ready=ctx_ready):
                        t.wait_ge(wld.h, dld_done)
                        t.wait_ge(rp.h, ctx_ready)
                        cx.war(t, psname)
                        for hc in range(NIC):
                            i = t.matmul(
                                ps[:, 0:128],
                                ctxT[:, hc * G512 + tt * 128: hc * G512 + (tt + 1) * 128],
                                dw[:, hc * 128:(hc + 1) * 128],
                                start=(hc == 0), stop=(hc == NIC - 1))
                        mm.inc(i, 1)
                    cx.on("tensor", dmm)
                    mm_done = mm.n
                    if tt == NTT - 1:
                        cx.use(dwname, mm, mm.n)

                    slot = (ob * NTT + tt) % 2
                    def evf(v, tt=tt, ps=ps, mm_done=mm_done, slot=slot,
                            xres_done=xres_done):
                        v.wait_ge(mm.h, mm_done)
                        v.wait_ge(ld.h, xres_done)
                        cx.war(v, f"outsb{slot}")
                        i = v.tensor_tensor(
                            outsb[:, slot * 128:(slot + 1) * 128], ps[:, 0:128],
                            xres[:, tt * 128:(tt + 1) * 128], op=ALU.add)
                        ev.inc(i, 1)
                    cx.on("vector", evf)
                    cx.use(psname, ev, ev.n)
                    cx.use_add("xres", ev, ev.n)
                    ev_done = ev.n

                    def stf(y, ob=ob, tt=tt, ev_done=ev_done, g=g, slot=slot):
                        y.wait_ge(ev.h, ev_done)
                        t0 = g * G512 + tt * 128
                        i = y.dma_start(
                            out=xout[t0:t0 + 128, ob * 128:(ob + 1) * 128],
                            in_=outsb[:, slot * 128:(slot + 1) * 128])
                        st.inc(i, 16)
                    cx.on("gpsimd", stf)
                    cx.use(f"outsb{slot}", st, st.n)

            cx.use("ctxT", mm, mm.n)
            cx.use("rq", mm, mm.n)
            cx.use("ktmp", mm, mm.n)
            cx.use("vtmp", lnt, lnt.n)

        final_st = st.n
        def fin(y, final_st=final_st):
            y.wait_ge(st.h, final_st)
        cx.on("gpsimd", fin)
        if fz is not None:
            fz["barrier"] = (st.h, st.n)

    return nc


def prep_l1_attn_inputs(inputs, consts, xcur, pre=""):
    qkvwT = np.ascontiguousarray(np.asarray(inputs["L1_attn_qkv_w"], dtype=np.float32).T)
    qkvbT = np.ascontiguousarray(
        np.asarray(inputs["L1_attn_qkv_b"], dtype=np.float32).reshape(36, 128).T)
    dwT = np.ascontiguousarray(np.asarray(inputs["L1_attn_dense_w"], dtype=np.float32).T)
    maps = []
    for core in range(8):
        maps.append({
            "xin": np.ascontiguousarray(xcur[core]),
            "qkvwT": qkvwT, "qkvbT": qkvbT, "dwT": dwT,
            "ident_bf": consts["ident_bf"],
            "ones_col_f": consts["ones_col_f"],
            "ones_row_f": consts["ones_row_f"],
            "ropeC": np.ascontiguousarray(consts["rope1_C"][:64]),
            "ropeS": np.ascontiguousarray(consts["rope1_S"][:64]),
            "ropeP": consts["rope1_P"],
            "diag01": consts["l1_diag01"],
        })
    return maps



def build_l1_ffn(fz=None):
    nc, pre, io = fz_begin(fz)
    xin = io.get("xin") if fz else nc.dram_tensor("xin", [S, D], F32, kind="ExternalInput").ap()
    wupT = nc.dram_tensor(pre + "wupT", [D, D], F32, kind="ExternalInput").ap()
    atgw = nc.dram_tensor(pre + "atgw", [D, 36], F32, kind="ExternalInput").ap()
    btc = nc.dram_tensor(pre + "btc", [8, 4 * D], F32, kind="ExternalInput").ap()
    ebias = nc.dram_tensor(pre + "ebias", [1, 4], F32, kind="ExternalInput").ap()
    ident_in = nc.dram_tensor(pre + "ident_bf", [128, 128], BF16, kind="ExternalInput").ap()
    onesr_in = nc.dram_tensor(pre + "ones_row_f", [1, 128], F32, kind="ExternalInput").ap()
    xout = io.get("xout") if fz else nc.dram_tensor("xout", [S, D], F32, kind="ExternalOutput").ap()

    OB = 256
    NOB = D // OB

    es = contextlib.ExitStack()
    with es:
        tens, sems, block = alloc(
            nc, es,
            sbufs=[("x_g", [128, NTT * D], F32), ("xlnT", [128, NIC * G512], BF16),
                   ("bs", [128, 48], F32), ("stats", [128, 16], F32),
                   ("xln_tok", [128, D], BF16), ("ident", [128, 128], BF16),
                   ("onesr", [1, 128], F32), ("ebrow", [1, 4], F32),
                   ("ebb", [128, 4], F32), ("atgw_bf", [128, NIC * 36], BF16),
                   ("bt_bf", [8, 4 * D], BF16),
                   ("wbf0", [128, NIC * OB], BF16), ("wbf1", [128, NIC * OB], BF16),
                   ("z_sb", [128, 32], F32), ("lg", [128, 8], F32),
                   ("oh", [128, 8], F32), ("zm_bf", [128, 32], BF16),
                   ("zmT", [8, NTT * 4 * 128], BF16), ("outsb", [128, 2 * OB], F32),
                   ("eps_col", [128, 1], F32)],
            psums=[("psA", [128, 512], F32), ("psB", [128, 512], F32),
                   ("psC", [128, 512], F32), ("psD", [128, 512], F32),
                   ("psZ", [128, 512], F32),
                   ("ptrA", [128, 1024], BF16), ("ptrB", [128, 1024], BF16)],
            semnames=["ld", "lnv", "lns", "lnt", "lne", "wld", "wdq", "mm", "ev", "st", "chv", "chs"],
            sem_es=(fz["es"] if fz else None), prefix=pre)
        (x_g, xlnT, stats, xln_tok, ident, onesr, ebrow, ebb, atgw_bf,
         bt_bf, wbf0, wbf1, z_sb, lg, oh, zm_bf, zmT, outsb,
         psA, psB, psZ, ptrA, ptrB) = (
            tens[k] for k in ("x_g", "xlnT", "stats", "xln_tok", "ident",
                              "onesr", "ebrow", "ebb", "atgw_bf", "bt_bf",
                              "wbf0", "wbf1", "z_sb", "lg", "oh", "zm_bf",
                              "zmT", "outsb", "psA", "psB", "psZ", "ptrA", "ptrB"))
        cx = Ctx(nc, block, sems)
        fz_barrier(cx, fz)
        ld, lnv, lns, lnt, lne, wld, wdq, mm, ev, st = (
            sems[k] for k in ("ld", "lnv", "lns", "lnt", "lne", "wld", "wdq",
                              "mm", "ev", "st"))

        sb = {"x_g": x_g, "xlnT": xlnT, "bs": tens["bs"], "stats": stats,
              "xln_tok": xln_tok, "ident": ident, "ptrA": ptrA, "ptrB": ptrB,
              "eps_col": tens["eps_col"]}
        def init_eps(v):
            v.memset(tens["eps_col"][:, :], EPS)
        cx.on("vector", init_eps)

        def prolog_dma(y):
            ld.inc(y.dma_start(out=ident[:, :], in_=ident_in[:, :]), 16)
            ld.inc(y.dma_start(out=onesr[:, :], in_=onesr_in[:, :]), 16)
            ld.inc(y.dma_start(out=ebrow[:, :], in_=ebias[:, :]), 16)
        cx.on("sync", prolog_dma)
        def prolog_dma2(y):
            src = atgw.rearrange("(ic p) c -> p ic c", p=128)
            wld.inc(y.dma_start(out=atgw_bf[:, :].rearrange("p (ic c) -> p ic c", c=36),
                                in_=src), 16)
            wld.inc(y.dma_start(out=bt_bf[:, :], in_=btc[:, :]), 16)
        cx.on("gpsimd", prolog_dma2)
        pro_ld = ld.n
        pro_wld = wld.n

        def prolog_mm(t, pro_ld=pro_ld):
            t.wait_ge(ld.h, pro_ld)
            i = t.matmul(psZ[:, 0:4], onesr[:, :], ebrow[:, :], start=True, stop=True)
            mm.inc(i, 1)
        cx.on("tensor", prolog_mm)
        pro_mm = mm.n

        def prolog_ev(v, pro_mm=pro_mm):
            v.wait_ge(mm.h, pro_mm)
            i = v.tensor_copy(ebb[:, :], psZ[:, 0:4])
            ev.inc(i, 1)
        cx.on("vector", prolog_ev)
        pro_ev = ev.n
        cx.use("psZ", ev, pro_ev)

        for g in range(NG):
            emit_ln_group(cx, g, xin, sb)
            xln_ready = lne.n

            for tt in range(NTT):
                def zmm(t, tt=tt, xln_ready=xln_ready, pro_wld=pro_wld):
                    t.wait_ge(lne.h, xln_ready)
                    t.wait_ge(wld.h, pro_wld)
                    cx.war(t, "psZ")
                    for ic in range(NIC):
                        i = t.matmul(psZ[:, 0:36],
                                     xlnT[:, ic * G512 + tt * 128: ic * G512 + (tt + 1) * 128],
                                     atgw_bf[:, ic * 36:(ic + 1) * 36],
                                     start=(ic == 0), stop=(ic == NIC - 1))
                    mm.inc(i, 1)
                cx.on("tensor", zmm)
                z_done = mm.n

                def zev(v, tt=tt, z_done=z_done):
                    chv = sems["chv"]
                    def ch(i):
                        chv.inc(i, 1); v.wait_ge(chv.h, chv.n)
                    v.wait_ge(mm.h, z_done)
                    v.tensor_copy(z_sb[:, :], psZ[:, 0:32])
                    ch(v.tensor_tensor(lg[:, 0:4], psZ[:, 32:36], ebb[:, :], op=ALU.add))
                    ch(v.tensor_reduce(lg[:, 4:5], lg[:, 0:4], axis=AX.X, op=ALU.max))
                    ch(v.tensor_scalar(oh[:, 0:4], lg[:, 0:4], lg[:, 4:5], None, op0=ALU.is_ge))
                    ch(v.tensor_scalar(oh[:, 4:5], oh[:, 0:1], -1.0, 1.0, op0=ALU.mult, op1=ALU.add))
                    for e in range(1, 4):
                        ch(v.tensor_tensor(oh[:, e:e + 1], oh[:, e:e + 1], oh[:, 4:5], op=ALU.mult))
                        if e < 3:
                            ch(v.tensor_scalar(oh[:, 5:6], oh[:, e:e + 1], -1.0, 1.0, op0=ALU.mult, op1=ALU.add))
                            ch(v.tensor_tensor(oh[:, 4:5], oh[:, 4:5], oh[:, 5:6], op=ALU.mult))
                    for e in range(4):
                        i = v.tensor_scalar(zm_bf[:, e * 8:(e + 1) * 8],
                                            z_sb[:, e * 8:(e + 1) * 8],
                                            oh[:, e:e + 1], None, op0=ALU.mult)
                    ev.inc(i, 1)
                cx.on("vector", zev)
                cx.use("psZ", ev, ev.n)
                zm_done = ev.n

                def ztr(t, tt=tt, zm_done=zm_done):
                    t.wait_ge(ev.h, zm_done)
                    cx.war(t, "ptrA")
                    for e in range(4):
                        i = t.transpose(ptrA[0:8, e * 128:(e + 1) * 128],
                                        zm_bf[:, e * 8:(e + 1) * 8], ident[:, :])
                    lnt.inc(i, 1)
                cx.on("tensor", ztr)
                ztr_done = lnt.n

                def ztev(v, tt=tt, ztr_done=ztr_done):
                    v.wait_ge(lnt.h, ztr_done)
                    i = v.tensor_copy(zmT[:, tt * 512:(tt + 1) * 512],
                                      ptrA[0:8, 0:512])
                    lne.inc(i, 1)
                cx.on("vector", ztev)
                cx.use("ptrA", lne, lne.n)
            zmT_ready = lne.n

            for ob in range(NOB):
                par = ob % 2
                wbf, wbname = ((wbf0, "wb0"), (wbf1, "wb1"))[par]

                def wload(y, ob=ob, wbf=wbf, wbname=wbname):
                    cx.war(y, wbname)
                    src = wupT.rearrange("(ic p) o -> p ic o", p=128)
                    for quarter in range(4):
                        dst = wbf[:, quarter * 8 * OB:(quarter + 1) * 8 * OB]
                        dst = dst.rearrange("p (ic o) -> p ic o", o=OB)
                        i = y.dma_start(
                            out=dst,
                            in_=src[:, quarter * 8:(quarter + 1) * 8,
                                    ob * OB:(ob + 1) * OB])
                        wld.inc(i, 16)
                cx.on("gpsimd", wload)
                wld_done = wld.n

                for tt in range(NTT):
                    ps, psname = ((psA, "psA"), (psB, "psB"), (tens["psC"], "psC"),
                                  (tens["psD"], "psD"))[tt % 4]
                    def mmf(t, ob=ob, tt=tt, ps=ps, psname=psname, wbf=wbf,
                            wld_done=wld_done, xln_ready=xln_ready,
                            zmT_ready=zmT_ready):
                        t.wait_ge(wld.h, wld_done)
                        t.wait_ge(lne.h, max(xln_ready, zmT_ready))
                        cx.war(t, psname)
                        for ic in range(NIC):
                            t.matmul(ps[:, 0:OB],
                                     xlnT[:, ic * G512 + tt * 128: ic * G512 + (tt + 1) * 128],
                                     wbf[:, ic * OB:(ic + 1) * OB],
                                     start=(ic == 0), stop=False)
                        for e in range(4):
                            i = t.matmul(ps[:, 0:OB],
                                         zmT[0:8, tt * 512 + e * 128: tt * 512 + (e + 1) * 128],
                                         bt_bf[0:8, e * D + ob * OB: e * D + (ob + 1) * OB],
                                         start=False, stop=(e == 3))
                        mm.inc(i, 1)
                    cx.on("tensor", mmf)
                    mm_done = mm.n
                    if tt == NTT - 1:
                        cx.use(wbname, mm, mm.n)

                    slot = (ob * NTT + tt) % 2
                    def evf(v, ob=ob, tt=tt, ps=ps, mm_done=mm_done, slot=slot):
                        v.wait_ge(mm.h, mm_done)
                        cx.war(v, f"outsb{slot}")
                        i = v.tensor_tensor(
                            outsb[:, slot * OB:(slot + 1) * OB], ps[:, 0:OB],
                            x_g[:, tt * D + ob * OB: tt * D + (ob + 1) * OB],
                            op=ALU.add)
                        ev.inc(i, 1)
                    cx.on("vector", evf)
                    cx.use(psname, ev, ev.n)
                    ev_done = ev.n

                    def stf(y, ob=ob, tt=tt, ev_done=ev_done, g=g, slot=slot):
                        y.wait_ge(ev.h, ev_done)
                        t0 = g * G512 + tt * 128
                        i = y.dma_start(out=xout[t0:t0 + 128, ob * OB:(ob + 1) * OB],
                                        in_=outsb[:, slot * OB:(slot + 1) * OB])
                        st.inc(i, 16)
                    cx.on("sync", stf)
                    cx.use(f"outsb{slot}", st, st.n)

            cx.use("x_g", ev, ev.n)
            cx.use("xlnT", mm, mm.n)
            cx.use("wb0", mm, mm.n)
            cx.use("wb1", mm, mm.n)

        final_st = st.n
        def fin(y, final_st=final_st):
            y.wait_ge(st.h, final_st)
        cx.on("sync", fin)
        if fz is not None:
            fz["barrier"] = (st.h, st.n)

    return nc


def prep_l1_ffn_inputs(inputs, consts, xcur, pre=""):
    wupT = np.ascontiguousarray(np.asarray(inputs["L1_ffn_W_up"], dtype=np.float32).T)
    A = np.asarray(inputs["L1_ffn_lora_A"], dtype=np.float32)
    gw = np.asarray(inputs["L1_ffn_gate_w"], dtype=np.float32)
    atgw = np.concatenate([A[e].T for e in range(4)] + [gw.T], axis=1)
    atgw = np.ascontiguousarray(atgw)
    B = np.asarray(inputs["L1_ffn_lora_B"], dtype=np.float32)
    btc = np.ascontiguousarray(
        np.concatenate([B[e].T for e in range(4)], axis=1))
    ebias = np.ascontiguousarray(
        np.asarray(inputs["L1_ffn_expert_bias"], dtype=np.float32).reshape(1, 4))
    maps = []
    for core in range(8):
        maps.append({
            "xin": np.ascontiguousarray(xcur[core]),
            "wupT": wupT, "atgw": atgw, "btc": btc, "ebias": ebias,
            "ident_bf": consts["ident_bf"],
            "ones_row_f": consts["ones_row_f"],
        })
    return maps



def prep_l0_ffn_inputs(inputs, consts, xcur, pre=""):
    wT = np.ascontiguousarray(inputs["L0_ffn_weight"].T)
    scflat = np.ascontiguousarray(
        np.asarray(inputs["L0_ffn_weight_scale"], dtype=np.float32).reshape(1, 1024))
    maps = []
    for core in range(8):
        maps.append({
            "xin": np.ascontiguousarray(xcur[core]),
            "wT": wT,
            "scflat": scflat,
            "ident_bf": consts["ident_bf"],
            "ones_row_f": consts["ones_row_f"],
        })
    return maps


def build_fused():
    nc = bass.Bass("TRN2", target_bir_lowering=False)
    x0 = nc.dram_tensor("x0", [S, D], F32, kind="ExternalInput").ap()
    xout = nc.dram_tensor("xout", [S, D], F32, kind="ExternalOutput").ap()
    xattT = nc.dram_tensor("t_xattT", [D, S], BF16).ap()
    x1 = nc.dram_tensor("t_x1", [S, D], F32).ap()
    x2 = nc.dram_tensor("t_x2", [S, D], F32).ap()
    x3 = nc.dram_tensor("t_x3", [S, D], F32).ap()
    es = contextlib.ExitStack()
    with es:
        fz = {"nc": nc, "es": es, "barrier": None,
              "prefix": "a_", "io": {"xin": x0, "xattT": xattT}}
        build_l0_attn(fz)
        fz["prefix"] = "b_"; fz["io"] = {"xres": x0, "xT": xattT, "xout": x1}
        build_proj_residual(fz)
        fz["prefix"] = "f_"; fz["io"] = {"xin": x1, "xout": x2}
        build_l0_ffn(fz)
        fz["prefix"] = "c_"; fz["io"] = {"xin": x2, "xout": x3}
        build_l1_attn(fz)
        fz["prefix"] = "d_"; fz["io"] = {"xin": x3, "xout": xout}
        build_l1_ffn(fz)
    return nc


def kernel(**inputs):
    consts = host_consts()
    x = np.ascontiguousarray(np.asarray(inputs["L0_attn_hidden_states"], dtype=np.float32))

    nc = build_fused()
    woT = np.ascontiguousarray(np.asarray(inputs["L0_attn_w_o"], dtype=np.float32).T)
    shared = {}
    for pre, prep in (("a_", prep_l0_attn_inputs), ("f_", prep_l0_ffn_inputs),
                      ("c_", prep_l1_attn_inputs), ("d_", prep_l1_ffn_inputs)):
        m = prep(inputs, consts, x)[0]
        shared.update({pre + k: v for k, v in m.items() if k != "xin"})
    shared["b_wT"] = woT
    maps = [dict(shared, x0=np.ascontiguousarray(x[core])) for core in range(8)]
    res = run_bass_kernel_spmd(nc, maps, core_ids=list(range(8)))
    return np.stack([res.results[i]["xout"] for i in range(8)], axis=0)
